# version 12
# baseline (speedup 1.0000x reference)
import contextlib
import numpy as np
import ml_dtypes
import concourse.bass as bass
import concourse.mybir as mybir
from concourse.bass_utils import run_bass_kernel_spmd

F32 = mybir.dt.float32
BF16 = mybir.dt.bfloat16
FP8 = mybir.dt.float8e4
AF = mybir.ActivationFunctionType
ALU = mybir.AluOpType
AX = mybir.AxisListType

BFD = ml_dtypes.bfloat16
F8D = ml_dtypes.float8_e4m3fn

L = 2048
D = 1024
N = 16
DN = D * N
BT = 114
NBLK = 18
NSLC = 32
SL = 512
RMAX = 13
NPAT = 6

DT_C = (0.1 + 0.001) / 2.0
DT_R = (0.1 - 0.001) / 2.0


def block_structure():
    blocks = []
    for k in range(NBLK):
        t0 = k * BT
        bt = min(BT, L - t0)
        regions = []
        lo = 0
        for r in range(1, bt + 1):
            t = t0 + r
            if r == bt or t % 10 == 1:
                hi = r - 1
                clamps = ((t0 + hi) % 10 == 0)
                regions.append((lo, hi, clamps))
                lo = r
        blocks.append((t0, bt, regions))
    return blocks


def pat_of(k):
    return 5 if k == NBLK - 1 else k % 5


def build_consts():
    blocks = block_structure()
    tri = np.zeros((NPAT, BT, 128), dtype=np.float32)
    mre = np.zeros((NPAT, 128, BT), dtype=np.float32)
    done = set()
    for k, (t0, bt, regions) in enumerate(blocks):
        p = pat_of(k)
        if p in done:
            continue
        done.add(p)
        for ci, (lo, hi, cl) in enumerate(regions):
            for tp in range(lo, hi + 1):
                tri[p, lo:tp + 1, tp] = 1.0
                tri[p, tp, 114 + ci] = 1.0
                mre[p, tp, tp] = 1.0
                mre[p, 114 + ci, tp] = 1.0
    ident = np.eye(128, dtype=np.float32)
    c = {
        "c_tri": tri.reshape(NPAT * BT, 128).astype(BFD),
        "c_mre": mre.reshape(NPAT * 128, BT).astype(BFD),
        "c_id_bf": ident.astype(BFD),
        "c_id_f32": ident.copy(),
        "c_id_f8": ident.astype(F8D),
        "c_ones_bf": np.ones((1, BT), dtype=np.float32).astype(BFD),
        "c_ones_row": np.ones((1, 128), dtype=np.float32),
        "c_ones_p": np.ones((128, 1), dtype=np.float32),
        "c_ones_sq": np.ones((128, 128), dtype=np.float32),
        "c_bias": np.tile(np.array([[1e-5, 1.0, -DT_C / DT_R]], dtype=np.float32), (128, 1)),
    }
    return blocks, c


def build_nc(debug=False, nblk=NBLK, stage=9):
    blocks, _ = build_consts()
    blocks = blocks[:nblk]
    nc = bass.Bass("TRN2", target_bir_lowering=False)

    x_h = nc.dram_tensor("x", [L, D], F32, kind="ExternalInput").ap()
    wdt_h = nc.dram_tensor("W_dtT", [D, D], F32, kind="ExternalInput").ap()
    bdt_h = nc.dram_tensor("b_dt", [1, D], F32, kind="ExternalInput").ap()
    wb_h = nc.dram_tensor("W_BT", [D, N], F32, kind="ExternalInput").ap()
    wc_h = nc.dram_tensor("W_CT", [D, N], F32, kind="ExternalInput").ap()
    alog_h = nc.dram_tensor("A_log", [D, N], F32, kind="ExternalInput").ap()
    dpar_h = nc.dram_tensor("D_param", [1, D], F32, kind="ExternalInput").ap()
    wf_h = nc.dram_tensor("W_ffnT", [D, D], F32, kind="ExternalInput").ap()
    bf_h = nc.dram_tensor("b_ffn", [1, D], F32, kind="ExternalInput").ap()
    tri_h = nc.dram_tensor("c_tri", [NPAT * BT, 128], BF16, kind="ExternalInput").ap()
    mre_h = nc.dram_tensor("c_mre", [NPAT * 128, BT], BF16, kind="ExternalInput").ap()
    idb_h = nc.dram_tensor("c_id_bf", [128, 128], BF16, kind="ExternalInput").ap()
    idf_h = nc.dram_tensor("c_id_f32", [128, 128], F32, kind="ExternalInput").ap()
    id8_h = nc.dram_tensor("c_id_f8", [128, 128], FP8, kind="ExternalInput").ap()
    onesb_h = nc.dram_tensor("c_ones_bf", [1, BT], BF16, kind="ExternalInput").ap()
    onesr_h = nc.dram_tensor("c_ones_row", [1, 128], F32, kind="ExternalInput").ap()
    onesp_h = nc.dram_tensor("c_ones_p", [128, 1], F32, kind="ExternalInput").ap()
    bias_h = nc.dram_tensor("c_bias", [128, 3], F32, kind="ExternalInput").ap()
    onessq_h = nc.dram_tensor("c_ones_sq", [128, 128], F32, kind="ExternalInput").ap()
    a16_d = nc.dram_tensor("a16_scratch", [D, N], BF16, kind="Internal").ap()
    a16_d2 = nc.dram_tensor("a16_scratch2", [N, D], BF16, kind="Internal").ap()
    chE_d = nc.dram_tensor("chE_scratch", [RMAX, DN], BF16, kind="Internal").ap()
    chS_d = nc.dram_tensor("chS_scratch", [RMAX, DN], BF16, kind="Internal").ap()
    h0_d = nc.dram_tensor("h0_scratch", [RMAX, DN], BF16, kind="Internal").ap()

    out_h = nc.dram_tensor("out", [L, D], F32, kind="ExternalOutput").ap()
    dbg = {}
    if debug:
        for nm, shp, dt_ in [("dbg_dt", [128, NBLK * D], F32), ("dbg_E0", [128, DN], F32),
                             ("dbg_S0", [128, DN], F32), ("dbg_y0", [BT, D], F32),
                             ("dbg_h0s", [128, RMAX * 128], F32),
                             ("dbg_sfull", [BT, 1], F32), ("dbg_bp", [128, NBLK * N], F32),
                             ("dbg_cp", [128, NBLK * N], F32), ("dbg_xn0", [BT, D], F32),
                             ("dbg_a16", [128, DN], F32)]:
            dbg[nm] = nc.dram_tensor(nm, shp, dt_, kind="ExternalOutput").ap()

    es = contextlib.ExitStack()

    def sb(name, shape, dtype):
        return es.enter_context(nc.sbuf_tensor(name, shape, dtype))

    def ps(name, shape, dtype):
        return es.enter_context(nc.psum_tensor(name, shape, dtype))

    dt_full = sb("dt_full", [128, NBLK * D], BF16)
    xqT = sb("xqT", [128, 8 * L], FP8)
    a16b = sb("a16b", [128, DN], BF16)
    s_blk = sb("s_blk", [128, DN], BF16)
    wqT = sb("wqT", [128, 8 * D], FP8)
    tri_c = sb("tri_c", [BT, NPAT * 128], BF16)
    mre_c = sb("mre_c", [128, NPAT * BT], BF16)
    bp_full = sb("bp_full", [128, NBLK * N], BF16)
    cp_full = sb("cp_full", [128, NBLK * N], F32)
    dbc = sb("dbc", [128, D], BF16)
    fbb = sb("fbb", [128, D], BF16)
    id_bf = sb("id_bf", [128, 128], BF16)
    id_f32 = sb("id_f32", [128, 128], F32)
    id_f8 = sb("id_f8", [128, 128], FP8)
    ones_bf = sb("ones_bf", [1, BT], BF16)
    ones_sq = sb("ones_sq", [128, 128], F32)
    bias_c = sb("bias_c", [128, 3], F32)
    chE = sb("chE", [128, RMAX * 128], BF16)
    chS = sb("chS", [128, RMAX * 128], BF16)
    h0st = sb("h0st", [128, RMAX * 128], BF16)
    h_t = sb("h_t", [128, 128], F32)
    ht_tmp = sb("ht_tmp", [128, 128], F32)
    ht_new = sb("ht_new", [128, 128], F32)
    sfull = [sb(f"sfull{i}", [BT, 1], F32) for i in range(2)]
    qcol = sb("qcol", [128, 1], F32)
    scol = sb("scol", [128, 1], F32)
    sm1 = sb("sm1", [128, 1], F32)
    x_t = sb("x_t", [128, D], F32)
    xn_t = sb("xn_t", [128, D], F32)
    w_t = sb("w_t", [128, D], BF16)
    mean_c = sb("mean_c", [128, 1], F32)
    sq_c = sb("sq_c", [128, 1], F32)
    var_c = sb("var_c", [128, 1], F32)
    rstd_c = sb("rstd_c", [128, 1], F32)
    garg = [sb(f"garg{i}", [BT, SL], BF16) for i in range(2)]
    g_t = [sb(f"g_t{i}", [BT, SL], BF16) for i in range(2)]
    en2_t = sb("en2_t", [BT, 2 * SL], BF16)
    r2_t = [sb(f"r2_t{i}", [BT, 2 * SL], BF16) for i in range(1)]
    ces_t = sb("ces_t", [BT, 2 * SL], BF16)
    yt_bf = sb("yt_bf", [BT, D], BF16)
    yt_blk = sb("yt_blk", [BT, D], F32)
    base_b = sb("base_b", [BT, D], F32)

    off0 = (nc.sbuf_base + 31) & ~31
    arena = sb("arena", [128, 16384], BF16)

    def at(name, shape, dtype, byte_off):
        return nc.alloc_sbuf_tensor_at(name, shape, dtype, offset=off0 + byte_off)

    e_blk = arena
    wdtT = at("wdtT", [128, 8 * D], BF16, 0)
    wslab = at("wslab", [128, D], F32, 16384)
    wslab2 = at("wslab2", [128, D], BF16, 20480)
    wslab8 = at("wslab8", [128, D], FP8, 22528)
    xnT_t = at("xnT_t", [128, 8 * BT], BF16, 23552)
    ep_e = at("ep_e", [128, 2 * SL], F32, 25376)
    wbT = at("wbT", [128, 8 * N], BF16, 29472)
    wcT = at("wcT", [128, 8 * N], BF16, 29728)
    bdt_row = at("bdt_row", [1, D], BF16, 29984)
    atile = at("atile", [128, 128], F32, 0)
    a16t = at("a16t", [128, 128], BF16, 512)

    p_ca = [ps(f"p_ca{i}", [128, 2 * SL], F32) for i in range(2)]
    p_sp = [ps(f"p_sp{i}", [128, SL], F32) for i in range(2)]
    p_ffn = [ps(f"p_ffn{i}", [128, SL], F32) for i in range(2)]
    p_trb = [p_ca[i][:].bitcast(BF16)[:, 0:128] for i in range(2)]
    p_tr8 = [p_ca[i][:].bitcast(FP8)[:, 512:768].rearrange("p (j t) -> p j t", t=2)[:, :, 0:1].squeeze(2) for i in range(2)]
    p_s = p_sp[0][:, 300:301]

    sem_d = es.enter_context(nc.semaphore("sem_d"))
    sem_g = es.enter_context(nc.semaphore("sem_g"))
    sem_p = es.enter_context(nc.semaphore("sem_p"))
    sem_t = es.enter_context(nc.semaphore("sem_t"))
    sem_v = es.enter_context(nc.semaphore("sem_v"))
    sem_s = es.enter_context(nc.semaphore("sem_s"))

    cnt = {"d": 0, "g": 0, "p": 0, "t": 0, "v": 0, "s": 0}
    sems = {"d": sem_d, "g": sem_g, "p": sem_p, "t": sem_t, "v": sem_v, "s": sem_s}
    prog = {"sync": [], "gpsimd": [], "tensor": [], "vector": [], "scalar": []}
    ENG_SEM = {"sync": "d", "gpsimd": "g", "tensor": "t", "vector": "v", "scalar": "s"}
    ENG_KEYS = {"sync": ("d",), "gpsimd": ("g", "p"), "tensor": ("t",),
                "vector": ("v",), "scalar": ("s",)}

    def emit(engine, fn, inc=1):
        if engine == "gpsimd":
            key = "g" if inc == 16 else "p"
        else:
            key = ENG_SEM[engine]
        befores = [(kk, cnt[kk]) for kk in ENG_KEYS[engine]]
        cnt[key] += inc
        sem = sems[key]
        if engine != "tensor":
            for kk, before in befores:
                if before > 0:
                    def wthunk(eng, sem=sems[kk], before=before):
                        eng.wait_ge(sem, before)
                    prog[engine].append(wthunk)

        def thunk(eng, fn=fn, sem=sem, inc=inc):
            fn(eng).then_inc(sem, inc)
        prog[engine].append(thunk)
        return (key, cnt[key])

    def wait(engine, ev):
        if ev is None:
            return
        key, val = ev
        if key in ENG_KEYS[engine]:
            return

        def thunk(eng, key=key, val=val):
            eng.wait_ge(sems[key], val)
        prog[engine].append(thunk)

    def waits(engine, evs):
        best = {}
        for ev in evs:
            if ev is None:
                continue
            kk, vv = ev
            if kk not in ENG_KEYS[engine] and best.get(kk, -1) < vv:
                best[kk] = vv
        for kk, vv in best.items():
            wait(engine, (kk, vv))

    def snapshot():
        return dict(cnt)

    def barrier(snap):
        evs = [(kk, vv) for kk, vv in snap.items()]
        for eng in ("tensor", "vector", "scalar", "gpsimd", "sync"):
            waits(eng, evs)

    ev_pre = []
    for dst, src in [(id_bf[:], idb_h), (id_f32[:], idf_h), (id_f8[:], id8_h),
                     (ones_bf[:], onesb_h),
                     (bias_c[:], bias_h), (ones_sq[:], onessq_h),
                     (atile[:], alog_h.rearrange("(q p) n -> p q n", p=128))]:
        ev_pre.append(emit("sync", lambda e, d=dst, s=src: e.dma_start(out=d, in_=s), inc=16))
    for p in range(NPAT):
        ev_pre.append(emit("sync", lambda e, p=p: e.dma_start(
            out=tri_c[:, p * 128:(p + 1) * 128], in_=tri_h[p * BT:(p + 1) * BT, :]), inc=16))
        ev_pre.append(emit("sync", lambda e, p=p: e.dma_start(
            out=mre_c[:, p * BT:(p + 1) * BT], in_=mre_h[p * 128:(p + 1) * 128, :]), inc=16))

    nc.const_aps.register(F32, 1e-5, bias_c[:, 0:1])
    nc.const_aps.register(F32, 1.0, bias_c[:, 1:2])
    nc.const_aps.register(F32, -DT_C / DT_R, bias_c[:, 2:3])
    waits("scalar", ev_pre)
    emit("scalar", lambda e: e.activation(out=atile[:], in_=atile[:], func=AF.Tanh, scale=0.1))
    ev = emit("scalar", lambda e: e.activation(out=atile[:], in_=atile[:], func=AF.Exp, scale=10.0))
    wait("vector", ev)
    ev_a16 = emit("vector", lambda e: e.tensor_scalar(
        out=a16t[:], in0=atile[:], scalar1=-1.0 / 16.0, scalar2=None, op0=ALU.mult))
    wait("gpsimd", ev_a16)
    ev_a16d = emit("gpsimd", lambda e: e.dma_start(out=a16_d.rearrange("(q p) n -> p q n", p=128), in_=a16t[:].rearrange("p (q n) -> p q n", n=N)), inc=16)
    wait("gpsimd", ev_a16d)
    for ni in range(N):
        ev_a16d = emit("gpsimd", lambda e, ni=ni: e.dma_start(
            out=a16_d2[ni:ni + 1, :], in_=a16_d[:, ni:ni + 1].rearrange("d n -> n d"),
            allow_slow_non_contiguous=True), inc=16)
    wait("gpsimd", ev_a16d)
    ev_a16b = emit("gpsimd", lambda e: e.dma_start(
        out=a16b[:],
        in_=a16_d2.rearrange("n d -> (n d)").unsqueeze(0).partition_broadcast(128).squeeze(1)), inc=16)
    waits("gpsimd", ev_pre)
    ev_dbc = emit("gpsimd", lambda e: e.dma_start(
        out=dbc[:], in_=dpar_h.partition_broadcast(128).squeeze(1)), inc=16)
    ev_fbb = emit("gpsimd", lambda e: e.dma_start(
        out=fbb[:], in_=bf_h.partition_broadcast(128).squeeze(1)), inc=16)

    ev_bd = emit("sync", lambda e: e.dma_start(out=x_t[0:1, :], in_=bdt_h), inc=16)
    wait("vector", ev_bd)
    ev_bdt = emit("vector", lambda e: e.tensor_copy(out=bdt_row[:], in_=x_t[0:1, :]))

    if stage >= 2:
        waits("sync", [ev_a16d, ev_dbc, ev_fbb, ev_a16b])
        waits("tensor", ev_pre)
        ev_cp_prev = ev_bdt
        for i in range(8):
            waits("sync", [ev_cp_prev])
            ev1 = emit("sync", lambda e, i=i: e.dma_start(out=wslab[:], in_=wdt_h[i * 128:(i + 1) * 128, :]), inc=16)
            wait("vector", ev1)
            ev_cp_prev = emit("vector", lambda e, i=i: e.tensor_copy(
                out=wdtT[:, i * D:(i + 1) * D], in_=wslab[:]))
        for i in range(8):
            waits("sync", [ev_cp_prev])
            ev1 = emit("sync", lambda e, i=i: e.dma_start(out=wslab[:], in_=wf_h[i * 128:(i + 1) * 128, :]), inc=16)
            wait("scalar", ev1)
            ev_cp_prev = emit("scalar", lambda e, i=i: e.copy(
                out=wqT[:, i * D:(i + 1) * D], in_=wslab[:]))
        waits("sync", [ev_cp_prev])
        ev1 = emit("sync", lambda e: e.dma_start(
            out=wslab[:, 0:8 * N].rearrange("p (i n) -> p i n", n=N),
            in_=wb_h.rearrange("(i p) n -> p i n", p=128)), inc=16)
        ev1b = emit("sync", lambda e: e.dma_start(
            out=wslab[:, 8 * N:16 * N].rearrange("p (i n) -> p i n", n=N),
            in_=wc_h.rearrange("(i p) n -> p i n", p=128)), inc=16)
        waits("vector", [ev1, ev1b])
        emit("vector", lambda e: e.tensor_copy(out=wbT[:], in_=wslab[:, 0:8 * N]))
        ev_cp_prev = emit("vector", lambda e: e.tensor_copy(out=wcT[:], in_=wslab[:, 8 * N:16 * N]))

    emit("vector", lambda e: e.memset(h_t[:], 0.0))
    emit("vector", lambda e: e.memset(h0st[:], 0.0))
    emit("vector", lambda e: e.memset(s_blk[:], 0.0))
    emit("vector", lambda e: e.memset(chE[:], 0.0))
    emit("vector", lambda e: e.memset(chS[:], 0.0))
    if debug:
        emit("vector", lambda e: e.memset(dt_full[:], 0.0))
        emit("vector", lambda e: e.memset(bp_full[:], 0.0))
        emit("vector", lambda e: e.memset(cp_full[:], 0.0))
        emit("vector", lambda e: e.memset(sfull[0][:], 0.0))
        emit("vector", lambda e: e.memset(sfull[1][:], 0.0))
        emit("vector", lambda e: e.memset(yt_blk[:], 0.0))
        emit("vector", lambda e: e.memset(xn_t[:], 0.0))
        barrier(snapshot())
    if debug:
        waits("gpsimd", [ev_a16b])
        emit("gpsimd", lambda e: e.dma_start(out=dbg["dbg_a16"], in_=a16b[:]), inc=16)

    a_sq = [None] * NBLK
    a_v2 = [None] * NBLK
    a_rstd = [None] * NBLK
    a_xn = [None] * NBLK
    a_trlast = [None] * NBLK
    a_xnTl = [None] * NBLK
    a_cp = [None] * NBLK
    a_e1 = [[None, None] for _ in range(NBLK)]
    for k, (t0, bt, regions) in enumerate(blocks if stage >= 3 else []):
        def pa(lst, i):
            return lst[i] if i >= 0 else None
        waits("sync", [pa(a_sq, k - 1), pa(a_xn, k - 1)])
        evx = emit("sync", lambda e, t0=t0, bt=bt: e.dma_start(
            out=x_t[0:bt, :], in_=x_h[t0:t0 + bt, :]), inc=16)
        wait("vector", evx)
        emit("vector", lambda e, bt=bt: e.tensor_reduce(
            out=mean_c[0:bt, :], in_=x_t[0:bt, :], axis=AX.X, op=ALU.add))
        waits("scalar", [evx, pa(a_v2, k - 1)])
        a_sq[k] = emit("scalar", lambda e, bt=bt: e.activation(
            out=yt_blk[0:bt, :], in_=x_t[0:bt, :], func=AF.Square, accum_out=sq_c[0:bt, :]))
        waits("vector", [a_sq[k], pa(a_rstd, k - 1)])
        emit("vector", lambda e, bt=bt: e.tensor_scalar(
            out=mean_c[0:bt, :], in0=mean_c[0:bt, :], scalar1=1.0 / D, scalar2=None, op0=ALU.mult))
        emit("vector", lambda e, bt=bt: e.tensor_tensor(
            out=var_c[0:bt, :], in0=mean_c[0:bt, :], in1=mean_c[0:bt, :], op=ALU.mult))
        a_v2[k] = emit("vector", lambda e, bt=bt: e.scalar_tensor_tensor(
            out=var_c[0:bt, :], in0=sq_c[0:bt, :], scalar=1.0 / D, in1=var_c[0:bt, :],
            op0=ALU.mult, op1=ALU.subtract))
        waits("scalar", [a_v2[k], pa(a_xn, k - 1)])
        emit("scalar", lambda e, bt=bt: e.activation(
            out=rstd_c[0:bt, :], in_=var_c[0:bt, :], func=AF.Ln, bias=1e-5))
        a_rstd[k] = emit("scalar", lambda e, bt=bt: e.activation(
            out=rstd_c[0:bt, :], in_=rstd_c[0:bt, :], func=AF.Exp, scale=-0.5))
        waits("vector", [a_rstd[k], pa(a_trlast, k - 1)])
        a_xn[k] = emit("vector", lambda e, bt=bt: e.tensor_scalar(
            out=xn_t[0:bt, :], in0=x_t[0:bt, :], scalar1=mean_c[0:bt, :], scalar2=rstd_c[0:bt, :],
            op0=ALU.subtract, op1=ALU.mult))
        ev_tr = [None] * 8
        ev_evv = [None] * 8
        ev_evs = [None] * 8
        for j in range(9):
            if j < 8:
                waits("tensor", [a_xn[k], pa(a_cp, k - 1),
                                 ev_evs[j - 2] if j >= 2 else None])
                ev_tr[j] = emit("tensor", lambda e, j=j, bt=bt: e.transpose(
                    out=p_ca[j % 2][:, 0:bt], in_=xn_t[0:bt, j * 128:(j + 1) * 128],
                    identity=id_f32[0:bt, 0:bt]))
            if j >= 1:
                jj = j - 1
                wait("vector", ev_tr[jj])
                ev_evv[jj] = emit("vector", lambda e, jj=jj, bt=bt: e.tensor_copy(
                    out=xnT_t[:, jj * BT: jj * BT + bt], in_=p_ca[jj % 2][:, 0:bt]))
                waits("scalar", [ev_tr[jj], ev_evv[jj]])
                ev_evs[jj] = emit("scalar", lambda e, jj=jj, bt=bt, t0=t0: e.copy(
                    out=xqT[:, jj * L + t0: jj * L + t0 + bt], in_=p_ca[jj % 2][:, 0:bt]))
        a_trlast[k] = ev_tr[7]
        a_xnTl[k] = ev_evv[7]
        ev_mmd = [None, None]
        for h in range(2):
            waits("tensor", [a_xnTl[k], a_e1[k - 1][h] if k >= 1 else None])
            for j in range(8):
                emit("tensor", lambda e, j=j, h=h, bt=bt: e.matmul(
                    out=p_ffn[h][0:bt, :], lhsT=xnT_t[:, j * BT: j * BT + bt],
                    rhs=wdtT[:, j * D + h * SL: j * D + h * SL + SL],
                    start=(j == 0), stop=False))
            ev_mmd[h] = emit("tensor", lambda e, h=h, bt=bt: e.matmul(
                out=p_ffn[h][0:bt, :], lhsT=ones_bf[:, 0:bt],
                rhs=bdt_row[:, h * SL: h * SL + SL], start=False, stop=True))
        for h in range(2):
            waits("scalar", [ev_mmd[h]])
            a_e1[k][h] = emit("scalar", lambda e, h=h, bt=bt: e.activation(
                out=ep_e[0:bt, h * SL:(h + 1) * SL], in_=p_ffn[h][0:bt, :], func=AF.Exp))
        for h in range(2):
            emit("scalar", lambda e, h=h, bt=bt: e.activation(
                out=ep_e[0:bt, h * SL:(h + 1) * SL], in_=ep_e[0:bt, h * SL:(h + 1) * SL],
                func=AF.Ln, bias=1.0))
        ev_sg = [None, None]
        for h in range(2):
            ev_sg[h] = emit("scalar", lambda e, h=h, bt=bt: e.activation(
                out=ep_e[0:bt, h * SL:(h + 1) * SL], in_=ep_e[0:bt, h * SL:(h + 1) * SL],
                func=AF.Sigmoid, scale=1.0 / DT_R, bias=-DT_C / DT_R))
        for h in range(2):
            wait("vector", ev_sg[h])
            emit("vector", lambda e, h=h, k=k, bt=bt: e.tensor_scalar(
                out=dt_full[0:bt, k * D + h * SL: k * D + h * SL + SL],
                in0=ep_e[0:bt, h * SL:(h + 1) * SL],
                scalar1=4.0 * DT_R, scalar2=DT_C - 2.0 * DT_R, op0=ALU.mult, op1=ALU.add))
        waits("tensor", [a_xnTl[k], ev_evs[6], ev_evs[7]])
        for j in range(8):
            emit("tensor", lambda e, j=j, bt=bt: e.matmul(
                out=p_ca[0][0:bt, 0:N], lhsT=xnT_t[:, j * BT: j * BT + bt],
                rhs=wbT[:, j * N:(j + 1) * N], start=(j == 0), stop=(j == 7)))
        for j in range(8):
            ev_mmc = emit("tensor", lambda e, j=j, bt=bt: e.matmul(
                out=p_ca[0][0:bt, N:2 * N], lhsT=xnT_t[:, j * BT: j * BT + bt],
                rhs=wcT[:, j * N:(j + 1) * N], start=(j == 0), stop=(j == 7)))
        wait("vector", ev_mmc)
        emit("vector", lambda e, k=k, bt=bt: e.tensor_copy(
            out=bp_full[0:bt, k * N:(k + 1) * N], in_=p_ca[0][0:bt, 0:N]))
        a_cp[k] = emit("vector", lambda e, k=k, bt=bt: e.tensor_copy(
            out=cp_full[0:bt, k * N:(k + 1) * N], in_=p_ca[0][0:bt, N:2 * N]))

    NG = NBLK * NSLC
    ev_g = [None] * NG
    ev_garg = [None] * NG
    ev_ca = [None] * NG
    ev_E = [None] * NG
    ev_en = [None] * NG
    ev_r = [None] * NG
    ev_smm = [None] * NG
    ev_evac = [None] * NG
    ev_re = [None] * NG
    ev_stt = [None] * NG
    ev_yadd = [None] * NG
    ev_sq_b = [None] * NBLK
    ev_v2_b = [None] * NBLK
    ev_sr_b = [None] * NBLK
    ev_xn_b = [None] * NBLK
    ev_w_b = [None] * NBLK
    ev_ext_b = [None] * NBLK
    ev_chain_b = [None] * NBLK
    ev_wb_b = [None] * NBLK
    ev_basex_b = [None] * NBLK
    ev_baseff_b = [None] * NBLK
    ev_store_b = [None] * NBLK
    ev_hcarry_b = [None] * NBLK
    ev_ffn_b = [None] * NBLK

    def pe(lst, i):
        return lst[i] if i >= 0 else None

    base2 = [yt_blk, base_b]

    def emit_head(k):
        t0, bt, regions = blocks[k]
        waits("sync", [pe(ev_sq_b, k - 1), pe(ev_basex_b, k - 1)])
        evx = emit("sync", lambda e, t0=t0, bt=bt: e.dma_start(
            out=x_t[0:bt, :], in_=x_h[t0:t0 + bt, :]), inc=16)
        wait("vector", evx)
        emit("vector", lambda e, bt=bt: e.tensor_reduce(
            out=mean_c[0:bt, :], in_=x_t[0:bt, :], axis=AX.X, op=ALU.add))
        waits("scalar", [evx, pe(ev_v2_b, k - 1), pe(ev_r, k * NSLC - 1)])
        ev_sq_b[k] = emit("scalar", lambda e, bt=bt: e.activation(
            out=w_t[0:bt, :], in_=x_t[0:bt, :], func=AF.Square, accum_out=sq_c[0:bt, :]))
        waits("vector", [ev_sq_b[k], pe(ev_sr_b, k - 1)])
        emit("vector", lambda e, bt=bt: e.tensor_scalar(
            out=mean_c[0:bt, :], in0=mean_c[0:bt, :], scalar1=1.0 / D, scalar2=None, op0=ALU.mult))
        emit("vector", lambda e, bt=bt: e.tensor_tensor(
            out=var_c[0:bt, :], in0=mean_c[0:bt, :], in1=mean_c[0:bt, :], op=ALU.mult))
        ev_v2_b[k] = emit("vector", lambda e, bt=bt: e.scalar_tensor_tensor(
            out=var_c[0:bt, :], in0=sq_c[0:bt, :], scalar=1.0 / D, in1=var_c[0:bt, :],
            op0=ALU.mult, op1=ALU.subtract))
        wait("scalar", ev_v2_b[k])
        ev_sr_b[k] = emit("scalar", lambda e, bt=bt: e.activation(
            out=rstd_c[0:bt, :], in_=var_c[0:bt, :], func=AF.Sqrt, bias=1e-5))
        wait("vector", ev_sr_b[k])
        emit("vector", lambda e, bt=bt: e.reciprocal(out=rstd_c[0:bt, :], in_=rstd_c[0:bt, :]))
        ev_xn_b[k] = emit("vector", lambda e, bt=bt: e.tensor_scalar(
            out=xn_t[0:bt, :], in0=x_t[0:bt, :], scalar1=mean_c[0:bt, :], scalar2=rstd_c[0:bt, :],
            op0=ALU.subtract, op1=ALU.mult))
        ev_w_b[k] = emit("vector", lambda e, k=k, bt=bt: e.tensor_tensor(
            out=w_t[0:bt, :], in0=xn_t[0:bt, :], in1=dt_full[0:bt, k * D:(k + 1) * D], op=ALU.mult))
        emit("vector", lambda e, k=k, bt=bt: e.memset(sfull[k % 2][0:bt, :], 1.0))
        ev_f = None
        waits("tensor", [pe(ev_stt, (k - 2) * NSLC + NSLC - 1) if k >= 2 else None])
        for h in range(2):
            for j in range(8):
                ev_f = emit("tensor", lambda e, j=j, h=h, bt=bt, t0=t0: e.matmul(
                    out=p_ffn[h][0:bt, :], lhsT=xqT[:, j * L + t0: j * L + t0 + bt],
                    rhs=wqT[:, j * D + h * SL: j * D + h * SL + SL],
                    start=(j == 0), stop=(j == 7)))
        ev_ffn_b[k] = ev_f
        bb = base2[k % 2]
        waits("vector", [pe(ev_store_b, k - 2)])
        emit("vector", lambda e, bt=bt, bb=bb: e.tensor_tensor(
            out=bb[0:bt, :], in0=xn_t[0:bt, :], in1=dbc[0:bt, :], op=ALU.mult))
        ev_basex_b[k] = emit("vector", lambda e, bt=bt, bb=bb: e.tensor_tensor(
            out=bb[0:bt, :], in0=bb[0:bt, :], in1=x_t[0:bt, :], op=ALU.add))
        waits("vector", [ev_ffn_b[k], ev_fbb])
        for h in range(2):
            emit("vector", lambda e, h=h, bt=bt, bb=bb: e.tensor_tensor(
                out=bb[0:bt, h * SL:(h + 1) * SL], in0=bb[0:bt, h * SL:(h + 1) * SL],
                in1=p_ffn[h][0:bt, :], op=ALU.add))
        ev_baseff_b[k] = emit("vector", lambda e, bt=bt, bb=bb: e.tensor_tensor(
            out=bb[0:bt, :], in0=bb[0:bt, :], in1=fbb[0:bt, :], op=ALU.add))

    def emit_2a_stage(k, i):
        t0, bt, regions = blocks[k]
        pat = pat_of(k)
        g0 = k * NSLC
        if i < NSLC:
            s = i
            gs = g0 + s
            b = s % 2
            dh = (s & 1) * SL
            geng = "gpsimd" if s % 2 == 0 else "vector"
            waits(geng, [pe(ev_g, gs - 2)])
            ev_garg[gs] = emit(geng, lambda e, k=k, bt=bt, dh=dh, s=s, b=b: e.tensor_tensor(
                out=garg[b][0:bt, :], in0=dt_full[0:bt, k * D + dh: k * D + dh + SL],
                in1=a16b[0:bt, s * SL:(s + 1) * SL], op=ALU.mult))
        if 0 <= i - 1 < NSLC:
            s = i - 1
            gs = g0 + s
            b = s % 2
            waits("scalar", [ev_garg[gs], pe(ev_ca, gs - 2), ev_w_b[k]])
            ev_g[gs] = emit("scalar", lambda e, bt=bt, b=b: e.activation(
                out=g_t[b][0:bt, :], in_=garg[b][0:bt, :], func=AF.Tanh))
            waits("tensor", [ev_g[gs], pe(ev_en, gs - 4)])
            ev_ca[gs] = emit("tensor", lambda e, pat=pat, bt=bt, s=s: e.matmul(
                out=p_ca[(s >> 1) % 2][:, (s & 1) * SL:((s & 1) + 1) * SL],
                lhsT=tri_c[0:bt, pat * 128:(pat + 1) * 128], rhs=g_t[s % 2][0:bt, :],
                start=True, stop=True))
        if 0 <= i - 2 < NSLC and (i - 2) % 2 == 1:
            s1 = i - 2
            s0 = s1 - 1
            gs1 = g0 + s1
            gs0 = g0 + s0
            pb = (s0 >> 1) % 2
            waits("scalar", [ev_ca[gs1], pe(ev_stt, gs0 - NSLC), pe(ev_stt, gs1 - NSLC), pe(ev_ext_b, k - 1)])
            ev_E[gs0] = ev_E[gs1] = emit("scalar", lambda e, s0=s0, pb=pb: e.activation(
                out=e_blk[:, s0 * SL:(s0 + 2) * SL], in_=p_ca[pb][:], func=AF.Exp, scale=16.0))
            waits("scalar", [pe(ev_r, gs0 - 2)])
            waits("scalar", [pe(ev_r, gs0 - 2)])
            ev_en[gs0] = ev_en[gs1] = emit("scalar", lambda e, bt=bt, pb=pb: e.activation(
                out=en2_t[0:bt, :], in_=p_ca[pb][0:bt, :], func=AF.Exp, scale=-16.0))
        if 0 <= i - 3 < NSLC and (i - 3) % 2 == 1:
            s1 = i - 3
            s0 = s1 - 1
            gs1 = g0 + s1
            gs0 = g0 + s0
            n_i = s0 >> 1
            waits("vector", [ev_en[gs1], pe(ev_smm, gs0 - 2), pe(ev_smm, gs1 - 2)])
            ev_r[gs0] = ev_r[gs1] = emit("vector", lambda e, k=k, bt=bt, n_i=n_i: e.scalar_tensor_tensor(
                out=r2_t[0][0:bt, :], in0=w_t[0:bt, :],
                scalar=bp_full[0:bt, k * N + n_i: k * N + n_i + 1],
                in1=en2_t[0:bt, :], op0=ALU.mult, op1=ALU.mult))
            for s in (s0, s1):
                gs = g0 + s
                waits("tensor", [ev_r[gs], pe(ev_evac, gs - 2)])
                ev_smm[gs] = emit("tensor", lambda e, pat=pat, bt=bt, b=s % 2, s=s: e.matmul(
                    out=p_sp[b][:], lhsT=tri_c[0:bt, pat * 128:(pat + 1) * 128],
                    rhs=r2_t[0][0:bt, (s & 1) * SL:((s & 1) + 1) * SL],
                    start=True, stop=True))
        if 0 <= i - 4 < NSLC:
            s = i - 4
            gs = g0 + s
            b = s % 2
            eng = "scalar" if (s % 2 == 0) else "vector"
            waits(eng, [ev_smm[gs], pe(ev_re, gs - NSLC), pe(ev_ext_b, k - 1)])
            if eng == "scalar":
                ev_evac[gs] = emit("scalar", lambda e, s=s, b=b: e.copy(
                    out=s_blk[:, s * SL:(s + 1) * SL], in_=p_sp[b][:]))
            else:
                ev_evac[gs] = emit("vector", lambda e, s=s, b=b: e.tensor_copy(
                    out=s_blk[:, s * SL:(s + 1) * SL], in_=p_sp[b][:]))

    def emit_2b_stage(k, i):
        t0, bt, regions = blocks[k]
        g0 = k * NSLC
        if i < NSLC:
            s = i
            gs = g0 + s
            waits("tensor", [ev_wb_b[k], ev_evac[gs], pe(ev_stt, gs - 2),
                             ev_baseff_b[k + 1] if k + 1 < NBLK and ev_baseff_b[k + 1] is not None else None])
            ev_re[gs] = emit("tensor", lambda e, k=k, bt=bt, s=s: e.matmul(
                out=p_ffn[s % 2][0:bt, :], lhsT=mre_c[:, pat_of(k) * BT: pat_of(k) * BT + bt],
                rhs=s_blk[:, s * SL:(s + 1) * SL], start=True, stop=True))
        if 0 <= i - 1 < NSLC:
            s = i - 1
            gs = g0 + s
            b = s % 2
            n_i = s >> 1
            dh = (s & 1) * SL
            waits("vector", [ev_re[gs]])
            if s < 2:
                ev_stt[gs] = emit("vector", lambda e, k=k, bt=bt, s=s, n_i=n_i, dh=dh, b=b: e.scalar_tensor_tensor(
                    out=yt_bf[0:bt, dh:dh + SL], in0=e_blk[0:bt, s * SL:(s + 1) * SL],
                    scalar=cp_full[0:bt, k * N + n_i: k * N + n_i + 1], in1=p_ffn[b][0:bt, :],
                    op0=ALU.mult, op1=ALU.mult))
                ev_yadd[gs] = ev_stt[gs]
            else:
                waits("vector", [pe(ev_yadd, gs - 2)])
                ev_stt[gs] = emit("vector", lambda e, k=k, bt=bt, s=s, n_i=n_i, b=b: e.scalar_tensor_tensor(
                    out=ces_t[0:bt, b * SL:(b + 1) * SL], in0=e_blk[0:bt, s * SL:(s + 1) * SL],
                    scalar=cp_full[0:bt, k * N + n_i: k * N + n_i + 1], in1=p_ffn[b][0:bt, :],
                    op0=ALU.mult, op1=ALU.mult))

                def _yadd(e, bt=bt, dh=dh, b=b):
                    with nc.allow_low_precision(reason="y partial sums kept in bf16; rvr-checked"):
                        return e.tensor_tensor(
                            out=yt_bf[0:bt, dh:dh + SL], in0=yt_bf[0:bt, dh:dh + SL],
                            in1=ces_t[0:bt, b * SL:(b + 1) * SL], op=ALU.add)
                waits("gpsimd", [ev_stt[gs]] + ([ev_stt[g0 + (s & 1)]] if s in (2, 3) else []))
                ev_yadd[gs] = emit("gpsimd", _yadd)

    def emit_finalize(k):
        t0, bt, regions = blocks[k]
        bb = base2[k % 2]
        waits("vector", [ev_yadd[k * NSLC + NSLC - 1]])
        ev_out = emit("vector", lambda e, k=k, bt=bt, bb=bb: e.scalar_tensor_tensor(
            out=bb[0:bt, :], in0=yt_bf[0:bt, :], scalar=sfull[k % 2][0:bt, :],
            in1=bb[0:bt, :], op0=ALU.mult, op1=ALU.add))
        wait("sync", ev_out)
        ev_store_b[k] = emit("sync", lambda e, t0=t0, bt=bt, bb=bb: e.dma_start(
            out=out_h[t0:t0 + bt, :], in_=bb[0:bt, :]), inc=16)

    def emit_chain(k):
        t0, bt, regions = blocks[k]
        nreg = len(regions)
        g0 = k * NSLC
        waits("gpsimd", [ev_E[g0 + NSLC - 1], ev_evac[g0 + NSLC - 1], ev_evac[g0 + NSLC - 2],
                         pe(ev_chain_b, k - 1)])
        emit("gpsimd", lambda e, nreg=nreg: e.dma_start(
            out=chE_d[0:nreg, :], in_=e_blk[114:114 + nreg, :]), inc=16)
        emit("gpsimd", lambda e, nreg=nreg: e.dma_start(
            out=chS_d[0:nreg, :], in_=s_blk[114:114 + nreg, :]), inc=16)
        emit("gpsimd", lambda e, nreg=nreg: e.dma_start(
            out=chE[:, 0:nreg * 128].rearrange("p (c j) -> p c j", j=128),
            in_=chE_d[0:nreg, :].rearrange("c (q j) -> q c j", j=128)), inc=16)
        ev_ext = [emit("gpsimd", lambda e, nreg=nreg: e.dma_start(
            out=chS[:, 0:nreg * 128].rearrange("p (c j) -> p c j", j=128),
            in_=chS_d[0:nreg, :].rearrange("c (q j) -> q c j", j=128)), inc=16)]
        ev_ext_b[k] = ev_ext[-1]
        waits("gpsimd", [pe(ev_hcarry_b, k - 1)])
        ev_h0w = [emit("gpsimd", lambda e: e.tensor_copy(out=h0st[:, 0:128], in_=h_t[:]))]
        waits("vector", ev_ext + [pe(ev_wb_b, k - 1)])
        emit("vector", lambda e: e.tensor_tensor(
            out=ht_tmp[:], in0=h_t[:], in1=chS[:, 0:128], op=ALU.add))
        pend_pool_h0 = None
        for c, (lo, hi, clamps) in enumerate(regions):
            last = (c == nreg - 1)
            waits("vector", [pend_pool_h0])
            pend_pool_h0 = None
            if clamps:
                emit("vector", lambda e, c=c: e.scalar_tensor_tensor(
                    out=ht_new[:], in0=chE[:, c * 128:(c + 1) * 128], scalar=1.0,
                    in1=ht_tmp[:], op0=ALU.mult, op1=ALU.mult))
                ev_q = emit("vector", lambda e: e.scalar_tensor_tensor(
                    out=ht_tmp[:], in0=ht_new[:], scalar=1.0, in1=ht_new[:],
                    op0=ALU.mult, op1=ALU.mult, accum_out=qcol[:]))
                wait("tensor", ev_q)
                ev_qs = emit("tensor", lambda e: e.matmul(
                    out=p_s, lhsT=ones_sq[:], rhs=qcol[:], start=True, stop=True))
                wait("scalar", ev_qs)
                ev_rt = emit("scalar", lambda e: e.activation(
                    out=scol[:], in_=p_s, func=AF.Sqrt))
                wait("vector", ev_rt)
                emit("vector", lambda e: e.tensor_scalar(
                    out=scol[:], in0=scol[:], scalar1=1e-8, scalar2=None, op0=ALU.add))
                emit("vector", lambda e: e.reciprocal(out=scol[:], in_=scol[:]))
                emit("vector", lambda e: e.tensor_scalar(
                    out=scol[:], in0=scol[:], scalar1=20.0, scalar2=1.0, op0=ALU.mult, op1=ALU.min))
                if last:
                    waits("vector", [ev_h0w[0]])
                    ev_hcarry_b[k] = emit("vector", lambda e: e.tensor_scalar(
                        out=h_t[:], in0=ht_new[:], scalar1=scol[:], scalar2=None, op0=ALU.mult))
                else:
                    emit("vector", lambda e, c=c: e.scalar_tensor_tensor(
                        out=ht_tmp[:], in0=ht_new[:], scalar=scol[:],
                        in1=chS[:, (c + 1) * 128:(c + 2) * 128], op0=ALU.mult, op1=ALU.add))
                    ev_h0w.append(emit("vector", lambda e, c=c: e.tensor_scalar(
                        out=h0st[:, (c + 1) * 128:(c + 2) * 128], in0=ht_new[:],
                        scalar1=scol[:], scalar2=None, op0=ALU.mult)))
                emit("vector", lambda e: e.tensor_scalar(
                    out=sm1[:], in0=scol[:], scalar1=-1.0, scalar2=None, op0=ALU.add))
                ev_chain_b[k] = emit("vector", lambda e, hi=hi, bt=bt, k=k: e.scalar_tensor_tensor(
                    out=sfull[k % 2][0:bt, :], in0=id_f32[0:bt, hi:hi + 1], scalar=sm1[0:bt, :],
                    in1=sfull[k % 2][0:bt, :], op0=ALU.mult, op1=ALU.add))
            else:
                if last:
                    waits("vector", [ev_h0w[0]])
                    ev_chain_b[k] = ev_hcarry_b[k] = emit("vector", lambda e, c=c: e.tensor_tensor(
                        out=h_t[:], in0=chE[:, c * 128:(c + 1) * 128], in1=ht_tmp[:], op=ALU.mult))
                else:
                    ev_hn = emit("vector", lambda e, c=c: e.tensor_tensor(
                        out=ht_new[:], in0=chE[:, c * 128:(c + 1) * 128], in1=ht_tmp[:], op=ALU.mult))
                    ev_chain_b[k] = emit("vector", lambda e, c=c: e.tensor_tensor(
                        out=ht_tmp[:], in0=ht_new[:], in1=chS[:, (c + 1) * 128:(c + 2) * 128], op=ALU.add))
                    waits("gpsimd", [ev_chain_b[k]])
                    pend_pool_h0 = emit("gpsimd", lambda e, c=c: e.tensor_copy(
                        out=h0st[:, (c + 1) * 128:(c + 2) * 128], in_=ht_new[:]))
                    ev_h0w.append(pend_pool_h0)
        waits("gpsimd", ev_h0w + [pe(ev_hcarry_b, k)])
        emit("gpsimd", lambda e, nreg=nreg: e.dma_start(
            out=h0_d[0:nreg, :].rearrange("c (q j) -> q c j", j=128),
            in_=h0st[:, 0:nreg * 128].rearrange("p (c j) -> p c j", j=128)), inc=16)
        ev_wb_b[k] = emit("gpsimd", lambda e, nreg=nreg: e.dma_start(
            out=s_blk[114:114 + nreg, :], in_=h0_d[0:nreg, :]), inc=16)

    barrier(snapshot())
    NBL = len(blocks)
    if stage >= 4:
        emit_head(0)
        for k in range(NBL):
            for i in range(NSLC + 5):
                emit_2a_stage(k, i)
                if k >= 1 and stage >= 6:
                    emit_2b_stage(k - 1, i)
            if k >= 1 and stage >= 6:
                emit_finalize(k - 1)
            if k + 1 < NBL:
                emit_head(k + 1)
            if stage >= 5:
                emit_chain(k)
            if debug and k == 0:
                barrier(snapshot())
                emit("gpsimd", lambda e: e.dma_start(out=dbg["dbg_E0"], in_=e_blk[:]), inc=16)
                emit("gpsimd", lambda e: e.dma_start(out=dbg["dbg_S0"], in_=s_blk[:]), inc=16)
                emit("gpsimd", lambda e: e.dma_start(out=dbg["dbg_h0s"], in_=h0st[:]), inc=16)
                emit("gpsimd", lambda e: e.dma_start(out=dbg["dbg_xn0"], in_=xn_t[0:BT, :]), inc=16)
                barrier(snapshot())
        if stage >= 6:
            for i in range(NSLC + 2):
                emit_2b_stage(NBL - 1, i)
            emit_finalize(NBL - 1)
            if debug:
                barrier(snapshot())
                emit("gpsimd", lambda e: e.dma_start(out=dbg["dbg_y0"], in_=base2[(NBL - 1) % 2][0:BT, :]), inc=16)
                emit("gpsimd", lambda e, k=0: e.dma_start(out=dbg["dbg_sfull"], in_=sfull[0][:]), inc=16)
                barrier(snapshot())

    if debug:
        barrier(snapshot())
        emit("gpsimd", lambda e: e.dma_start(out=dbg["dbg_dt"], in_=dt_full[:]), inc=16)
        emit("gpsimd", lambda e: e.dma_start(out=dbg["dbg_bp"], in_=bp_full[:]), inc=16)
        emit("gpsimd", lambda e: e.dma_start(out=dbg["dbg_cp"], in_=cp_full[:]), inc=16)

    def thunk_final_d(eng):
        eng.wait_ge(sem_d, cnt["d"])
    prog["sync"].append(thunk_final_d)

    def thunk_final_g(eng):
        eng.wait_ge(sem_g, cnt["g"])
        eng.wait_ge(sem_p, cnt["p"])
    prog["gpsimd"].append(thunk_final_g)

    with es, nc.Block() as block:
        @block.sync
        def _(eng):
            for f in prog["sync"]:
                f(eng)

        @block.gpsimd
        def _(eng):
            for f in prog["gpsimd"]:
                f(eng)

        @block.tensor
        def _(eng):
            for f in prog["tensor"]:
                f(eng)

        @block.vector
        def _(eng):
            for f in prog["vector"]:
                f(eng)

        @block.scalar
        def _(eng):
            for f in prog["scalar"]:
                f(eng)

    return nc


def kernel(L0_attn_x, L0_attn_W_dt, L0_attn_b_dt, L0_attn_W_B, L0_attn_W_C,
           L0_attn_A_log, L0_attn_D_param, L0_ffn_weight, L0_ffn_bias):
    _, consts = build_consts()
    nc = build_nc(debug=False)
    base = {
        "W_dtT": np.ascontiguousarray(np.asarray(L0_attn_W_dt, dtype=np.float32).T),
        "b_dt": np.ascontiguousarray(L0_attn_b_dt, dtype=np.float32).reshape(1, D),
        "W_BT": np.ascontiguousarray(np.asarray(L0_attn_W_B, dtype=np.float32).T),
        "W_CT": np.ascontiguousarray(np.asarray(L0_attn_W_C, dtype=np.float32).T),
        "A_log": np.ascontiguousarray(L0_attn_A_log, dtype=np.float32),
        "D_param": np.ascontiguousarray(L0_attn_D_param, dtype=np.float32).reshape(1, D),
        "W_ffnT": np.ascontiguousarray(np.asarray(L0_ffn_weight, dtype=np.float32).T),
        "b_ffn": np.ascontiguousarray(L0_ffn_bias, dtype=np.float32).reshape(1, D),
    }
    base.update(consts)
    x = np.ascontiguousarray(L0_attn_x, dtype=np.float32)
    in_maps = []
    for i in range(8):
        m = dict(base)
        m["x"] = x[i]
        in_maps.append(m)
    res = run_bass_kernel_spmd(nc, in_maps, core_ids=list(range(8)))
    out = np.stack([np.asarray(r["out"]) for r in res.results], axis=0).astype(np.float32)
    return out


# version 13
# speedup vs baseline: 1.0178x; 1.0178x over previous
import contextlib
import numpy as np
import ml_dtypes
import concourse.bass as bass
import concourse.mybir as mybir
from concourse.bass_utils import run_bass_kernel_spmd

F32 = mybir.dt.float32
BF16 = mybir.dt.bfloat16
FP8 = mybir.dt.float8e4
AF = mybir.ActivationFunctionType
ALU = mybir.AluOpType
AX = mybir.AxisListType

BFD = ml_dtypes.bfloat16
F8D = ml_dtypes.float8_e4m3fn

L = 2048
D = 1024
N = 16
DN = D * N
BT = 114
NBLK = 18
NSLC = 32
SL = 512
RMAX = 13
NPAT = 6

DT_C = (0.1 + 0.001) / 2.0
DT_R = (0.1 - 0.001) / 2.0


def block_structure():
    blocks = []
    for k in range(NBLK):
        t0 = k * BT
        bt = min(BT, L - t0)
        regions = []
        lo = 0
        for r in range(1, bt + 1):
            t = t0 + r
            if r == bt or t % 10 == 1:
                hi = r - 1
                clamps = ((t0 + hi) % 10 == 0)
                regions.append((lo, hi, clamps))
                lo = r
        blocks.append((t0, bt, regions))
    return blocks


def pat_of(k):
    return 5 if k == NBLK - 1 else k % 5


def build_consts():
    blocks = block_structure()
    tri = np.zeros((NPAT, BT, 128), dtype=np.float32)
    mre = np.zeros((NPAT, 128, BT), dtype=np.float32)
    done = set()
    for k, (t0, bt, regions) in enumerate(blocks):
        p = pat_of(k)
        if p in done:
            continue
        done.add(p)
        for ci, (lo, hi, cl) in enumerate(regions):
            for tp in range(lo, hi + 1):
                tri[p, lo:tp + 1, tp] = 1.0
                tri[p, tp, 114 + ci] = 1.0
                mre[p, tp, tp] = 1.0
                mre[p, 114 + ci, tp] = 1.0
    ident = np.eye(128, dtype=np.float32)
    c = {
        "c_tri": tri.reshape(NPAT * BT, 128).astype(BFD),
        "c_mre": mre.reshape(NPAT * 128, BT).astype(BFD),
        "c_id_bf": ident.astype(BFD),
        "c_id_f32": ident.copy(),
        "c_id_f8": ident.astype(F8D),
        "c_ones_bf": np.ones((1, BT), dtype=np.float32).astype(BFD),
        "c_ones_row": np.ones((1, 128), dtype=np.float32),
        "c_ones_p": np.ones((128, 1), dtype=np.float32),
        "c_ones_sq": np.ones((128, 128), dtype=np.float32),
        "c_bias": np.tile(np.array([[1e-5, 1.0, -DT_C / DT_R]], dtype=np.float32), (128, 1)),
    }
    return blocks, c


def build_nc(debug=False, nblk=NBLK, stage=9):
    blocks, _ = build_consts()
    blocks = blocks[:nblk]
    nc = bass.Bass("TRN2", target_bir_lowering=False)

    x_h = nc.dram_tensor("x", [L, D], F32, kind="ExternalInput").ap()
    wdt_h = nc.dram_tensor("W_dtT", [D, D], F32, kind="ExternalInput").ap()
    bdt_h = nc.dram_tensor("b_dt", [1, D], F32, kind="ExternalInput").ap()
    wb_h = nc.dram_tensor("W_BT", [D, N], F32, kind="ExternalInput").ap()
    wc_h = nc.dram_tensor("W_CT", [D, N], F32, kind="ExternalInput").ap()
    alog_h = nc.dram_tensor("A_log", [D, N], F32, kind="ExternalInput").ap()
    dpar_h = nc.dram_tensor("D_param", [1, D], F32, kind="ExternalInput").ap()
    wf_h = nc.dram_tensor("W_ffnT", [D, D], F32, kind="ExternalInput").ap()
    bf_h = nc.dram_tensor("b_ffn", [1, D], F32, kind="ExternalInput").ap()
    tri_h = nc.dram_tensor("c_tri", [NPAT * BT, 128], BF16, kind="ExternalInput").ap()
    mre_h = nc.dram_tensor("c_mre", [NPAT * 128, BT], BF16, kind="ExternalInput").ap()
    idb_h = nc.dram_tensor("c_id_bf", [128, 128], BF16, kind="ExternalInput").ap()
    idf_h = nc.dram_tensor("c_id_f32", [128, 128], F32, kind="ExternalInput").ap()
    id8_h = nc.dram_tensor("c_id_f8", [128, 128], FP8, kind="ExternalInput").ap()
    onesb_h = nc.dram_tensor("c_ones_bf", [1, BT], BF16, kind="ExternalInput").ap()
    onesr_h = nc.dram_tensor("c_ones_row", [1, 128], F32, kind="ExternalInput").ap()
    onesp_h = nc.dram_tensor("c_ones_p", [128, 1], F32, kind="ExternalInput").ap()
    bias_h = nc.dram_tensor("c_bias", [128, 3], F32, kind="ExternalInput").ap()
    onessq_h = nc.dram_tensor("c_ones_sq", [128, 128], F32, kind="ExternalInput").ap()
    a16_d = nc.dram_tensor("a16_scratch", [D, N], BF16, kind="Internal").ap()
    a16_d2 = nc.dram_tensor("a16_scratch2", [N, D], BF16, kind="Internal").ap()
    chE_d = nc.dram_tensor("chE_scratch", [RMAX, DN], BF16, kind="Internal").ap()
    chS_d = nc.dram_tensor("chS_scratch", [RMAX, DN], BF16, kind="Internal").ap()
    h0_d = nc.dram_tensor("h0_scratch", [RMAX, DN], BF16, kind="Internal").ap()
    xq_d = nc.dram_tensor("xq_scratch", [128, NBLK * 8 * BT], FP8, kind="Internal").ap()

    out_h = nc.dram_tensor("out", [L, D], F32, kind="ExternalOutput").ap()
    dbg = {}
    if debug:
        for nm, shp, dt_ in [("dbg_dt", [128, NBLK * D], F32), ("dbg_E0", [128, DN], F32),
                             ("dbg_S0", [128, DN], F32), ("dbg_y0", [BT, D], F32),
                             ("dbg_h0s", [128, RMAX * 128], F32),
                             ("dbg_sfull", [BT, 1], F32), ("dbg_bp", [128, NBLK * N], F32),
                             ("dbg_cp", [128, NBLK * N], F32), ("dbg_xn0", [BT, D], F32),
                             ("dbg_a16", [128, DN], F32)]:
            dbg[nm] = nc.dram_tensor(nm, shp, dt_, kind="ExternalOutput").ap()

    es = contextlib.ExitStack()

    def sb(name, shape, dtype):
        return es.enter_context(nc.sbuf_tensor(name, shape, dtype))

    def ps(name, shape, dtype):
        return es.enter_context(nc.psum_tensor(name, shape, dtype))

    dt_full = sb("dt_full", [128, NBLK * D], BF16)
    xqs = [sb(f"xqs{i}", [128, 8 * BT], FP8) for i in range(2)]
    a16b = sb("a16b", [128, DN], BF16)
    s_blk = sb("s_blk", [128, DN], BF16)
    wqT = sb("wqT", [128, 8 * D], FP8)
    tri_c = sb("tri_c", [BT, NPAT * 128], BF16)
    mre_c = sb("mre_c", [128, NPAT * BT], BF16)
    bp_full = sb("bp_full", [128, NBLK * N], BF16)
    cp_full = sb("cp_full", [128, NBLK * N], F32)
    dbc = sb("dbc", [128, D], BF16)
    fbb = sb("fbb", [128, D], BF16)
    id_bf = sb("id_bf", [128, 128], BF16)
    id_f32 = sb("id_f32", [128, 128], F32)
    id_f8 = sb("id_f8", [128, 128], FP8)
    ones_bf = sb("ones_bf", [1, BT], BF16)
    ones_sq = sb("ones_sq", [128, 128], F32)
    bias_c = sb("bias_c", [128, 3], F32)
    chE = sb("chE", [128, RMAX * 128], BF16)
    chS = sb("chS", [128, RMAX * 128], BF16)
    h0st = sb("h0st", [128, RMAX * 128], BF16)
    h_t = sb("h_t", [128, 128], F32)
    ht_tmp = sb("ht_tmp", [128, 128], F32)
    ht_new = sb("ht_new", [128, 128], F32)
    sfull = [sb(f"sfull{i}", [BT, 1], F32) for i in range(2)]
    qcol = sb("qcol", [128, 1], F32)
    scol = sb("scol", [128, 1], F32)
    sm1 = sb("sm1", [128, 1], F32)
    x_t = sb("x_t", [128, D], F32)
    xn_t = sb("xn_t", [128, D], F32)
    w_t = sb("w_t", [128, D], BF16)
    mean_c = sb("mean_c", [128, 1], F32)
    sq_c = sb("sq_c", [128, 1], F32)
    var_c = sb("var_c", [128, 1], F32)
    rstd_c = sb("rstd_c", [128, 1], F32)
    garg = [sb(f"garg{i}", [BT, SL], BF16) for i in range(4)]
    g_t = [sb(f"g_t{i}", [BT, SL], BF16) for i in range(2)]
    en2_t = sb("en2_t", [BT, 2 * SL], BF16)
    r2_t = [sb(f"r2_t{i}", [BT, 2 * SL], BF16) for i in range(1)]
    ces_t = sb("ces_t", [BT, 4 * SL], BF16)
    yt_bf = sb("yt_bf", [BT, D], BF16)
    yt_blk = sb("yt_blk", [BT, D], F32)
    base_b = sb("base_b", [BT, D], F32)

    off0 = (nc.sbuf_base + 31) & ~31
    arena = sb("arena", [128, 16384], BF16)

    def at(name, shape, dtype, byte_off):
        return nc.alloc_sbuf_tensor_at(name, shape, dtype, offset=off0 + byte_off)

    e_blk = arena
    wdtT = at("wdtT", [128, 8 * D], BF16, 0)
    wslab = at("wslab", [128, D], F32, 16384)
    wslab2 = at("wslab2", [128, D], BF16, 20480)
    wslab8 = at("wslab8", [128, D], FP8, 22528)
    xnT_t = at("xnT_t", [128, 8 * BT], BF16, 23552)
    ep_e = at("ep_e", [128, 2 * SL], F32, 25376)
    wbT = at("wbT", [128, 8 * N], BF16, 29472)
    wcT = at("wcT", [128, 8 * N], BF16, 29728)
    bdt_row = at("bdt_row", [1, D], BF16, 29984)
    atile = at("atile", [128, 128], F32, 0)
    a16t = at("a16t", [128, 128], BF16, 512)

    p_ca = [ps(f"p_ca{i}", [128, 2 * SL], F32) for i in range(2)]
    p_sp = [ps(f"p_sp{i}", [128, SL], F32) for i in range(2)]
    p_ffn = [ps(f"p_ffn{i}", [128, SL], F32) for i in range(2)]
    p_trb = [p_ca[i][:].bitcast(BF16)[:, 0:128] for i in range(2)]
    p_tr8 = [p_ca[i][:].bitcast(FP8)[:, 512:768].rearrange("p (j t) -> p j t", t=2)[:, :, 0:1].squeeze(2) for i in range(2)]
    p_s = p_sp[0][:, 300:301]

    sem_d = es.enter_context(nc.semaphore("sem_d"))
    sem_g = es.enter_context(nc.semaphore("sem_g"))
    sem_p = es.enter_context(nc.semaphore("sem_p"))
    sem_t = es.enter_context(nc.semaphore("sem_t"))
    sem_v = es.enter_context(nc.semaphore("sem_v"))
    sem_s = es.enter_context(nc.semaphore("sem_s"))

    cnt = {"d": 0, "g": 0, "p": 0, "t": 0, "v": 0, "s": 0}
    sems = {"d": sem_d, "g": sem_g, "p": sem_p, "t": sem_t, "v": sem_v, "s": sem_s}
    prog = {"sync": [], "gpsimd": [], "tensor": [], "vector": [], "scalar": []}
    ENG_SEM = {"sync": "d", "gpsimd": "g", "tensor": "t", "vector": "v", "scalar": "s"}
    ENG_KEYS = {"sync": ("d",), "gpsimd": ("g", "p"), "tensor": ("t",),
                "vector": ("v",), "scalar": ("s",)}

    def emit(engine, fn, inc=1):
        if engine == "gpsimd":
            key = "g" if inc == 16 else "p"
        else:
            key = ENG_SEM[engine]
        befores = [(kk, cnt[kk]) for kk in ENG_KEYS[engine]]
        cnt[key] += inc
        sem = sems[key]
        if engine != "tensor":
            for kk, before in befores:
                if before > 0:
                    def wthunk(eng, sem=sems[kk], before=before):
                        eng.wait_ge(sem, before)
                    prog[engine].append(wthunk)

        def thunk(eng, fn=fn, sem=sem, inc=inc):
            fn(eng).then_inc(sem, inc)
        prog[engine].append(thunk)
        return (key, cnt[key])

    def wait(engine, ev):
        if ev is None:
            return
        key, val = ev
        if key in ENG_KEYS[engine]:
            return

        def thunk(eng, key=key, val=val):
            eng.wait_ge(sems[key], val)
        prog[engine].append(thunk)

    def waits(engine, evs):
        best = {}
        for ev in evs:
            if ev is None:
                continue
            kk, vv = ev
            if kk not in ENG_KEYS[engine] and best.get(kk, -1) < vv:
                best[kk] = vv
        for kk, vv in best.items():
            wait(engine, (kk, vv))

    def snapshot():
        return dict(cnt)

    def barrier(snap):
        evs = [(kk, vv) for kk, vv in snap.items()]
        for eng in ("tensor", "vector", "scalar", "gpsimd", "sync"):
            waits(eng, evs)

    ev_pre = []
    for dst, src in [(id_bf[:], idb_h), (id_f32[:], idf_h), (id_f8[:], id8_h),
                     (ones_bf[:], onesb_h),
                     (bias_c[:], bias_h), (ones_sq[:], onessq_h),
                     (atile[:], alog_h.rearrange("(q p) n -> p q n", p=128))]:
        ev_pre.append(emit("sync", lambda e, d=dst, s=src: e.dma_start(out=d, in_=s), inc=16))
    for p in range(NPAT):
        ev_pre.append(emit("sync", lambda e, p=p: e.dma_start(
            out=tri_c[:, p * 128:(p + 1) * 128], in_=tri_h[p * BT:(p + 1) * BT, :]), inc=16))
        ev_pre.append(emit("sync", lambda e, p=p: e.dma_start(
            out=mre_c[:, p * BT:(p + 1) * BT], in_=mre_h[p * 128:(p + 1) * 128, :]), inc=16))

    nc.const_aps.register(F32, 1e-5, bias_c[:, 0:1])
    nc.const_aps.register(F32, 1.0, bias_c[:, 1:2])
    nc.const_aps.register(F32, -DT_C / DT_R, bias_c[:, 2:3])
    waits("scalar", ev_pre)
    emit("scalar", lambda e: e.activation(out=atile[:], in_=atile[:], func=AF.Tanh, scale=0.1))
    ev = emit("scalar", lambda e: e.activation(out=atile[:], in_=atile[:], func=AF.Exp, scale=10.0))
    wait("vector", ev)
    ev_a16 = emit("vector", lambda e: e.tensor_scalar(
        out=a16t[:], in0=atile[:], scalar1=-1.0 / 16.0, scalar2=None, op0=ALU.mult))
    wait("gpsimd", ev_a16)
    ev_a16d = emit("gpsimd", lambda e: e.dma_start(out=a16_d.rearrange("(q p) n -> p q n", p=128), in_=a16t[:].rearrange("p (q n) -> p q n", n=N)), inc=16)
    wait("gpsimd", ev_a16d)
    for ni in range(N):
        ev_a16d = emit("gpsimd", lambda e, ni=ni: e.dma_start(
            out=a16_d2[ni:ni + 1, :], in_=a16_d[:, ni:ni + 1].rearrange("d n -> n d"),
            allow_slow_non_contiguous=True), inc=16)
    wait("gpsimd", ev_a16d)
    ev_a16b = emit("gpsimd", lambda e: e.dma_start(
        out=a16b[:],
        in_=a16_d2.rearrange("n d -> (n d)").unsqueeze(0).partition_broadcast(128).squeeze(1)), inc=16)
    waits("gpsimd", ev_pre)
    ev_dbc = emit("gpsimd", lambda e: e.dma_start(
        out=dbc[:], in_=dpar_h.partition_broadcast(128).squeeze(1)), inc=16)
    ev_fbb = emit("gpsimd", lambda e: e.dma_start(
        out=fbb[:], in_=bf_h.partition_broadcast(128).squeeze(1)), inc=16)

    ev_bd = emit("sync", lambda e: e.dma_start(out=x_t[0:1, :], in_=bdt_h), inc=16)
    wait("vector", ev_bd)
    ev_bdt = emit("vector", lambda e: e.tensor_copy(out=bdt_row[:], in_=x_t[0:1, :]))

    if stage >= 2:
        waits("sync", [ev_a16d, ev_dbc, ev_fbb, ev_a16b])
        waits("tensor", ev_pre)
        ev_cp_prev = ev_bdt
        for i in range(8):
            waits("sync", [ev_cp_prev])
            ev1 = emit("sync", lambda e, i=i: e.dma_start(out=wslab[:], in_=wdt_h[i * 128:(i + 1) * 128, :]), inc=16)
            wait("vector", ev1)
            ev_cp_prev = emit("vector", lambda e, i=i: e.tensor_copy(
                out=wdtT[:, i * D:(i + 1) * D], in_=wslab[:]))
        for i in range(8):
            waits("sync", [ev_cp_prev])
            ev1 = emit("sync", lambda e, i=i: e.dma_start(out=wslab[:], in_=wf_h[i * 128:(i + 1) * 128, :]), inc=16)
            wait("scalar", ev1)
            ev_cp_prev = emit("scalar", lambda e, i=i: e.copy(
                out=wqT[:, i * D:(i + 1) * D], in_=wslab[:]))
        waits("sync", [ev_cp_prev])
        ev1 = emit("sync", lambda e: e.dma_start(
            out=wslab[:, 0:8 * N].rearrange("p (i n) -> p i n", n=N),
            in_=wb_h.rearrange("(i p) n -> p i n", p=128)), inc=16)
        ev1b = emit("sync", lambda e: e.dma_start(
            out=wslab[:, 8 * N:16 * N].rearrange("p (i n) -> p i n", n=N),
            in_=wc_h.rearrange("(i p) n -> p i n", p=128)), inc=16)
        waits("vector", [ev1, ev1b])
        emit("vector", lambda e: e.tensor_copy(out=wbT[:], in_=wslab[:, 0:8 * N]))
        ev_cp_prev = emit("vector", lambda e: e.tensor_copy(out=wcT[:], in_=wslab[:, 8 * N:16 * N]))

    emit("vector", lambda e: e.memset(h_t[:], 0.0))
    emit("vector", lambda e: e.memset(h0st[:], 0.0))
    emit("vector", lambda e: e.memset(s_blk[:], 0.0))
    emit("vector", lambda e: e.memset(chE[:], 0.0))
    emit("vector", lambda e: e.memset(chS[:], 0.0))
    if debug:
        emit("vector", lambda e: e.memset(dt_full[:], 0.0))
        emit("vector", lambda e: e.memset(bp_full[:], 0.0))
        emit("vector", lambda e: e.memset(cp_full[:], 0.0))
        emit("vector", lambda e: e.memset(sfull[0][:], 0.0))
        emit("vector", lambda e: e.memset(sfull[1][:], 0.0))
        emit("vector", lambda e: e.memset(yt_blk[:], 0.0))
        emit("vector", lambda e: e.memset(xn_t[:], 0.0))
        barrier(snapshot())
    if debug:
        waits("gpsimd", [ev_a16b])
        emit("gpsimd", lambda e: e.dma_start(out=dbg["dbg_a16"], in_=a16b[:]), inc=16)

    a_sq = [None] * NBLK
    a_v2 = [None] * NBLK
    a_rstd = [None] * NBLK
    a_xn = [None] * NBLK
    a_trlast = [None] * NBLK
    a_xnTl = [None] * NBLK
    a_cp = [None] * NBLK
    a_e1 = [[None, None] for _ in range(NBLK)]
    a_xqd = [None] * NBLK
    a_xqin = [None] * NBLK
    for k, (t0, bt, regions) in enumerate(blocks if stage >= 3 else []):
        def pa(lst, i):
            return lst[i] if i >= 0 else None
        waits("sync", [pa(a_sq, k - 1), pa(a_xn, k - 1)])
        evx = emit("sync", lambda e, t0=t0, bt=bt: e.dma_start(
            out=x_t[0:bt, :], in_=x_h[t0:t0 + bt, :]), inc=16)
        wait("vector", evx)
        emit("vector", lambda e, bt=bt: e.tensor_reduce(
            out=mean_c[0:bt, :], in_=x_t[0:bt, :], axis=AX.X, op=ALU.add))
        waits("scalar", [evx, pa(a_v2, k - 1)])
        a_sq[k] = emit("scalar", lambda e, bt=bt: e.activation(
            out=yt_blk[0:bt, :], in_=x_t[0:bt, :], func=AF.Square, accum_out=sq_c[0:bt, :]))
        waits("vector", [a_sq[k], pa(a_rstd, k - 1)])
        emit("vector", lambda e, bt=bt: e.tensor_scalar(
            out=mean_c[0:bt, :], in0=mean_c[0:bt, :], scalar1=1.0 / D, scalar2=None, op0=ALU.mult))
        emit("vector", lambda e, bt=bt: e.tensor_tensor(
            out=var_c[0:bt, :], in0=mean_c[0:bt, :], in1=mean_c[0:bt, :], op=ALU.mult))
        a_v2[k] = emit("vector", lambda e, bt=bt: e.scalar_tensor_tensor(
            out=var_c[0:bt, :], in0=sq_c[0:bt, :], scalar=1.0 / D, in1=var_c[0:bt, :],
            op0=ALU.mult, op1=ALU.subtract))
        waits("scalar", [a_v2[k], pa(a_xn, k - 1)])
        emit("scalar", lambda e, bt=bt: e.activation(
            out=rstd_c[0:bt, :], in_=var_c[0:bt, :], func=AF.Ln, bias=1e-5))
        a_rstd[k] = emit("scalar", lambda e, bt=bt: e.activation(
            out=rstd_c[0:bt, :], in_=rstd_c[0:bt, :], func=AF.Exp, scale=-0.5))
        waits("vector", [a_rstd[k], pa(a_trlast, k - 1)])
        a_xn[k] = emit("vector", lambda e, bt=bt: e.tensor_scalar(
            out=xn_t[0:bt, :], in0=x_t[0:bt, :], scalar1=mean_c[0:bt, :], scalar2=rstd_c[0:bt, :],
            op0=ALU.subtract, op1=ALU.mult))
        ev_tr = [None] * 8
        ev_evv = [None] * 8
        ev_evs = [None] * 8
        for j in range(9):
            if j < 8:
                waits("tensor", [a_xn[k], pa(a_cp, k - 1),
                                 ev_evs[j - 2] if j >= 2 else None])
                ev_tr[j] = emit("tensor", lambda e, j=j, bt=bt: e.transpose(
                    out=p_ca[j % 2][:, 0:bt], in_=xn_t[0:bt, j * 128:(j + 1) * 128],
                    identity=id_f32[0:bt, 0:bt]))
            if j >= 1:
                jj = j - 1
                wait("vector", ev_tr[jj])
                ev_evv[jj] = emit("vector", lambda e, jj=jj, bt=bt: e.tensor_copy(
                    out=xnT_t[:, jj * BT: jj * BT + bt], in_=p_ca[jj % 2][:, 0:bt]))
                waits("scalar", [ev_tr[jj], ev_evv[jj], pa(a_xqd, k - 2) if jj == 0 else None])
                ev_evs[jj] = emit("scalar", lambda e, jj=jj, bt=bt, k=k: e.copy(
                    out=xqs[k % 2][:, jj * BT: jj * BT + bt], in_=p_ca[jj % 2][:, 0:bt]))
        a_trlast[k] = ev_tr[7]
        a_xnTl[k] = ev_evv[7]
        wait("sync", ev_evs[7])
        a_xqd[k] = emit("sync", lambda e, k=k, bt=bt: e.dma_start(
            out=xq_d[:, k * 8 * BT:(k + 1) * 8 * BT].rearrange("p (j t) -> p j t", t=BT)[:, :, 0:bt],
            in_=xqs[k % 2][:].rearrange("p (j t) -> p j t", t=BT)[:, :, 0:bt]), inc=16)
        ev_mmd = [None, None]
        for h in range(2):
            waits("tensor", [a_xnTl[k], a_e1[k - 1][h] if k >= 1 else None])
            for j in range(8):
                emit("tensor", lambda e, j=j, h=h, bt=bt: e.matmul(
                    out=p_ffn[h][0:bt, :], lhsT=xnT_t[:, j * BT: j * BT + bt],
                    rhs=wdtT[:, j * D + h * SL: j * D + h * SL + SL],
                    start=(j == 0), stop=False))
            ev_mmd[h] = emit("tensor", lambda e, h=h, bt=bt: e.matmul(
                out=p_ffn[h][0:bt, :], lhsT=ones_bf[:, 0:bt],
                rhs=bdt_row[:, h * SL: h * SL + SL], start=False, stop=True))
        for h in range(2):
            waits("scalar", [ev_mmd[h]])
            a_e1[k][h] = emit("scalar", lambda e, h=h, bt=bt: e.activation(
                out=ep_e[0:bt, h * SL:(h + 1) * SL], in_=p_ffn[h][0:bt, :], func=AF.Exp))
        for h in range(2):
            emit("scalar", lambda e, h=h, bt=bt: e.activation(
                out=ep_e[0:bt, h * SL:(h + 1) * SL], in_=ep_e[0:bt, h * SL:(h + 1) * SL],
                func=AF.Ln, bias=1.0))
        ev_sg = [None, None]
        for h in range(2):
            ev_sg[h] = emit("scalar", lambda e, h=h, bt=bt: e.activation(
                out=ep_e[0:bt, h * SL:(h + 1) * SL], in_=ep_e[0:bt, h * SL:(h + 1) * SL],
                func=AF.Sigmoid, scale=1.0 / DT_R, bias=-DT_C / DT_R))
        for h in range(2):
            wait("vector", ev_sg[h])
            emit("vector", lambda e, h=h, k=k, bt=bt: e.tensor_scalar(
                out=dt_full[0:bt, k * D + h * SL: k * D + h * SL + SL],
                in0=ep_e[0:bt, h * SL:(h + 1) * SL],
                scalar1=4.0 * DT_R, scalar2=DT_C - 2.0 * DT_R, op0=ALU.mult, op1=ALU.add))
        waits("tensor", [a_xnTl[k], ev_evs[6], ev_evs[7]])
        for j in range(8):
            emit("tensor", lambda e, j=j, bt=bt: e.matmul(
                out=p_ca[0][0:bt, 0:N], lhsT=xnT_t[:, j * BT: j * BT + bt],
                rhs=wbT[:, j * N:(j + 1) * N], start=(j == 0), stop=(j == 7)))
        for j in range(8):
            ev_mmc = emit("tensor", lambda e, j=j, bt=bt: e.matmul(
                out=p_ca[0][0:bt, N:2 * N], lhsT=xnT_t[:, j * BT: j * BT + bt],
                rhs=wcT[:, j * N:(j + 1) * N], start=(j == 0), stop=(j == 7)))
        wait("vector", ev_mmc)
        emit("vector", lambda e, k=k, bt=bt: e.tensor_copy(
            out=bp_full[0:bt, k * N:(k + 1) * N], in_=p_ca[0][0:bt, 0:N]))
        a_cp[k] = emit("vector", lambda e, k=k, bt=bt: e.tensor_copy(
            out=cp_full[0:bt, k * N:(k + 1) * N], in_=p_ca[0][0:bt, N:2 * N]))

    NG = NBLK * NSLC
    ev_g = [None] * NG
    ev_garg = [None] * NG
    ev_ca = [None] * NG
    ev_E = [None] * NG
    ev_en = [None] * NG
    ev_r = [None] * NG
    ev_smm = [None] * NG
    ev_evac = [None] * NG
    ev_re = [None] * NG
    ev_stt = [None] * NG
    ev_yadd = [None] * NG
    ev_sq_b = [None] * NBLK
    ev_v2_b = [None] * NBLK
    ev_sr_b = [None] * NBLK
    ev_xn_b = [None] * NBLK
    ev_w_b = [None] * NBLK
    ev_ext_b = [None] * NBLK
    ev_chain_b = [None] * NBLK
    ev_wb_b = [None] * NBLK
    ev_basex_b = [None] * NBLK
    ev_baseff_b = [None] * NBLK
    ev_store_b = [None] * NBLK
    ev_hcarry_b = [None] * NBLK
    ev_ffn_b = [None] * NBLK

    def pe(lst, i):
        return lst[i] if i >= 0 else None

    base2 = [yt_blk, base_b]

    def emit_head(k):
        t0, bt, regions = blocks[k]
        waits("sync", [pe(ev_sq_b, k - 1), pe(ev_basex_b, k - 1)])
        evx = emit("sync", lambda e, t0=t0, bt=bt: e.dma_start(
            out=x_t[0:bt, :], in_=x_h[t0:t0 + bt, :]), inc=16)
        wait("vector", evx)
        emit("vector", lambda e, bt=bt: e.tensor_reduce(
            out=mean_c[0:bt, :], in_=x_t[0:bt, :], axis=AX.X, op=ALU.add))
        waits("scalar", [evx, pe(ev_v2_b, k - 1), pe(ev_r, k * NSLC - 1)])
        ev_sq_b[k] = emit("scalar", lambda e, bt=bt: e.activation(
            out=w_t[0:bt, :], in_=x_t[0:bt, :], func=AF.Square, accum_out=sq_c[0:bt, :]))
        waits("vector", [ev_sq_b[k], pe(ev_sr_b, k - 1)])
        emit("vector", lambda e, bt=bt: e.tensor_scalar(
            out=mean_c[0:bt, :], in0=mean_c[0:bt, :], scalar1=1.0 / D, scalar2=None, op0=ALU.mult))
        emit("vector", lambda e, bt=bt: e.tensor_tensor(
            out=var_c[0:bt, :], in0=mean_c[0:bt, :], in1=mean_c[0:bt, :], op=ALU.mult))
        ev_v2_b[k] = emit("vector", lambda e, bt=bt: e.scalar_tensor_tensor(
            out=var_c[0:bt, :], in0=sq_c[0:bt, :], scalar=1.0 / D, in1=var_c[0:bt, :],
            op0=ALU.mult, op1=ALU.subtract))
        wait("scalar", ev_v2_b[k])
        ev_sr_b[k] = emit("scalar", lambda e, bt=bt: e.activation(
            out=rstd_c[0:bt, :], in_=var_c[0:bt, :], func=AF.Sqrt, bias=1e-5))
        wait("vector", ev_sr_b[k])
        emit("vector", lambda e, bt=bt: e.reciprocal(out=rstd_c[0:bt, :], in_=rstd_c[0:bt, :]))
        ev_xn_b[k] = emit("vector", lambda e, bt=bt: e.tensor_scalar(
            out=xn_t[0:bt, :], in0=x_t[0:bt, :], scalar1=mean_c[0:bt, :], scalar2=rstd_c[0:bt, :],
            op0=ALU.subtract, op1=ALU.mult))
        ev_w_b[k] = emit("vector", lambda e, k=k, bt=bt: e.tensor_tensor(
            out=w_t[0:bt, :], in0=xn_t[0:bt, :], in1=dt_full[0:bt, k * D:(k + 1) * D], op=ALU.mult))
        emit("vector", lambda e, k=k, bt=bt: e.memset(sfull[k % 2][0:bt, :], 1.0))
        waits("sync", [pe(ev_ffn_b, k - 2)])
        a_xqin[k] = emit("sync", lambda e, k=k, bt=bt: e.dma_start(
            out=xqs[k % 2][:].rearrange("p (j t) -> p j t", t=BT)[:, :, 0:bt],
            in_=xq_d[:, k * 8 * BT:(k + 1) * 8 * BT].rearrange("p (j t) -> p j t", t=BT)[:, :, 0:bt]), inc=16)
        ev_f = None
        waits("tensor", [pe(ev_stt, (k - 2) * NSLC + NSLC - 1) if k >= 2 else None, a_xqin[k]])
        for h in range(2):
            for j in range(8):
                ev_f = emit("tensor", lambda e, j=j, h=h, bt=bt, k=k: e.matmul(
                    out=p_ffn[h][0:bt, :], lhsT=xqs[k % 2][:, j * BT: j * BT + bt],
                    rhs=wqT[:, j * D + h * SL: j * D + h * SL + SL],
                    start=(j == 0), stop=(j == 7)))
        ev_ffn_b[k] = ev_f
        bb = base2[k % 2]
        waits("vector", [pe(ev_store_b, k - 2)])
        emit("vector", lambda e, bt=bt, bb=bb: e.tensor_tensor(
            out=bb[0:bt, :], in0=xn_t[0:bt, :], in1=dbc[0:bt, :], op=ALU.mult))
        ev_basex_b[k] = emit("vector", lambda e, bt=bt, bb=bb: e.tensor_tensor(
            out=bb[0:bt, :], in0=bb[0:bt, :], in1=x_t[0:bt, :], op=ALU.add))
        waits("vector", [ev_ffn_b[k], ev_fbb])
        for h in range(2):
            emit("vector", lambda e, h=h, bt=bt, bb=bb: e.tensor_tensor(
                out=bb[0:bt, h * SL:(h + 1) * SL], in0=bb[0:bt, h * SL:(h + 1) * SL],
                in1=p_ffn[h][0:bt, :], op=ALU.add))
        ev_baseff_b[k] = emit("vector", lambda e, bt=bt, bb=bb: e.tensor_tensor(
            out=bb[0:bt, :], in0=bb[0:bt, :], in1=fbb[0:bt, :], op=ALU.add))

    def emit_2a_stage(k, i):
        t0, bt, regions = blocks[k]
        pat = pat_of(k)
        g0 = k * NSLC
        if i < NSLC:
            s = i
            gs = g0 + s
            b = s % 4
            dh = (s & 1) * SL
            geng = "gpsimd" if s % 2 == 0 else "vector"
            waits(geng, [pe(ev_g, gs - 4)])
            ev_garg[gs] = emit(geng, lambda e, k=k, bt=bt, dh=dh, s=s, b=b: e.tensor_tensor(
                out=garg[b][0:bt, :], in0=dt_full[0:bt, k * D + dh: k * D + dh + SL],
                in1=a16b[0:bt, s * SL:(s + 1) * SL], op=ALU.mult))
        if 0 <= i - 1 < NSLC:
            s = i - 1
            gs = g0 + s
            waits("scalar", [ev_garg[gs], pe(ev_ca, gs - 2), ev_w_b[k]])
            ev_g[gs] = emit("scalar", lambda e, bt=bt, gb=s % 4, b=s % 2: e.activation(
                out=g_t[b][0:bt, :], in_=garg[gb][0:bt, :], func=AF.Tanh))
            waits("tensor", [ev_g[gs], pe(ev_en, gs - 4)])
            ev_ca[gs] = emit("tensor", lambda e, pat=pat, bt=bt, s=s: e.matmul(
                out=p_ca[(s >> 1) % 2][:, (s & 1) * SL:((s & 1) + 1) * SL],
                lhsT=tri_c[0:bt, pat * 128:(pat + 1) * 128], rhs=g_t[s % 2][0:bt, :],
                start=True, stop=True))
        if 0 <= i - 2 < NSLC and (i - 2) % 2 == 1:
            s1 = i - 2
            s0 = s1 - 1
            gs1 = g0 + s1
            gs0 = g0 + s0
            pb = (s0 >> 1) % 2
            waits("scalar", [ev_ca[gs1], pe(ev_stt, gs0 - NSLC), pe(ev_stt, gs1 - NSLC), pe(ev_ext_b, k - 1)])
            ev_E[gs0] = ev_E[gs1] = emit("scalar", lambda e, s0=s0, pb=pb: e.activation(
                out=e_blk[:, s0 * SL:(s0 + 2) * SL], in_=p_ca[pb][:], func=AF.Exp, scale=16.0))
            waits("scalar", [pe(ev_r, gs0 - 2)])
            waits("scalar", [pe(ev_r, gs0 - 2)])
            ev_en[gs0] = ev_en[gs1] = emit("scalar", lambda e, bt=bt, pb=pb: e.activation(
                out=en2_t[0:bt, :], in_=p_ca[pb][0:bt, :], func=AF.Exp, scale=-16.0))
        if 0 <= i - 3 < NSLC and (i - 3) % 2 == 1:
            s1 = i - 3
            s0 = s1 - 1
            gs1 = g0 + s1
            gs0 = g0 + s0
            n_i = s0 >> 1
            waits("vector", [ev_en[gs1], pe(ev_smm, gs0 - 2), pe(ev_smm, gs1 - 2)])
            ev_r[gs0] = ev_r[gs1] = emit("vector", lambda e, k=k, bt=bt, n_i=n_i: e.scalar_tensor_tensor(
                out=r2_t[0][0:bt, :], in0=w_t[0:bt, :],
                scalar=bp_full[0:bt, k * N + n_i: k * N + n_i + 1],
                in1=en2_t[0:bt, :], op0=ALU.mult, op1=ALU.mult))
            for s in (s0, s1):
                gs = g0 + s
                waits("tensor", [ev_r[gs], pe(ev_evac, gs - 2)])
                ev_smm[gs] = emit("tensor", lambda e, pat=pat, bt=bt, b=s % 2, s=s: e.matmul(
                    out=p_sp[b][:], lhsT=tri_c[0:bt, pat * 128:(pat + 1) * 128],
                    rhs=r2_t[0][0:bt, (s & 1) * SL:((s & 1) + 1) * SL],
                    start=True, stop=True))
        if 0 <= i - 4 < NSLC:
            s = i - 4
            gs = g0 + s
            b = s % 2
            eng = "scalar" if (s % 2 == 0) else "vector"
            waits(eng, [ev_smm[gs], pe(ev_re, gs - NSLC), pe(ev_ext_b, k - 1)])
            if eng == "scalar":
                ev_evac[gs] = emit("scalar", lambda e, s=s, b=b: e.copy(
                    out=s_blk[:, s * SL:(s + 1) * SL], in_=p_sp[b][:]))
            else:
                ev_evac[gs] = emit("vector", lambda e, s=s, b=b: e.tensor_copy(
                    out=s_blk[:, s * SL:(s + 1) * SL], in_=p_sp[b][:]))

    def emit_2b_stage(k, i):
        t0, bt, regions = blocks[k]
        g0 = k * NSLC
        if i < NSLC:
            s = i
            gs = g0 + s
            waits("tensor", [ev_wb_b[k], ev_evac[gs], pe(ev_stt, gs - 2),
                             ev_baseff_b[k + 1] if k + 1 < NBLK and ev_baseff_b[k + 1] is not None else None])
            ev_re[gs] = emit("tensor", lambda e, k=k, bt=bt, s=s: e.matmul(
                out=p_ffn[s % 2][0:bt, :], lhsT=mre_c[:, pat_of(k) * BT: pat_of(k) * BT + bt],
                rhs=s_blk[:, s * SL:(s + 1) * SL], start=True, stop=True))
        if 0 <= i - 1 < NSLC:
            s = i - 1
            gs = g0 + s
            b = s % 2
            n_i = s >> 1
            dh = (s & 1) * SL
            waits("vector", [ev_re[gs]])
            if s < 2:
                ev_stt[gs] = emit("vector", lambda e, k=k, bt=bt, s=s, n_i=n_i, dh=dh, b=b: e.scalar_tensor_tensor(
                    out=yt_bf[0:bt, dh:dh + SL], in0=e_blk[0:bt, s * SL:(s + 1) * SL],
                    scalar=cp_full[0:bt, k * N + n_i: k * N + n_i + 1], in1=p_ffn[b][0:bt, :],
                    op0=ALU.mult, op1=ALU.mult))
                ev_yadd[gs] = ev_stt[gs]
            else:
                waits("vector", [pe(ev_yadd, gs - 4)])
                ev_stt[gs] = emit("vector", lambda e, k=k, bt=bt, s=s, n_i=n_i, cb=s % 4, b=b: e.scalar_tensor_tensor(
                    out=ces_t[0:bt, cb * SL:(cb + 1) * SL], in0=e_blk[0:bt, s * SL:(s + 1) * SL],
                    scalar=cp_full[0:bt, k * N + n_i: k * N + n_i + 1], in1=p_ffn[b][0:bt, :],
                    op0=ALU.mult, op1=ALU.mult))

                def _yadd(e, bt=bt, dh=dh, cb=s % 4):
                    with nc.allow_low_precision(reason="y partial sums kept in bf16; rvr-checked"):
                        return e.tensor_tensor(
                            out=yt_bf[0:bt, dh:dh + SL], in0=yt_bf[0:bt, dh:dh + SL],
                            in1=ces_t[0:bt, cb * SL:(cb + 1) * SL], op=ALU.add)
                waits("gpsimd", [ev_stt[gs]] + ([ev_stt[g0 + (s & 1)]] if s in (2, 3) else []))
                ev_yadd[gs] = emit("gpsimd", _yadd)

    def emit_finalize(k):
        t0, bt, regions = blocks[k]
        bb = base2[k % 2]
        waits("vector", [ev_yadd[k * NSLC + NSLC - 1]])
        ev_out = emit("vector", lambda e, k=k, bt=bt, bb=bb: e.scalar_tensor_tensor(
            out=bb[0:bt, :], in0=yt_bf[0:bt, :], scalar=sfull[k % 2][0:bt, :],
            in1=bb[0:bt, :], op0=ALU.mult, op1=ALU.add))
        wait("sync", ev_out)
        ev_store_b[k] = emit("sync", lambda e, t0=t0, bt=bt, bb=bb: e.dma_start(
            out=out_h[t0:t0 + bt, :], in_=bb[0:bt, :]), inc=16)

    def emit_chain(k):
        t0, bt, regions = blocks[k]
        nreg = len(regions)
        g0 = k * NSLC
        waits("gpsimd", [ev_E[g0 + NSLC - 1], ev_evac[g0 + NSLC - 1], ev_evac[g0 + NSLC - 2],
                         pe(ev_chain_b, k - 1)])
        emit("gpsimd", lambda e, nreg=nreg: e.dma_start(
            out=chE_d[0:nreg, :], in_=e_blk[114:114 + nreg, :]), inc=16)
        emit("gpsimd", lambda e, nreg=nreg: e.dma_start(
            out=chS_d[0:nreg, :], in_=s_blk[114:114 + nreg, :]), inc=16)
        emit("gpsimd", lambda e, nreg=nreg: e.dma_start(
            out=chE[:, 0:nreg * 128].rearrange("p (c j) -> p c j", j=128),
            in_=chE_d[0:nreg, :].rearrange("c (q j) -> q c j", j=128)), inc=16)
        ev_ext = [emit("gpsimd", lambda e, nreg=nreg: e.dma_start(
            out=chS[:, 0:nreg * 128].rearrange("p (c j) -> p c j", j=128),
            in_=chS_d[0:nreg, :].rearrange("c (q j) -> q c j", j=128)), inc=16)]
        ev_ext_b[k] = ev_ext[-1]
        waits("gpsimd", [pe(ev_hcarry_b, k - 1)])
        ev_h0w = [emit("gpsimd", lambda e: e.tensor_copy(out=h0st[:, 0:128], in_=h_t[:]))]
        waits("vector", ev_ext + [pe(ev_wb_b, k - 1)])
        emit("vector", lambda e: e.tensor_tensor(
            out=ht_tmp[:], in0=h_t[:], in1=chS[:, 0:128], op=ALU.add))
        pend_pool_h0 = None
        for c, (lo, hi, clamps) in enumerate(regions):
            last = (c == nreg - 1)
            waits("vector", [pend_pool_h0])
            pend_pool_h0 = None
            if clamps:
                emit("vector", lambda e, c=c: e.scalar_tensor_tensor(
                    out=ht_new[:], in0=chE[:, c * 128:(c + 1) * 128], scalar=1.0,
                    in1=ht_tmp[:], op0=ALU.mult, op1=ALU.mult))
                ev_q = emit("vector", lambda e: e.scalar_tensor_tensor(
                    out=ht_tmp[:], in0=ht_new[:], scalar=1.0, in1=ht_new[:],
                    op0=ALU.mult, op1=ALU.mult, accum_out=qcol[:]))
                wait("tensor", ev_q)
                ev_qs = emit("tensor", lambda e: e.matmul(
                    out=p_s, lhsT=ones_sq[:], rhs=qcol[:], start=True, stop=True))
                wait("scalar", ev_qs)
                ev_rt = emit("scalar", lambda e: e.activation(
                    out=scol[:], in_=p_s, func=AF.Sqrt))
                wait("vector", ev_rt)
                emit("vector", lambda e: e.tensor_scalar(
                    out=scol[:], in0=scol[:], scalar1=1e-8, scalar2=None, op0=ALU.add))
                emit("vector", lambda e: e.reciprocal(out=scol[:], in_=scol[:]))
                emit("vector", lambda e: e.tensor_scalar(
                    out=scol[:], in0=scol[:], scalar1=20.0, scalar2=1.0, op0=ALU.mult, op1=ALU.min))
                if last:
                    waits("vector", [ev_h0w[0]])
                    ev_hcarry_b[k] = emit("vector", lambda e: e.tensor_scalar(
                        out=h_t[:], in0=ht_new[:], scalar1=scol[:], scalar2=None, op0=ALU.mult))
                else:
                    emit("vector", lambda e, c=c: e.scalar_tensor_tensor(
                        out=ht_tmp[:], in0=ht_new[:], scalar=scol[:],
                        in1=chS[:, (c + 1) * 128:(c + 2) * 128], op0=ALU.mult, op1=ALU.add))
                    ev_h0w.append(emit("vector", lambda e, c=c: e.tensor_scalar(
                        out=h0st[:, (c + 1) * 128:(c + 2) * 128], in0=ht_new[:],
                        scalar1=scol[:], scalar2=None, op0=ALU.mult)))
                emit("vector", lambda e: e.tensor_scalar(
                    out=sm1[:], in0=scol[:], scalar1=-1.0, scalar2=None, op0=ALU.add))
                ev_chain_b[k] = emit("vector", lambda e, hi=hi, bt=bt, k=k: e.scalar_tensor_tensor(
                    out=sfull[k % 2][0:bt, :], in0=id_f32[0:bt, hi:hi + 1], scalar=sm1[0:bt, :],
                    in1=sfull[k % 2][0:bt, :], op0=ALU.mult, op1=ALU.add))
            else:
                if last:
                    waits("vector", [ev_h0w[0]])
                    ev_chain_b[k] = ev_hcarry_b[k] = emit("vector", lambda e, c=c: e.tensor_tensor(
                        out=h_t[:], in0=chE[:, c * 128:(c + 1) * 128], in1=ht_tmp[:], op=ALU.mult))
                else:
                    ev_hn = emit("vector", lambda e, c=c: e.tensor_tensor(
                        out=ht_new[:], in0=chE[:, c * 128:(c + 1) * 128], in1=ht_tmp[:], op=ALU.mult))
                    ev_chain_b[k] = emit("vector", lambda e, c=c: e.tensor_tensor(
                        out=ht_tmp[:], in0=ht_new[:], in1=chS[:, (c + 1) * 128:(c + 2) * 128], op=ALU.add))
                    waits("gpsimd", [ev_chain_b[k]])
                    pend_pool_h0 = emit("gpsimd", lambda e, c=c: e.tensor_copy(
                        out=h0st[:, (c + 1) * 128:(c + 2) * 128], in_=ht_new[:]))
                    ev_h0w.append(pend_pool_h0)
        waits("gpsimd", ev_h0w + [pe(ev_hcarry_b, k)])
        emit("gpsimd", lambda e, nreg=nreg: e.dma_start(
            out=h0_d[0:nreg, :].rearrange("c (q j) -> q c j", j=128),
            in_=h0st[:, 0:nreg * 128].rearrange("p (c j) -> p c j", j=128)), inc=16)
        ev_wb_b[k] = emit("gpsimd", lambda e, nreg=nreg: e.dma_start(
            out=s_blk[114:114 + nreg, :], in_=h0_d[0:nreg, :]), inc=16)

    barrier(snapshot())
    NBL = len(blocks)
    if stage >= 4:
        emit_head(0)
        for k in range(NBL):
            for i in range(NSLC + 5):
                emit_2a_stage(k, i)
                if k >= 1 and stage >= 6:
                    emit_2b_stage(k - 1, i)
            if k >= 1 and stage >= 6:
                emit_finalize(k - 1)
            if k + 1 < NBL:
                emit_head(k + 1)
            if stage >= 5:
                emit_chain(k)
            if debug and k == 0:
                barrier(snapshot())
                emit("gpsimd", lambda e: e.dma_start(out=dbg["dbg_E0"], in_=e_blk[:]), inc=16)
                emit("gpsimd", lambda e: e.dma_start(out=dbg["dbg_S0"], in_=s_blk[:]), inc=16)
                emit("gpsimd", lambda e: e.dma_start(out=dbg["dbg_h0s"], in_=h0st[:]), inc=16)
                emit("gpsimd", lambda e: e.dma_start(out=dbg["dbg_xn0"], in_=xn_t[0:BT, :]), inc=16)
                barrier(snapshot())
        if stage >= 6:
            for i in range(NSLC + 2):
                emit_2b_stage(NBL - 1, i)
            emit_finalize(NBL - 1)
            if debug:
                barrier(snapshot())
                emit("gpsimd", lambda e: e.dma_start(out=dbg["dbg_y0"], in_=base2[(NBL - 1) % 2][0:BT, :]), inc=16)
                emit("gpsimd", lambda e, k=0: e.dma_start(out=dbg["dbg_sfull"], in_=sfull[0][:]), inc=16)
                barrier(snapshot())

    if debug:
        barrier(snapshot())
        emit("gpsimd", lambda e: e.dma_start(out=dbg["dbg_dt"], in_=dt_full[:]), inc=16)
        emit("gpsimd", lambda e: e.dma_start(out=dbg["dbg_bp"], in_=bp_full[:]), inc=16)
        emit("gpsimd", lambda e: e.dma_start(out=dbg["dbg_cp"], in_=cp_full[:]), inc=16)

    def thunk_final_d(eng):
        eng.wait_ge(sem_d, cnt["d"])
    prog["sync"].append(thunk_final_d)

    def thunk_final_g(eng):
        eng.wait_ge(sem_g, cnt["g"])
        eng.wait_ge(sem_p, cnt["p"])
    prog["gpsimd"].append(thunk_final_g)

    with es, nc.Block() as block:
        @block.sync
        def _(eng):
            for f in prog["sync"]:
                f(eng)

        @block.gpsimd
        def _(eng):
            for f in prog["gpsimd"]:
                f(eng)

        @block.tensor
        def _(eng):
            for f in prog["tensor"]:
                f(eng)

        @block.vector
        def _(eng):
            for f in prog["vector"]:
                f(eng)

        @block.scalar
        def _(eng):
            for f in prog["scalar"]:
                f(eng)

    return nc


def kernel(L0_attn_x, L0_attn_W_dt, L0_attn_b_dt, L0_attn_W_B, L0_attn_W_C,
           L0_attn_A_log, L0_attn_D_param, L0_ffn_weight, L0_ffn_bias):
    _, consts = build_consts()
    nc = build_nc(debug=False)
    base = {
        "W_dtT": np.ascontiguousarray(np.asarray(L0_attn_W_dt, dtype=np.float32).T),
        "b_dt": np.ascontiguousarray(L0_attn_b_dt, dtype=np.float32).reshape(1, D),
        "W_BT": np.ascontiguousarray(np.asarray(L0_attn_W_B, dtype=np.float32).T),
        "W_CT": np.ascontiguousarray(np.asarray(L0_attn_W_C, dtype=np.float32).T),
        "A_log": np.ascontiguousarray(L0_attn_A_log, dtype=np.float32),
        "D_param": np.ascontiguousarray(L0_attn_D_param, dtype=np.float32).reshape(1, D),
        "W_ffnT": np.ascontiguousarray(np.asarray(L0_ffn_weight, dtype=np.float32).T),
        "b_ffn": np.ascontiguousarray(L0_ffn_bias, dtype=np.float32).reshape(1, D),
    }
    base.update(consts)
    x = np.ascontiguousarray(L0_attn_x, dtype=np.float32)
    in_maps = []
    for i in range(8):
        m = dict(base)
        m["x"] = x[i]
        in_maps.append(m)
    res = run_bass_kernel_spmd(nc, in_maps, core_ids=list(range(8)))
    out = np.stack([np.asarray(r["out"]) for r in res.results], axis=0).astype(np.float32)
    return out


# version 14
# speedup vs baseline: 1.0622x; 1.0436x over previous
import contextlib
import numpy as np
import ml_dtypes
import concourse.bass as bass
import concourse.mybir as mybir
from concourse.bass_utils import run_bass_kernel_spmd

F32 = mybir.dt.float32
BF16 = mybir.dt.bfloat16
FP8 = mybir.dt.float8e4
AF = mybir.ActivationFunctionType
ALU = mybir.AluOpType
AX = mybir.AxisListType

BFD = ml_dtypes.bfloat16
F8D = ml_dtypes.float8_e4m3fn

L = 2048
D = 1024
N = 16
DN = D * N
BT = 114
NBLK = 18
NSLC = 32
SL = 512
RMAX = 13
NPAT = 6

DT_C = (0.1 + 0.001) / 2.0
DT_R = (0.1 - 0.001) / 2.0


def block_structure():
    blocks = []
    for k in range(NBLK):
        t0 = k * BT
        bt = min(BT, L - t0)
        regions = []
        lo = 0
        for r in range(1, bt + 1):
            t = t0 + r
            if r == bt or t % 10 == 1:
                hi = r - 1
                clamps = ((t0 + hi) % 10 == 0)
                regions.append((lo, hi, clamps))
                lo = r
        blocks.append((t0, bt, regions))
    return blocks


def pat_of(k):
    return 5 if k == NBLK - 1 else k % 5


def build_consts():
    blocks = block_structure()
    tri = np.zeros((NPAT, BT, 128), dtype=np.float32)
    mre = np.zeros((NPAT, 128, BT), dtype=np.float32)
    done = set()
    for k, (t0, bt, regions) in enumerate(blocks):
        p = pat_of(k)
        if p in done:
            continue
        done.add(p)
        for ci, (lo, hi, cl) in enumerate(regions):
            for tp in range(lo, hi + 1):
                tri[p, lo:tp + 1, tp] = 1.0
                tri[p, tp, 114 + ci] = 1.0
                mre[p, tp, tp] = 1.0
                mre[p, 114 + ci, tp] = 1.0
    ident = np.eye(128, dtype=np.float32)
    c = {
        "c_tri": tri.reshape(NPAT * BT, 128).astype(BFD),
        "c_mre": mre.reshape(NPAT * 128, BT).astype(BFD),
        "c_id_bf": ident.astype(BFD),
        "c_id_f32": ident.copy(),
        "c_id_f8": ident.astype(F8D),
        "c_ones_bf": np.ones((1, BT), dtype=np.float32).astype(BFD),
        "c_ones_row": np.ones((1, 128), dtype=np.float32),
        "c_ones_p": np.ones((128, 1), dtype=np.float32),
        "c_ones_sq": np.ones((128, 128), dtype=np.float32),
        "c_bias": np.tile(np.array([[1e-5, 1.0, -DT_C / DT_R]], dtype=np.float32), (128, 1)),
    }
    return blocks, c


def build_nc(debug=False, nblk=NBLK, stage=9):
    blocks, _ = build_consts()
    blocks = blocks[:nblk]
    nc = bass.Bass("TRN2", target_bir_lowering=False)

    x_h = nc.dram_tensor("x", [L, D], F32, kind="ExternalInput").ap()
    wdt_h = nc.dram_tensor("W_dtT", [D, D], F32, kind="ExternalInput").ap()
    bdt_h = nc.dram_tensor("b_dt", [1, D], F32, kind="ExternalInput").ap()
    wb_h = nc.dram_tensor("W_BT", [D, N], F32, kind="ExternalInput").ap()
    wc_h = nc.dram_tensor("W_CT", [D, N], F32, kind="ExternalInput").ap()
    alog_h = nc.dram_tensor("A_log", [D, N], F32, kind="ExternalInput").ap()
    dpar_h = nc.dram_tensor("D_param", [1, D], F32, kind="ExternalInput").ap()
    wf_h = nc.dram_tensor("W_ffnT", [D, D], F32, kind="ExternalInput").ap()
    bf_h = nc.dram_tensor("b_ffn", [1, D], F32, kind="ExternalInput").ap()
    tri_h = nc.dram_tensor("c_tri", [NPAT * BT, 128], BF16, kind="ExternalInput").ap()
    mre_h = nc.dram_tensor("c_mre", [NPAT * 128, BT], BF16, kind="ExternalInput").ap()
    idb_h = nc.dram_tensor("c_id_bf", [128, 128], BF16, kind="ExternalInput").ap()
    idf_h = nc.dram_tensor("c_id_f32", [128, 128], F32, kind="ExternalInput").ap()
    id8_h = nc.dram_tensor("c_id_f8", [128, 128], FP8, kind="ExternalInput").ap()
    onesb_h = nc.dram_tensor("c_ones_bf", [1, BT], BF16, kind="ExternalInput").ap()
    onesr_h = nc.dram_tensor("c_ones_row", [1, 128], F32, kind="ExternalInput").ap()
    onesp_h = nc.dram_tensor("c_ones_p", [128, 1], F32, kind="ExternalInput").ap()
    bias_h = nc.dram_tensor("c_bias", [128, 3], F32, kind="ExternalInput").ap()
    onessq_h = nc.dram_tensor("c_ones_sq", [128, 128], F32, kind="ExternalInput").ap()
    a16_d = nc.dram_tensor("a16_scratch", [D, N], BF16, kind="Internal").ap()
    a16_d2 = nc.dram_tensor("a16_scratch2", [N, D], BF16, kind="Internal").ap()
    chE_d = nc.dram_tensor("chE_scratch", [RMAX, DN], BF16, kind="Internal").ap()
    chS_d = nc.dram_tensor("chS_scratch", [RMAX, DN], BF16, kind="Internal").ap()
    h0_d = nc.dram_tensor("h0_scratch", [RMAX, DN], BF16, kind="Internal").ap()
    xq_d = nc.dram_tensor("xq_scratch", [128, NBLK * 8 * BT], FP8, kind="Internal").ap()

    out_h = nc.dram_tensor("out", [L, D], F32, kind="ExternalOutput").ap()
    dbg = {}
    if debug:
        for nm, shp, dt_ in [("dbg_dt", [128, NBLK * D], F32), ("dbg_E0", [128, DN], F32),
                             ("dbg_S0", [128, DN], F32), ("dbg_y0", [BT, D], F32),
                             ("dbg_h0s", [128, RMAX * 128], F32),
                             ("dbg_sfull", [BT, 1], F32), ("dbg_bp", [128, NBLK * N], F32),
                             ("dbg_cp", [128, NBLK * N], F32), ("dbg_xn0", [BT, D], F32),
                             ("dbg_a16", [128, DN], F32)]:
            dbg[nm] = nc.dram_tensor(nm, shp, dt_, kind="ExternalOutput").ap()

    es = contextlib.ExitStack()

    def sb(name, shape, dtype):
        return es.enter_context(nc.sbuf_tensor(name, shape, dtype))

    def ps(name, shape, dtype):
        return es.enter_context(nc.psum_tensor(name, shape, dtype))

    dt_full = sb("dt_full", [128, NBLK * D], BF16)
    xqs = [sb(f"xqs{i}", [128, 8 * BT], FP8) for i in range(2)]
    a16b = sb("a16b", [128, DN], BF16)
    s_blk = sb("s_blk", [128, DN], BF16)
    wqT = sb("wqT", [128, 8 * D], FP8)
    tri_c = sb("tri_c", [BT, NPAT * 128], BF16)
    mre_c = sb("mre_c", [128, NPAT * BT], BF16)
    bp_full = sb("bp_full", [128, NBLK * N], BF16)
    cp_full = sb("cp_full", [128, NBLK * N], F32)
    dbc = sb("dbc", [128, D], BF16)
    fbb = sb("fbb", [128, D], BF16)
    id_bf = sb("id_bf", [128, 128], BF16)
    id_f32 = sb("id_f32", [128, 128], F32)
    id_f8 = sb("id_f8", [128, 128], FP8)
    ones_bf = sb("ones_bf", [1, BT], BF16)
    ones_sq = sb("ones_sq", [128, 128], F32)
    bias_c = sb("bias_c", [128, 3], F32)
    chE = sb("chE", [128, RMAX * 128], BF16)
    chS = sb("chS", [128, RMAX * 128], BF16)
    h0st = sb("h0st", [128, RMAX * 128], BF16)
    h_t = sb("h_t", [128, 128], F32)
    ht_tmp = sb("ht_tmp", [128, 128], F32)
    ht_new = sb("ht_new", [128, 128], F32)
    sfull = [sb(f"sfull{i}", [BT, 1], F32) for i in range(2)]
    qcol = sb("qcol", [128, 1], F32)
    scol = sb("scol", [128, 1], F32)
    sm1 = sb("sm1", [128, 1], F32)
    x_t = sb("x_t", [128, D], F32)
    xn_t = sb("xn_t", [128, D], F32)
    w_t = sb("w_t", [128, D], BF16)
    mean_c = sb("mean_c", [128, 1], F32)
    sq_c = sb("sq_c", [128, 1], F32)
    var_c = sb("var_c", [128, 1], F32)
    rstd_c = sb("rstd_c", [128, 1], F32)
    garg = [sb(f"garg{i}", [BT, SL], BF16) for i in range(4)]
    g_t = [sb(f"g_t{i}", [BT, SL], BF16) for i in range(2)]
    en2_t = sb("en2_t", [BT, 2 * SL], BF16)
    r2_t = [sb(f"r2_t{i}", [BT, 2 * SL], BF16) for i in range(1)]
    ces_t = sb("ces_t", [BT, 4 * SL], BF16)
    yt_bf = sb("yt_bf", [BT, D], BF16)
    yt_blk = sb("yt_blk", [BT, D], F32)
    base_b = sb("base_b", [BT, D], F32)

    off0 = (nc.sbuf_base + 31) & ~31
    arena = sb("arena", [128, 16384], BF16)

    def at(name, shape, dtype, byte_off):
        return nc.alloc_sbuf_tensor_at(name, shape, dtype, offset=off0 + byte_off)

    e_blk = arena
    wdtT = at("wdtT", [128, 8 * D], BF16, 0)
    wslab = at("wslab", [128, D], F32, 16384)
    wslab2 = at("wslab2", [128, D], BF16, 20480)
    wslab8 = at("wslab8", [128, D], FP8, 22528)
    xnT_t = at("xnT_t", [128, 8 * BT], BF16, 23552)
    ep_e = at("ep_e", [128, 2 * SL], F32, 25376)
    wbT = at("wbT", [128, 8 * N], BF16, 29472)
    wcT = at("wcT", [128, 8 * N], BF16, 29728)
    bdt_row = at("bdt_row", [1, D], BF16, 29984)
    atile = at("atile", [128, 128], F32, 0)
    a16t = at("a16t", [128, 128], BF16, 512)

    p_ca = [ps(f"p_ca{i}", [128, 2 * SL], F32) for i in range(2)]
    p_sp = [ps(f"p_sp{i}", [128, SL], F32) for i in range(2)]
    p_ffn = [ps(f"p_ffn{i}", [128, SL], F32) for i in range(2)]
    p_trb = [p_ca[i][:].bitcast(BF16)[:, 0:128] for i in range(2)]
    p_tr8 = [p_ca[i][:].bitcast(FP8)[:, 512:768].rearrange("p (j t) -> p j t", t=2)[:, :, 0:1].squeeze(2) for i in range(2)]
    p_s = p_sp[0][:, 300:301]

    sem_d = es.enter_context(nc.semaphore("sem_d"))
    sem_g = es.enter_context(nc.semaphore("sem_g"))
    sem_p = es.enter_context(nc.semaphore("sem_p"))
    sem_t = es.enter_context(nc.semaphore("sem_t"))
    sem_v = es.enter_context(nc.semaphore("sem_v"))
    sem_s = es.enter_context(nc.semaphore("sem_s"))

    cnt = {"d": 0, "g": 0, "p": 0, "t": 0, "v": 0, "s": 0}
    sems = {"d": sem_d, "g": sem_g, "p": sem_p, "t": sem_t, "v": sem_v, "s": sem_s}
    prog = {"sync": [], "gpsimd": [], "tensor": [], "vector": [], "scalar": []}
    ENG_SEM = {"sync": "d", "gpsimd": "g", "tensor": "t", "vector": "v", "scalar": "s"}
    ENG_KEYS = {"sync": ("d",), "gpsimd": ("g", "p"), "tensor": ("t",),
                "vector": ("v",), "scalar": ("s",)}

    def emit(engine, fn, inc=1):
        if engine == "gpsimd":
            key = "g" if inc == 16 else "p"
        else:
            key = ENG_SEM[engine]
        befores = [(kk, cnt[kk]) for kk in ENG_KEYS[engine]]
        cnt[key] += inc
        sem = sems[key]
        if engine != "tensor":
            for kk, before in befores:
                if before > 0:
                    def wthunk(eng, sem=sems[kk], before=before):
                        eng.wait_ge(sem, before)
                    prog[engine].append(wthunk)

        def thunk(eng, fn=fn, sem=sem, inc=inc):
            fn(eng).then_inc(sem, inc)
        prog[engine].append(thunk)
        return (key, cnt[key])

    def wait(engine, ev):
        if ev is None:
            return
        key, val = ev
        if key in ENG_KEYS[engine]:
            return

        def thunk(eng, key=key, val=val):
            eng.wait_ge(sems[key], val)
        prog[engine].append(thunk)

    def waits(engine, evs):
        best = {}
        for ev in evs:
            if ev is None:
                continue
            kk, vv = ev
            if kk not in ENG_KEYS[engine] and best.get(kk, -1) < vv:
                best[kk] = vv
        for kk, vv in best.items():
            wait(engine, (kk, vv))

    def snapshot():
        return dict(cnt)

    def barrier(snap):
        evs = [(kk, vv) for kk, vv in snap.items()]
        for eng in ("tensor", "vector", "scalar", "gpsimd", "sync"):
            waits(eng, evs)

    ev_pre = []
    for dst, src in [(id_bf[:], idb_h), (id_f32[:], idf_h), (id_f8[:], id8_h),
                     (ones_bf[:], onesb_h),
                     (bias_c[:], bias_h), (ones_sq[:], onessq_h),
                     (atile[:], alog_h.rearrange("(q p) n -> p q n", p=128))]:
        ev_pre.append(emit("sync", lambda e, d=dst, s=src: e.dma_start(out=d, in_=s), inc=16))
    for p in range(NPAT):
        ev_pre.append(emit("sync", lambda e, p=p: e.dma_start(
            out=tri_c[:, p * 128:(p + 1) * 128], in_=tri_h[p * BT:(p + 1) * BT, :]), inc=16))
        ev_pre.append(emit("sync", lambda e, p=p: e.dma_start(
            out=mre_c[:, p * BT:(p + 1) * BT], in_=mre_h[p * 128:(p + 1) * 128, :]), inc=16))

    nc.const_aps.register(F32, 1e-5, bias_c[:, 0:1])
    nc.const_aps.register(F32, 1.0, bias_c[:, 1:2])
    nc.const_aps.register(F32, -DT_C / DT_R, bias_c[:, 2:3])
    waits("scalar", ev_pre)
    emit("scalar", lambda e: e.activation(out=atile[:], in_=atile[:], func=AF.Tanh, scale=0.1))
    ev = emit("scalar", lambda e: e.activation(out=atile[:], in_=atile[:], func=AF.Exp, scale=10.0))
    wait("vector", ev)
    ev_a16 = emit("vector", lambda e: e.tensor_scalar(
        out=a16t[:], in0=atile[:], scalar1=-1.0 / 16.0, scalar2=None, op0=ALU.mult))
    wait("gpsimd", ev_a16)
    ev_a16d = emit("gpsimd", lambda e: e.dma_start(out=a16_d.rearrange("(q p) n -> p q n", p=128), in_=a16t[:].rearrange("p (q n) -> p q n", n=N)), inc=16)
    wait("gpsimd", ev_a16d)
    for ni in range(N):
        ev_a16d = emit("gpsimd", lambda e, ni=ni: e.dma_start(
            out=a16_d2[ni:ni + 1, :], in_=a16_d[:, ni:ni + 1].rearrange("d n -> n d"),
            allow_slow_non_contiguous=True), inc=16)
    wait("gpsimd", ev_a16d)
    ev_a16b = emit("gpsimd", lambda e: e.dma_start(
        out=a16b[:],
        in_=a16_d2.rearrange("n d -> (n d)").unsqueeze(0).partition_broadcast(128).squeeze(1)), inc=16)
    waits("gpsimd", ev_pre)
    ev_dbc = emit("gpsimd", lambda e: e.dma_start(
        out=dbc[:], in_=dpar_h.partition_broadcast(128).squeeze(1)), inc=16)
    ev_fbb = emit("gpsimd", lambda e: e.dma_start(
        out=fbb[:], in_=bf_h.partition_broadcast(128).squeeze(1)), inc=16)

    ev_bd = emit("sync", lambda e: e.dma_start(out=x_t[0:1, :], in_=bdt_h), inc=16)
    wait("vector", ev_bd)
    ev_bdt = emit("vector", lambda e: e.tensor_copy(out=bdt_row[:], in_=x_t[0:1, :]))

    if stage >= 2:
        waits("sync", [ev_a16d, ev_dbc, ev_fbb, ev_a16b])
        waits("tensor", ev_pre)
        ev_cp_prev = ev_bdt
        for i in range(8):
            waits("sync", [ev_cp_prev])
            ev1 = emit("sync", lambda e, i=i: e.dma_start(out=wslab[:], in_=wdt_h[i * 128:(i + 1) * 128, :]), inc=16)
            wait("vector", ev1)
            ev_cp_prev = emit("vector", lambda e, i=i: e.tensor_copy(
                out=wdtT[:, i * D:(i + 1) * D], in_=wslab[:]))
        for i in range(8):
            waits("sync", [ev_cp_prev])
            ev1 = emit("sync", lambda e, i=i: e.dma_start(out=wslab[:], in_=wf_h[i * 128:(i + 1) * 128, :]), inc=16)
            wait("scalar", ev1)
            ev_cp_prev = emit("scalar", lambda e, i=i: e.copy(
                out=wqT[:, i * D:(i + 1) * D], in_=wslab[:]))
        waits("sync", [ev_cp_prev])
        ev1 = emit("sync", lambda e: e.dma_start(
            out=wslab[:, 0:8 * N].rearrange("p (i n) -> p i n", n=N),
            in_=wb_h.rearrange("(i p) n -> p i n", p=128)), inc=16)
        ev1b = emit("sync", lambda e: e.dma_start(
            out=wslab[:, 8 * N:16 * N].rearrange("p (i n) -> p i n", n=N),
            in_=wc_h.rearrange("(i p) n -> p i n", p=128)), inc=16)
        waits("vector", [ev1, ev1b])
        emit("vector", lambda e: e.tensor_copy(out=wbT[:], in_=wslab[:, 0:8 * N]))
        ev_cp_prev = emit("vector", lambda e: e.tensor_copy(out=wcT[:], in_=wslab[:, 8 * N:16 * N]))

    emit("vector", lambda e: e.memset(h_t[:], 0.0))
    emit("vector", lambda e: e.memset(h0st[:], 0.0))
    emit("vector", lambda e: e.memset(s_blk[:], 0.0))
    emit("vector", lambda e: e.memset(chE[:], 0.0))
    emit("vector", lambda e: e.memset(chS[:], 0.0))
    if debug:
        emit("vector", lambda e: e.memset(dt_full[:], 0.0))
        emit("vector", lambda e: e.memset(bp_full[:], 0.0))
        emit("vector", lambda e: e.memset(cp_full[:], 0.0))
        emit("vector", lambda e: e.memset(sfull[0][:], 0.0))
        emit("vector", lambda e: e.memset(sfull[1][:], 0.0))
        emit("vector", lambda e: e.memset(yt_blk[:], 0.0))
        emit("vector", lambda e: e.memset(xn_t[:], 0.0))
        barrier(snapshot())
    if debug:
        waits("gpsimd", [ev_a16b])
        emit("gpsimd", lambda e: e.dma_start(out=dbg["dbg_a16"], in_=a16b[:]), inc=16)

    a_sq = [None] * NBLK
    a_v2 = [None] * NBLK
    a_rstd = [None] * NBLK
    a_xn = [None] * NBLK
    a_trlast = [None] * NBLK
    a_xnTl = [None] * NBLK
    a_cp = [None] * NBLK
    a_e1 = [[None, None] for _ in range(NBLK)]
    a_xqd = [None] * NBLK
    a_xqin = [None] * NBLK
    for k, (t0, bt, regions) in enumerate(blocks if stage >= 3 else []):
        def pa(lst, i):
            return lst[i] if i >= 0 else None
        waits("sync", [pa(a_sq, k - 1), pa(a_xn, k - 1)])
        evx = emit("sync", lambda e, t0=t0, bt=bt: e.dma_start(
            out=x_t[0:bt, :], in_=x_h[t0:t0 + bt, :]), inc=16)
        wait("vector", evx)
        emit("vector", lambda e, bt=bt: e.tensor_reduce(
            out=mean_c[0:bt, :], in_=x_t[0:bt, :], axis=AX.X, op=ALU.add))
        waits("scalar", [evx, pa(a_v2, k - 1)])
        a_sq[k] = emit("scalar", lambda e, bt=bt: e.activation(
            out=yt_blk[0:bt, :], in_=x_t[0:bt, :], func=AF.Square, accum_out=sq_c[0:bt, :]))
        waits("vector", [a_sq[k], pa(a_rstd, k - 1)])
        emit("vector", lambda e, bt=bt: e.tensor_scalar(
            out=mean_c[0:bt, :], in0=mean_c[0:bt, :], scalar1=1.0 / D, scalar2=None, op0=ALU.mult))
        emit("vector", lambda e, bt=bt: e.tensor_tensor(
            out=var_c[0:bt, :], in0=mean_c[0:bt, :], in1=mean_c[0:bt, :], op=ALU.mult))
        a_v2[k] = emit("vector", lambda e, bt=bt: e.scalar_tensor_tensor(
            out=var_c[0:bt, :], in0=sq_c[0:bt, :], scalar=1.0 / D, in1=var_c[0:bt, :],
            op0=ALU.mult, op1=ALU.subtract))
        waits("scalar", [a_v2[k], pa(a_xn, k - 1)])
        emit("scalar", lambda e, bt=bt: e.activation(
            out=rstd_c[0:bt, :], in_=var_c[0:bt, :], func=AF.Ln, bias=1e-5))
        a_rstd[k] = emit("scalar", lambda e, bt=bt: e.activation(
            out=rstd_c[0:bt, :], in_=rstd_c[0:bt, :], func=AF.Exp, scale=-0.5))
        waits("vector", [a_rstd[k], pa(a_trlast, k - 1)])
        a_xn[k] = emit("vector", lambda e, bt=bt: e.tensor_scalar(
            out=xn_t[0:bt, :], in0=x_t[0:bt, :], scalar1=mean_c[0:bt, :], scalar2=rstd_c[0:bt, :],
            op0=ALU.subtract, op1=ALU.mult))
        ev_tr = [None] * 8
        ev_evv = [None] * 8
        ev_evs = [None] * 8
        for j in range(9):
            if j < 8:
                waits("tensor", [a_xn[k], pa(a_cp, k - 1),
                                 ev_evs[j - 2] if j >= 2 else None])
                ev_tr[j] = emit("tensor", lambda e, j=j, bt=bt: e.transpose(
                    out=p_ca[j % 2][:, 0:bt], in_=xn_t[0:bt, j * 128:(j + 1) * 128],
                    identity=id_f32[0:bt, 0:bt]))
            if j >= 1:
                jj = j - 1
                wait("vector", ev_tr[jj])
                ev_evv[jj] = emit("vector", lambda e, jj=jj, bt=bt: e.tensor_copy(
                    out=xnT_t[:, jj * BT: jj * BT + bt], in_=p_ca[jj % 2][:, 0:bt]))
                waits("scalar", [ev_tr[jj], ev_evv[jj], pa(a_xqd, k - 2) if jj == 0 else None])
                ev_evs[jj] = emit("scalar", lambda e, jj=jj, bt=bt, k=k: e.copy(
                    out=xqs[k % 2][:, jj * BT: jj * BT + bt], in_=p_ca[jj % 2][:, 0:bt]))
        a_trlast[k] = ev_tr[7]
        a_xnTl[k] = ev_evv[7]
        wait("sync", ev_evs[7])
        a_xqd[k] = emit("sync", lambda e, k=k, bt=bt: e.dma_start(
            out=xq_d[:, k * 8 * BT:(k + 1) * 8 * BT].rearrange("p (j t) -> p j t", t=BT)[:, :, 0:bt],
            in_=xqs[k % 2][:].rearrange("p (j t) -> p j t", t=BT)[:, :, 0:bt]), inc=16)
        ev_mmd = [None, None]
        for h in range(2):
            waits("tensor", [a_xnTl[k], a_e1[k - 1][h] if k >= 1 else None])
            for j in range(8):
                emit("tensor", lambda e, j=j, h=h, bt=bt: e.matmul(
                    out=p_ffn[h][0:bt, :], lhsT=xnT_t[:, j * BT: j * BT + bt],
                    rhs=wdtT[:, j * D + h * SL: j * D + h * SL + SL],
                    start=(j == 0), stop=False))
            ev_mmd[h] = emit("tensor", lambda e, h=h, bt=bt: e.matmul(
                out=p_ffn[h][0:bt, :], lhsT=ones_bf[:, 0:bt],
                rhs=bdt_row[:, h * SL: h * SL + SL], start=False, stop=True))
        for h in range(2):
            waits("scalar", [ev_mmd[h]])
            a_e1[k][h] = emit("scalar", lambda e, h=h, bt=bt: e.activation(
                out=ep_e[0:bt, h * SL:(h + 1) * SL], in_=p_ffn[h][0:bt, :], func=AF.Exp))
        for h in range(2):
            emit("scalar", lambda e, h=h, bt=bt: e.activation(
                out=ep_e[0:bt, h * SL:(h + 1) * SL], in_=ep_e[0:bt, h * SL:(h + 1) * SL],
                func=AF.Ln, bias=1.0))
        ev_sg = [None, None]
        for h in range(2):
            ev_sg[h] = emit("scalar", lambda e, h=h, bt=bt: e.activation(
                out=ep_e[0:bt, h * SL:(h + 1) * SL], in_=ep_e[0:bt, h * SL:(h + 1) * SL],
                func=AF.Sigmoid, scale=1.0 / DT_R, bias=-DT_C / DT_R))
        for h in range(2):
            wait("vector", ev_sg[h])
            emit("vector", lambda e, h=h, k=k, bt=bt: e.tensor_scalar(
                out=dt_full[0:bt, k * D + h * SL: k * D + h * SL + SL],
                in0=ep_e[0:bt, h * SL:(h + 1) * SL],
                scalar1=4.0 * DT_R, scalar2=DT_C - 2.0 * DT_R, op0=ALU.mult, op1=ALU.add))
        waits("tensor", [a_xnTl[k], ev_evs[6], ev_evs[7]])
        for j in range(8):
            emit("tensor", lambda e, j=j, bt=bt: e.matmul(
                out=p_ca[0][0:bt, 0:N], lhsT=xnT_t[:, j * BT: j * BT + bt],
                rhs=wbT[:, j * N:(j + 1) * N], start=(j == 0), stop=(j == 7)))
        for j in range(8):
            ev_mmc = emit("tensor", lambda e, j=j, bt=bt: e.matmul(
                out=p_ca[0][0:bt, N:2 * N], lhsT=xnT_t[:, j * BT: j * BT + bt],
                rhs=wcT[:, j * N:(j + 1) * N], start=(j == 0), stop=(j == 7)))
        wait("vector", ev_mmc)
        emit("vector", lambda e, k=k, bt=bt: e.tensor_copy(
            out=bp_full[0:bt, k * N:(k + 1) * N], in_=p_ca[0][0:bt, 0:N]))
        a_cp[k] = emit("vector", lambda e, k=k, bt=bt: e.tensor_copy(
            out=cp_full[0:bt, k * N:(k + 1) * N], in_=p_ca[0][0:bt, N:2 * N]))

    NG = NBLK * NSLC
    ev_g = [None] * NG
    ev_garg = [None] * NG
    ev_ca = [None] * NG
    ev_E = [None] * NG
    ev_en = [None] * NG
    ev_r = [None] * NG
    ev_smm = [None] * NG
    ev_evac = [None] * NG
    ev_re = [None] * NG
    ev_stt = [None] * NG
    ev_yadd = [None] * NG
    ev_sq_b = [None] * NBLK
    ev_v2_b = [None] * NBLK
    ev_sr_b = [None] * NBLK
    ev_xn_b = [None] * NBLK
    ev_w_b = [None] * NBLK
    ev_ext_b = [None] * NBLK
    ev_chain_b = [None] * NBLK
    ev_wb_b = [None] * NBLK
    ev_basex_b = [None] * NBLK
    ev_baseff_b = [None] * NBLK
    ev_store_b = [None] * NBLK
    ev_hcarry_b = [None] * NBLK
    ev_ffn_b = [None] * NBLK

    def pe(lst, i):
        return lst[i] if i >= 0 else None

    base2 = [yt_blk, base_b]

    def emit_head(k):
        t0, bt, regions = blocks[k]
        waits("sync", [pe(ev_sq_b, k - 1), pe(ev_basex_b, k - 1)])
        evx = emit("sync", lambda e, t0=t0, bt=bt: e.dma_start(
            out=x_t[0:bt, :], in_=x_h[t0:t0 + bt, :]), inc=16)
        wait("vector", evx)
        emit("vector", lambda e, bt=bt: e.tensor_reduce(
            out=mean_c[0:bt, :], in_=x_t[0:bt, :], axis=AX.X, op=ALU.add))
        waits("scalar", [evx, pe(ev_v2_b, k - 1), pe(ev_r, k * NSLC - 1)])
        ev_sq_b[k] = emit("scalar", lambda e, bt=bt: e.activation(
            out=w_t[0:bt, :], in_=x_t[0:bt, :], func=AF.Square, accum_out=sq_c[0:bt, :]))
        waits("vector", [ev_sq_b[k], pe(ev_sr_b, k - 1)])
        emit("vector", lambda e, bt=bt: e.tensor_scalar(
            out=mean_c[0:bt, :], in0=mean_c[0:bt, :], scalar1=1.0 / D, scalar2=None, op0=ALU.mult))
        emit("vector", lambda e, bt=bt: e.tensor_tensor(
            out=var_c[0:bt, :], in0=mean_c[0:bt, :], in1=mean_c[0:bt, :], op=ALU.mult))
        ev_v2_b[k] = emit("vector", lambda e, bt=bt: e.scalar_tensor_tensor(
            out=var_c[0:bt, :], in0=sq_c[0:bt, :], scalar=1.0 / D, in1=var_c[0:bt, :],
            op0=ALU.mult, op1=ALU.subtract))
        wait("scalar", ev_v2_b[k])
        ev_sr_b[k] = emit("scalar", lambda e, bt=bt: e.activation(
            out=rstd_c[0:bt, :], in_=var_c[0:bt, :], func=AF.Sqrt, bias=1e-5))
        wait("vector", ev_sr_b[k])
        emit("vector", lambda e, bt=bt: e.reciprocal(out=rstd_c[0:bt, :], in_=rstd_c[0:bt, :]))
        ev_xn_b[k] = emit("vector", lambda e, bt=bt: e.tensor_scalar(
            out=xn_t[0:bt, :], in0=x_t[0:bt, :], scalar1=mean_c[0:bt, :], scalar2=rstd_c[0:bt, :],
            op0=ALU.subtract, op1=ALU.mult))
        ev_w_b[k] = emit("vector", lambda e, k=k, bt=bt: e.tensor_tensor(
            out=w_t[0:bt, :], in0=xn_t[0:bt, :], in1=dt_full[0:bt, k * D:(k + 1) * D], op=ALU.mult))
        emit("vector", lambda e, k=k, bt=bt: e.memset(sfull[k % 2][0:bt, :], 1.0))
        waits("sync", [pe(ev_ffn_b, k - 2)])
        a_xqin[k] = emit("sync", lambda e, k=k, bt=bt: e.dma_start(
            out=xqs[k % 2][:].rearrange("p (j t) -> p j t", t=BT)[:, :, 0:bt],
            in_=xq_d[:, k * 8 * BT:(k + 1) * 8 * BT].rearrange("p (j t) -> p j t", t=BT)[:, :, 0:bt]), inc=16)
        ev_f = None
        waits("tensor", [pe(ev_stt, (k - 2) * NSLC + NSLC - 1) if k >= 2 else None, a_xqin[k]])
        for h in range(2):
            for j in range(8):
                ev_f = emit("tensor", lambda e, j=j, h=h, bt=bt, k=k: e.matmul(
                    out=p_ffn[h][0:bt, :], lhsT=xqs[k % 2][:, j * BT: j * BT + bt],
                    rhs=wqT[:, j * D + h * SL: j * D + h * SL + SL],
                    start=(j == 0), stop=(j == 7)))
        ev_ffn_b[k] = ev_f
        bb = base2[k % 2]
        waits("vector", [pe(ev_store_b, k - 2)])
        emit("vector", lambda e, bt=bt, bb=bb: e.tensor_tensor(
            out=bb[0:bt, :], in0=xn_t[0:bt, :], in1=dbc[0:bt, :], op=ALU.mult))
        ev_basex_b[k] = emit("vector", lambda e, bt=bt, bb=bb: e.tensor_tensor(
            out=bb[0:bt, :], in0=bb[0:bt, :], in1=x_t[0:bt, :], op=ALU.add))
        waits("vector", [ev_ffn_b[k], ev_fbb])
        for h in range(2):
            emit("vector", lambda e, h=h, bt=bt, bb=bb: e.tensor_tensor(
                out=bb[0:bt, h * SL:(h + 1) * SL], in0=bb[0:bt, h * SL:(h + 1) * SL],
                in1=p_ffn[h][0:bt, :], op=ALU.add))
        ev_baseff_b[k] = emit("vector", lambda e, bt=bt, bb=bb: e.tensor_tensor(
            out=bb[0:bt, :], in0=bb[0:bt, :], in1=fbb[0:bt, :], op=ALU.add))

    def emit_2a_stage(k, i):
        t0, bt, regions = blocks[k]
        pat = pat_of(k)
        g0 = k * NSLC
        if i < NSLC:
            s = i
            gs = g0 + s
            b = s % 4
            dh = (s & 1) * SL
            geng = "vector"
            waits(geng, [pe(ev_g, gs - 4)])
            ev_garg[gs] = emit(geng, lambda e, k=k, bt=bt, dh=dh, s=s, b=b: e.tensor_tensor(
                out=garg[b][0:bt, :], in0=dt_full[0:bt, k * D + dh: k * D + dh + SL],
                in1=a16b[0:bt, s * SL:(s + 1) * SL], op=ALU.mult))
        if 0 <= i - 1 < NSLC:
            s = i - 1
            gs = g0 + s
            waits("scalar", [ev_garg[gs], pe(ev_ca, gs - 2), ev_w_b[k]])
            ev_g[gs] = emit("scalar", lambda e, bt=bt, gb=s % 4, b=s % 2: e.activation(
                out=g_t[b][0:bt, :], in_=garg[gb][0:bt, :], func=AF.Tanh))
            waits("tensor", [ev_g[gs], pe(ev_en, gs - 4)])
            ev_ca[gs] = emit("tensor", lambda e, pat=pat, bt=bt, s=s: e.matmul(
                out=p_ca[(s >> 1) % 2][:, (s & 1) * SL:((s & 1) + 1) * SL],
                lhsT=tri_c[0:bt, pat * 128:(pat + 1) * 128], rhs=g_t[s % 2][0:bt, :],
                start=True, stop=True))
        if 0 <= i - 2 < NSLC and (i - 2) % 2 == 1:
            s1 = i - 2
            s0 = s1 - 1
            gs1 = g0 + s1
            gs0 = g0 + s0
            pb = (s0 >> 1) % 2
            waits("scalar", [ev_ca[gs1], pe(ev_stt, gs0 - NSLC), pe(ev_stt, gs1 - NSLC), pe(ev_ext_b, k - 1)])
            ev_E[gs0] = ev_E[gs1] = emit("scalar", lambda e, s0=s0, pb=pb: e.activation(
                out=e_blk[:, s0 * SL:(s0 + 2) * SL], in_=p_ca[pb][:], func=AF.Exp, scale=16.0))
            waits("scalar", [pe(ev_r, gs0 - 2)])
            waits("scalar", [pe(ev_r, gs0 - 2)])
            ev_en[gs0] = ev_en[gs1] = emit("scalar", lambda e, bt=bt, pb=pb: e.activation(
                out=en2_t[0:bt, :], in_=p_ca[pb][0:bt, :], func=AF.Exp, scale=-16.0))
        if 0 <= i - 3 < NSLC and (i - 3) % 2 == 1:
            s1 = i - 3
            s0 = s1 - 1
            gs1 = g0 + s1
            gs0 = g0 + s0
            n_i = s0 >> 1
            waits("vector", [ev_en[gs1], pe(ev_smm, gs0 - 2), pe(ev_smm, gs1 - 2)])
            ev_r[gs0] = ev_r[gs1] = emit("vector", lambda e, k=k, bt=bt, n_i=n_i: e.scalar_tensor_tensor(
                out=r2_t[0][0:bt, :], in0=w_t[0:bt, :],
                scalar=bp_full[0:bt, k * N + n_i: k * N + n_i + 1],
                in1=en2_t[0:bt, :], op0=ALU.mult, op1=ALU.mult))
            for s in (s0, s1):
                gs = g0 + s
                waits("tensor", [ev_r[gs], pe(ev_evac, gs - 2)])
                ev_smm[gs] = emit("tensor", lambda e, pat=pat, bt=bt, b=s % 2, s=s: e.matmul(
                    out=p_sp[b][:], lhsT=tri_c[0:bt, pat * 128:(pat + 1) * 128],
                    rhs=r2_t[0][0:bt, (s & 1) * SL:((s & 1) + 1) * SL],
                    start=True, stop=True))
        if 0 <= i - 4 < NSLC:
            s = i - 4
            gs = g0 + s
            b = s % 2
            eng = "scalar" if (s % 2 == 0) else "vector"
            waits(eng, [ev_smm[gs], pe(ev_re, gs - NSLC), pe(ev_ext_b, k - 1)])
            if eng == "scalar":
                ev_evac[gs] = emit("scalar", lambda e, s=s, b=b: e.copy(
                    out=s_blk[:, s * SL:(s + 1) * SL], in_=p_sp[b][:]))
            else:
                ev_evac[gs] = emit("vector", lambda e, s=s, b=b: e.tensor_copy(
                    out=s_blk[:, s * SL:(s + 1) * SL], in_=p_sp[b][:]))

    def emit_2b_stage(k, i):
        t0, bt, regions = blocks[k]
        g0 = k * NSLC
        if i < NSLC:
            s = i
            gs = g0 + s
            waits("tensor", [ev_wb_b[k], ev_evac[gs], pe(ev_stt, gs - 2),
                             ev_baseff_b[k + 1] if k + 1 < NBLK and ev_baseff_b[k + 1] is not None else None])
            ev_re[gs] = emit("tensor", lambda e, k=k, bt=bt, s=s: e.matmul(
                out=p_ffn[s % 2][0:bt, :], lhsT=mre_c[:, pat_of(k) * BT: pat_of(k) * BT + bt],
                rhs=s_blk[:, s * SL:(s + 1) * SL], start=True, stop=True))
        if 0 <= i - 1 < NSLC:
            s = i - 1
            gs = g0 + s
            b = s % 2
            n_i = s >> 1
            dh = (s & 1) * SL
            waits("vector", [ev_re[gs]])
            if s < 2:
                ev_stt[gs] = emit("vector", lambda e, k=k, bt=bt, s=s, n_i=n_i, dh=dh, b=b: e.scalar_tensor_tensor(
                    out=yt_bf[0:bt, dh:dh + SL], in0=e_blk[0:bt, s * SL:(s + 1) * SL],
                    scalar=cp_full[0:bt, k * N + n_i: k * N + n_i + 1], in1=p_ffn[b][0:bt, :],
                    op0=ALU.mult, op1=ALU.mult))
                ev_yadd[gs] = ev_stt[gs]
            else:
                waits("vector", [pe(ev_yadd, gs - 4)])
                ev_stt[gs] = emit("vector", lambda e, k=k, bt=bt, s=s, n_i=n_i, cb=s % 4, b=b: e.scalar_tensor_tensor(
                    out=ces_t[0:bt, cb * SL:(cb + 1) * SL], in0=e_blk[0:bt, s * SL:(s + 1) * SL],
                    scalar=cp_full[0:bt, k * N + n_i: k * N + n_i + 1], in1=p_ffn[b][0:bt, :],
                    op0=ALU.mult, op1=ALU.mult))

                def _yadd(e, bt=bt, dh=dh, cb=s % 4):
                    with nc.allow_low_precision(reason="y partial sums kept in bf16; rvr-checked"):
                        return e.tensor_tensor(
                            out=yt_bf[0:bt, dh:dh + SL], in0=yt_bf[0:bt, dh:dh + SL],
                            in1=ces_t[0:bt, cb * SL:(cb + 1) * SL], op=ALU.add)
                ev_yadd[gs] = emit("vector", _yadd)

    def emit_finalize(k):
        t0, bt, regions = blocks[k]
        bb = base2[k % 2]
        waits("vector", [ev_yadd[k * NSLC + NSLC - 1]])
        ev_out = emit("vector", lambda e, k=k, bt=bt, bb=bb: e.scalar_tensor_tensor(
            out=bb[0:bt, :], in0=yt_bf[0:bt, :], scalar=sfull[k % 2][0:bt, :],
            in1=bb[0:bt, :], op0=ALU.mult, op1=ALU.add))
        wait("sync", ev_out)
        ev_store_b[k] = emit("sync", lambda e, t0=t0, bt=bt, bb=bb: e.dma_start(
            out=out_h[t0:t0 + bt, :], in_=bb[0:bt, :]), inc=16)

    def emit_chain(k):
        t0, bt, regions = blocks[k]
        nreg = len(regions)
        g0 = k * NSLC
        waits("gpsimd", [ev_E[g0 + NSLC - 1], ev_evac[g0 + NSLC - 1], ev_evac[g0 + NSLC - 2],
                         pe(ev_chain_b, k - 1)])
        emit("gpsimd", lambda e, nreg=nreg: e.dma_start(
            out=chE_d[0:nreg, :], in_=e_blk[114:114 + nreg, :]), inc=16)
        emit("gpsimd", lambda e, nreg=nreg: e.dma_start(
            out=chS_d[0:nreg, :], in_=s_blk[114:114 + nreg, :]), inc=16)
        emit("gpsimd", lambda e, nreg=nreg: e.dma_start(
            out=chE[:, 0:nreg * 128].rearrange("p (c j) -> p c j", j=128),
            in_=chE_d[0:nreg, :].rearrange("c (q j) -> q c j", j=128)), inc=16)
        ev_ext = [emit("gpsimd", lambda e, nreg=nreg: e.dma_start(
            out=chS[:, 0:nreg * 128].rearrange("p (c j) -> p c j", j=128),
            in_=chS_d[0:nreg, :].rearrange("c (q j) -> q c j", j=128)), inc=16)]
        ev_ext_b[k] = ev_ext[-1]
        waits("gpsimd", [pe(ev_hcarry_b, k - 1)])
        ev_h0w = [emit("gpsimd", lambda e: e.tensor_copy(out=h0st[:, 0:128], in_=h_t[:]))]
        waits("vector", ev_ext + [pe(ev_wb_b, k - 1)])
        emit("vector", lambda e: e.tensor_tensor(
            out=ht_tmp[:], in0=h_t[:], in1=chS[:, 0:128], op=ALU.add))
        pend_pool_h0 = None
        for c, (lo, hi, clamps) in enumerate(regions):
            last = (c == nreg - 1)
            waits("vector", [pend_pool_h0])
            pend_pool_h0 = None
            if clamps:
                emit("vector", lambda e, c=c: e.scalar_tensor_tensor(
                    out=ht_new[:], in0=chE[:, c * 128:(c + 1) * 128], scalar=1.0,
                    in1=ht_tmp[:], op0=ALU.mult, op1=ALU.mult))
                ev_q = emit("vector", lambda e: e.scalar_tensor_tensor(
                    out=ht_tmp[:], in0=ht_new[:], scalar=1.0, in1=ht_new[:],
                    op0=ALU.mult, op1=ALU.mult, accum_out=qcol[:]))
                wait("tensor", ev_q)
                ev_qs = emit("tensor", lambda e: e.matmul(
                    out=p_s, lhsT=ones_sq[:], rhs=qcol[:], start=True, stop=True))
                wait("scalar", ev_qs)
                ev_rt = emit("scalar", lambda e: e.activation(
                    out=scol[:], in_=p_s, func=AF.Sqrt))
                wait("vector", ev_rt)
                emit("vector", lambda e: e.reciprocal(out=scol[:], in_=scol[:]))
                emit("vector", lambda e: e.tensor_scalar(
                    out=scol[:], in0=scol[:], scalar1=20.0, scalar2=1.0, op0=ALU.mult, op1=ALU.min))
                if last:
                    waits("vector", [ev_h0w[0]])
                    ev_hcarry_b[k] = emit("vector", lambda e: e.tensor_scalar(
                        out=h_t[:], in0=ht_new[:], scalar1=scol[:], scalar2=None, op0=ALU.mult))
                else:
                    emit("vector", lambda e, c=c: e.scalar_tensor_tensor(
                        out=ht_tmp[:], in0=ht_new[:], scalar=scol[:],
                        in1=chS[:, (c + 1) * 128:(c + 2) * 128], op0=ALU.mult, op1=ALU.add))
                    ev_h0w.append(emit("vector", lambda e, c=c: e.tensor_scalar(
                        out=h0st[:, (c + 1) * 128:(c + 2) * 128], in0=ht_new[:],
                        scalar1=scol[:], scalar2=None, op0=ALU.mult)))
                emit("vector", lambda e: e.tensor_scalar(
                    out=sm1[:], in0=scol[:], scalar1=-1.0, scalar2=None, op0=ALU.add))
                ev_chain_b[k] = emit("vector", lambda e, hi=hi, bt=bt, k=k: e.scalar_tensor_tensor(
                    out=sfull[k % 2][0:bt, :], in0=id_f32[0:bt, hi:hi + 1], scalar=sm1[0:bt, :],
                    in1=sfull[k % 2][0:bt, :], op0=ALU.mult, op1=ALU.add))
            else:
                if last:
                    waits("vector", [ev_h0w[0]])
                    ev_chain_b[k] = ev_hcarry_b[k] = emit("vector", lambda e, c=c: e.tensor_tensor(
                        out=h_t[:], in0=chE[:, c * 128:(c + 1) * 128], in1=ht_tmp[:], op=ALU.mult))
                else:
                    ev_hn = emit("vector", lambda e, c=c: e.tensor_tensor(
                        out=ht_new[:], in0=chE[:, c * 128:(c + 1) * 128], in1=ht_tmp[:], op=ALU.mult))
                    ev_chain_b[k] = emit("vector", lambda e, c=c: e.tensor_tensor(
                        out=ht_tmp[:], in0=ht_new[:], in1=chS[:, (c + 1) * 128:(c + 2) * 128], op=ALU.add))
                    waits("gpsimd", [ev_chain_b[k]])
                    pend_pool_h0 = emit("gpsimd", lambda e, c=c: e.tensor_copy(
                        out=h0st[:, (c + 1) * 128:(c + 2) * 128], in_=ht_new[:]))
                    ev_h0w.append(pend_pool_h0)
        waits("gpsimd", ev_h0w + [pe(ev_hcarry_b, k)])
        emit("gpsimd", lambda e, nreg=nreg: e.dma_start(
            out=h0_d[0:nreg, :].rearrange("c (q j) -> q c j", j=128),
            in_=h0st[:, 0:nreg * 128].rearrange("p (c j) -> p c j", j=128)), inc=16)
        ev_wb_b[k] = emit("gpsimd", lambda e, nreg=nreg: e.dma_start(
            out=s_blk[114:114 + nreg, :], in_=h0_d[0:nreg, :]), inc=16)

    barrier(snapshot())
    NBL = len(blocks)
    if stage >= 4:
        emit_head(0)
        for k in range(NBL):
            for i in range(NSLC + 5):
                emit_2a_stage(k, i)
                if k >= 1 and stage >= 6:
                    emit_2b_stage(k - 1, i)
            if k >= 1 and stage >= 6:
                emit_finalize(k - 1)
            if k + 1 < NBL:
                emit_head(k + 1)
            if stage >= 5:
                emit_chain(k)
            if debug and k == 0:
                barrier(snapshot())
                emit("gpsimd", lambda e: e.dma_start(out=dbg["dbg_E0"], in_=e_blk[:]), inc=16)
                emit("gpsimd", lambda e: e.dma_start(out=dbg["dbg_S0"], in_=s_blk[:]), inc=16)
                emit("gpsimd", lambda e: e.dma_start(out=dbg["dbg_h0s"], in_=h0st[:]), inc=16)
                emit("gpsimd", lambda e: e.dma_start(out=dbg["dbg_xn0"], in_=xn_t[0:BT, :]), inc=16)
                barrier(snapshot())
        if stage >= 6:
            for i in range(NSLC + 2):
                emit_2b_stage(NBL - 1, i)
            emit_finalize(NBL - 1)
            if debug:
                barrier(snapshot())
                emit("gpsimd", lambda e: e.dma_start(out=dbg["dbg_y0"], in_=base2[(NBL - 1) % 2][0:BT, :]), inc=16)
                emit("gpsimd", lambda e, k=0: e.dma_start(out=dbg["dbg_sfull"], in_=sfull[0][:]), inc=16)
                barrier(snapshot())

    if debug:
        barrier(snapshot())
        emit("gpsimd", lambda e: e.dma_start(out=dbg["dbg_dt"], in_=dt_full[:]), inc=16)
        emit("gpsimd", lambda e: e.dma_start(out=dbg["dbg_bp"], in_=bp_full[:]), inc=16)
        emit("gpsimd", lambda e: e.dma_start(out=dbg["dbg_cp"], in_=cp_full[:]), inc=16)

    def thunk_final_d(eng):
        eng.wait_ge(sem_d, cnt["d"])
    prog["sync"].append(thunk_final_d)

    def thunk_final_g(eng):
        eng.wait_ge(sem_g, cnt["g"])
        eng.wait_ge(sem_p, cnt["p"])
    prog["gpsimd"].append(thunk_final_g)

    with es, nc.Block() as block:
        @block.sync
        def _(eng):
            for f in prog["sync"]:
                f(eng)

        @block.gpsimd
        def _(eng):
            for f in prog["gpsimd"]:
                f(eng)

        @block.tensor
        def _(eng):
            for f in prog["tensor"]:
                f(eng)

        @block.vector
        def _(eng):
            for f in prog["vector"]:
                f(eng)

        @block.scalar
        def _(eng):
            for f in prog["scalar"]:
                f(eng)

    return nc


def kernel(L0_attn_x, L0_attn_W_dt, L0_attn_b_dt, L0_attn_W_B, L0_attn_W_C,
           L0_attn_A_log, L0_attn_D_param, L0_ffn_weight, L0_ffn_bias):
    _, consts = build_consts()
    nc = build_nc(debug=False)
    base = {
        "W_dtT": np.ascontiguousarray(np.asarray(L0_attn_W_dt, dtype=np.float32).T),
        "b_dt": np.ascontiguousarray(L0_attn_b_dt, dtype=np.float32).reshape(1, D),
        "W_BT": np.ascontiguousarray(np.asarray(L0_attn_W_B, dtype=np.float32).T),
        "W_CT": np.ascontiguousarray(np.asarray(L0_attn_W_C, dtype=np.float32).T),
        "A_log": np.ascontiguousarray(L0_attn_A_log, dtype=np.float32),
        "D_param": np.ascontiguousarray(L0_attn_D_param, dtype=np.float32).reshape(1, D),
        "W_ffnT": np.ascontiguousarray(np.asarray(L0_ffn_weight, dtype=np.float32).T),
        "b_ffn": np.ascontiguousarray(L0_ffn_bias, dtype=np.float32).reshape(1, D),
    }
    base.update(consts)
    x = np.ascontiguousarray(L0_attn_x, dtype=np.float32)
    in_maps = []
    for i in range(8):
        m = dict(base)
        m["x"] = x[i]
        in_maps.append(m)
    res = run_bass_kernel_spmd(nc, in_maps, core_ids=list(range(8)))
    out = np.stack([np.asarray(r["out"]) for r in res.results], axis=0).astype(np.float32)
    return out


# version 15
# speedup vs baseline: 1.0923x; 1.0284x over previous
import contextlib
import numpy as np
import ml_dtypes
import concourse.bass as bass
import concourse.mybir as mybir
from concourse.bass_utils import run_bass_kernel_spmd

F32 = mybir.dt.float32
BF16 = mybir.dt.bfloat16
FP8 = mybir.dt.float8e4
AF = mybir.ActivationFunctionType
ALU = mybir.AluOpType
AX = mybir.AxisListType

BFD = ml_dtypes.bfloat16
F8D = ml_dtypes.float8_e4m3fn

L = 2048
D = 1024
N = 16
DN = D * N
BT = 114
NBLK = 18
NSLC = 32
SL = 512
RMAX = 13
NPAT = 6

DT_C = (0.1 + 0.001) / 2.0
DT_R = (0.1 - 0.001) / 2.0


def block_structure():
    blocks = []
    for k in range(NBLK):
        t0 = k * BT
        bt = min(BT, L - t0)
        regions = []
        lo = 0
        for r in range(1, bt + 1):
            t = t0 + r
            if r == bt or t % 10 == 1:
                hi = r - 1
                clamps = ((t0 + hi) % 10 == 0)
                regions.append((lo, hi, clamps))
                lo = r
        blocks.append((t0, bt, regions))
    return blocks


def pat_of(k):
    return 5 if k == NBLK - 1 else k % 5


def build_consts():
    blocks = block_structure()
    tri = np.zeros((NPAT, BT, 128), dtype=np.float32)
    mre = np.zeros((NPAT, 128, BT), dtype=np.float32)
    done = set()
    for k, (t0, bt, regions) in enumerate(blocks):
        p = pat_of(k)
        if p in done:
            continue
        done.add(p)
        for ci, (lo, hi, cl) in enumerate(regions):
            for tp in range(lo, hi + 1):
                tri[p, lo:tp + 1, tp] = 1.0
                tri[p, tp, 114 + ci] = 1.0
                mre[p, tp, tp] = 1.0
                mre[p, 114 + ci, tp] = 1.0
    ident = np.eye(128, dtype=np.float32)
    c = {
        "c_tri": tri.reshape(NPAT * BT, 128).astype(BFD),
        "c_mre": mre.reshape(NPAT * 128, BT).astype(BFD),
        "c_id_bf": ident.astype(BFD),
        "c_id_f32": ident.copy(),
        "c_id_f8": ident.astype(F8D),
        "c_ones_bf": np.ones((1, BT), dtype=np.float32).astype(BFD),
        "c_ones_row": np.ones((1, 128), dtype=np.float32),
        "c_ones_p": np.ones((128, 1), dtype=np.float32),
        "c_ones_sq": np.ones((128, 128), dtype=np.float32),
        "c_bias": np.tile(np.array([[1e-5, 1.0, -DT_C / DT_R]], dtype=np.float32), (128, 1)),
    }
    return blocks, c


def build_nc(debug=False, nblk=NBLK, stage=9):
    blocks, _ = build_consts()
    blocks = blocks[:nblk]
    nc = bass.Bass("TRN2", target_bir_lowering=False)

    x_h = nc.dram_tensor("x", [L, D], F32, kind="ExternalInput").ap()
    wdt_h = nc.dram_tensor("W_dtT", [D, D], F32, kind="ExternalInput").ap()
    bdt_h = nc.dram_tensor("b_dt", [1, D], F32, kind="ExternalInput").ap()
    wb_h = nc.dram_tensor("W_BT", [D, N], F32, kind="ExternalInput").ap()
    wc_h = nc.dram_tensor("W_CT", [D, N], F32, kind="ExternalInput").ap()
    alog_h = nc.dram_tensor("A_log", [D, N], F32, kind="ExternalInput").ap()
    dpar_h = nc.dram_tensor("D_param", [1, D], F32, kind="ExternalInput").ap()
    wf_h = nc.dram_tensor("W_ffnT", [D, D], F32, kind="ExternalInput").ap()
    bf_h = nc.dram_tensor("b_ffn", [1, D], F32, kind="ExternalInput").ap()
    tri_h = nc.dram_tensor("c_tri", [NPAT * BT, 128], BF16, kind="ExternalInput").ap()
    mre_h = nc.dram_tensor("c_mre", [NPAT * 128, BT], BF16, kind="ExternalInput").ap()
    idb_h = nc.dram_tensor("c_id_bf", [128, 128], BF16, kind="ExternalInput").ap()
    idf_h = nc.dram_tensor("c_id_f32", [128, 128], F32, kind="ExternalInput").ap()
    id8_h = nc.dram_tensor("c_id_f8", [128, 128], FP8, kind="ExternalInput").ap()
    onesb_h = nc.dram_tensor("c_ones_bf", [1, BT], BF16, kind="ExternalInput").ap()
    onesr_h = nc.dram_tensor("c_ones_row", [1, 128], F32, kind="ExternalInput").ap()
    onesp_h = nc.dram_tensor("c_ones_p", [128, 1], F32, kind="ExternalInput").ap()
    bias_h = nc.dram_tensor("c_bias", [128, 3], F32, kind="ExternalInput").ap()
    onessq_h = nc.dram_tensor("c_ones_sq", [128, 128], F32, kind="ExternalInput").ap()
    a16_d = nc.dram_tensor("a16_scratch", [D, N], BF16, kind="Internal").ap()
    a16_d2 = nc.dram_tensor("a16_scratch2", [N, D], BF16, kind="Internal").ap()
    chE_d = nc.dram_tensor("chE_scratch", [RMAX, DN], BF16, kind="Internal").ap()
    chS_d = nc.dram_tensor("chS_scratch", [RMAX, DN], BF16, kind="Internal").ap()
    h0_d = nc.dram_tensor("h0_scratch", [RMAX, DN], BF16, kind="Internal").ap()
    xq_d = nc.dram_tensor("xq_scratch", [128, NBLK * 8 * BT], FP8, kind="Internal").ap()

    out_h = nc.dram_tensor("out", [L, D], F32, kind="ExternalOutput").ap()
    dbg = {}
    if debug:
        for nm, shp, dt_ in [("dbg_E0", [128, DN], F32),
                             ("dbg_S0", [128, DN], F32), ("dbg_y0", [BT, D], F32),
                             ("dbg_h0s", [128, RMAX * 128], F32),
                             ("dbg_sfull", [BT, 1], F32), ("dbg_bp", [128, NBLK * N], F32),
                             ("dbg_cp", [128, NBLK * N], F32), ("dbg_xn0", [BT, D], F32),
                             ("dbg_a16", [128, DN], F32)]:
            dbg[nm] = nc.dram_tensor(nm, shp, dt_, kind="ExternalOutput").ap()

    es = contextlib.ExitStack()

    def sb(name, shape, dtype):
        return es.enter_context(nc.sbuf_tensor(name, shape, dtype))

    def ps(name, shape, dtype):
        return es.enter_context(nc.psum_tensor(name, shape, dtype))

    dt2 = sb("dt2", [128, 2 * D], BF16)
    wdtT = sb("wdtT", [128, 8 * D], BF16)
    xnT_t = sb("xnT_t", [128, 8 * BT], BF16)
    ep_e = sb("ep_e", [128, 2 * SL], F32)
    wbT = sb("wbT", [128, 8 * N], BF16)
    wcT = sb("wcT", [128, 8 * N], BF16)
    bdt_row = sb("bdt_row", [1, D], BF16)
    xqs = [sb(f"xqs{i}", [128, 8 * BT], FP8) for i in range(2)]
    a16b = sb("a16b", [128, DN], BF16)
    s_blk = sb("s_blk", [128, DN], BF16)
    wqT = sb("wqT", [128, 8 * D], FP8)
    tri_c = sb("tri_c", [BT, NPAT * 128], BF16)
    mre_c = sb("mre_c", [128, NPAT * BT], BF16)
    bp_full = sb("bp_full", [128, NBLK * N], BF16)
    cp_full = sb("cp_full", [128, NBLK * N], F32)
    dbc = sb("dbc", [128, D], BF16)
    fbb = sb("fbb", [128, D], BF16)
    id_bf = sb("id_bf", [128, 128], BF16)
    id_f32 = sb("id_f32", [128, 128], F32)
    id_f8 = sb("id_f8", [128, 128], FP8)
    ones_bf = sb("ones_bf", [1, BT], BF16)
    ones_sq = sb("ones_sq", [128, 128], F32)
    bias_c = sb("bias_c", [128, 3], F32)
    chE = sb("chE", [128, RMAX * 128], BF16)
    chS = sb("chS", [128, RMAX * 128], BF16)
    h0st = sb("h0st", [128, RMAX * 128], BF16)
    h_t = sb("h_t", [128, 128], F32)
    ht_tmp = sb("ht_tmp", [128, 128], F32)
    ht_new = sb("ht_new", [128, 128], F32)
    sfull = [sb(f"sfull{i}", [BT, 1], F32) for i in range(2)]
    qcol = sb("qcol", [128, 1], F32)
    scol = sb("scol", [128, 1], F32)
    sm1 = sb("sm1", [128, 1], F32)
    x_t = sb("x_t", [128, D], F32)
    xn_t = sb("xn_t", [128, D], F32)
    w_t = sb("w_t", [128, D], BF16)
    mean_c = sb("mean_c", [128, 1], F32)
    sq_c = sb("sq_c", [128, 1], F32)
    var_c = sb("var_c", [128, 1], F32)
    rstd_c = sb("rstd_c", [128, 1], F32)
    garg = [sb(f"garg{i}", [BT, SL], BF16) for i in range(4)]
    g_t = [sb(f"g_t{i}", [BT, SL], BF16) for i in range(2)]
    en2_t = sb("en2_t", [BT, 2 * SL], BF16)
    r2_t = [sb(f"r2_t{i}", [BT, 2 * SL], BF16) for i in range(1)]
    ces_t = sb("ces_t", [BT, 4 * SL], BF16)
    yt_bf = sb("yt_bf", [BT, D], BF16)
    yt_blk = sb("yt_blk", [BT, D], F32)
    base_b = sb("base_b", [BT, D], F32)

    off0 = (nc.sbuf_base + 31) & ~31
    arena = sb("arena", [128, 16384], BF16)

    def at(name, shape, dtype, byte_off):
        return nc.alloc_sbuf_tensor_at(name, shape, dtype, offset=off0 + byte_off)

    e_blk = arena
    wslab = at("wslab", [128, D], F32, 16384)
    atile = at("atile", [128, 128], F32, 0)
    a16t = at("a16t", [128, 128], BF16, 512)

    p_ca = [ps(f"p_ca{i}", [128, 2 * SL], F32) for i in range(2)]
    p_sp = [ps(f"p_sp{i}", [128, SL], F32) for i in range(2)]
    p_ffn = [ps(f"p_ffn{i}", [128, SL], F32) for i in range(2)]
    p_trb = [p_ca[i][:].bitcast(BF16)[:, 0:128] for i in range(2)]
    p_tr8 = [p_ca[i][:].bitcast(FP8)[:, 512:768].rearrange("p (j t) -> p j t", t=2)[:, :, 0:1].squeeze(2) for i in range(2)]
    p_s = p_sp[0][:, 300:301]

    sem_d = es.enter_context(nc.semaphore("sem_d"))
    sem_g = es.enter_context(nc.semaphore("sem_g"))
    sem_p = es.enter_context(nc.semaphore("sem_p"))
    sem_t = es.enter_context(nc.semaphore("sem_t"))
    sem_v = es.enter_context(nc.semaphore("sem_v"))
    sem_s = es.enter_context(nc.semaphore("sem_s"))

    cnt = {"d": 0, "g": 0, "p": 0, "t": 0, "v": 0, "s": 0}
    sems = {"d": sem_d, "g": sem_g, "p": sem_p, "t": sem_t, "v": sem_v, "s": sem_s}
    prog = {"sync": [], "gpsimd": [], "tensor": [], "vector": [], "scalar": []}
    ENG_SEM = {"sync": "d", "gpsimd": "g", "tensor": "t", "vector": "v", "scalar": "s"}
    ENG_KEYS = {"sync": ("d",), "gpsimd": ("g", "p"), "tensor": ("t",),
                "vector": ("v",), "scalar": ("s",)}

    def emit(engine, fn, inc=1):
        if engine == "gpsimd":
            key = "g" if inc == 16 else "p"
        else:
            key = ENG_SEM[engine]
        befores = [(kk, cnt[kk]) for kk in ENG_KEYS[engine]]
        cnt[key] += inc
        sem = sems[key]
        if engine != "tensor":
            for kk, before in befores:
                if before > 0:
                    def wthunk(eng, sem=sems[kk], before=before):
                        eng.wait_ge(sem, before)
                    prog[engine].append(wthunk)

        def thunk(eng, fn=fn, sem=sem, inc=inc):
            fn(eng).then_inc(sem, inc)
        prog[engine].append(thunk)
        return (key, cnt[key])

    def wait(engine, ev):
        if ev is None:
            return
        key, val = ev
        if key in ENG_KEYS[engine]:
            return

        def thunk(eng, key=key, val=val):
            eng.wait_ge(sems[key], val)
        prog[engine].append(thunk)

    def waits(engine, evs):
        best = {}
        for ev in evs:
            if ev is None:
                continue
            kk, vv = ev
            if kk not in ENG_KEYS[engine] and best.get(kk, -1) < vv:
                best[kk] = vv
        for kk, vv in best.items():
            wait(engine, (kk, vv))

    def snapshot():
        return dict(cnt)

    def barrier(snap):
        evs = [(kk, vv) for kk, vv in snap.items()]
        for eng in ("tensor", "vector", "scalar", "gpsimd", "sync"):
            waits(eng, evs)

    ev_pre = []
    for dst, src in [(id_bf[:], idb_h), (id_f32[:], idf_h), (id_f8[:], id8_h),
                     (ones_bf[:], onesb_h),
                     (bias_c[:], bias_h), (ones_sq[:], onessq_h),
                     (atile[:], alog_h.rearrange("(q p) n -> p q n", p=128))]:
        ev_pre.append(emit("sync", lambda e, d=dst, s=src: e.dma_start(out=d, in_=s), inc=16))
    for p in range(NPAT):
        ev_pre.append(emit("sync", lambda e, p=p: e.dma_start(
            out=tri_c[:, p * 128:(p + 1) * 128], in_=tri_h[p * BT:(p + 1) * BT, :]), inc=16))
        ev_pre.append(emit("sync", lambda e, p=p: e.dma_start(
            out=mre_c[:, p * BT:(p + 1) * BT], in_=mre_h[p * 128:(p + 1) * 128, :]), inc=16))

    nc.const_aps.register(F32, 1e-5, bias_c[:, 0:1])
    nc.const_aps.register(F32, 1.0, bias_c[:, 1:2])
    nc.const_aps.register(F32, -DT_C / DT_R, bias_c[:, 2:3])
    waits("scalar", ev_pre)
    emit("scalar", lambda e: e.activation(out=atile[:], in_=atile[:], func=AF.Tanh, scale=0.1))
    ev = emit("scalar", lambda e: e.activation(out=atile[:], in_=atile[:], func=AF.Exp, scale=10.0))
    wait("vector", ev)
    ev_a16 = emit("vector", lambda e: e.tensor_scalar(
        out=a16t[:], in0=atile[:], scalar1=-1.0 / 16.0, scalar2=None, op0=ALU.mult))
    wait("gpsimd", ev_a16)
    ev_a16d = emit("gpsimd", lambda e: e.dma_start(out=a16_d.rearrange("(q p) n -> p q n", p=128), in_=a16t[:].rearrange("p (q n) -> p q n", n=N)), inc=16)
    wait("gpsimd", ev_a16d)
    for ni in range(N):
        ev_a16d = emit("gpsimd", lambda e, ni=ni: e.dma_start(
            out=a16_d2[ni:ni + 1, :], in_=a16_d[:, ni:ni + 1].rearrange("d n -> n d"),
            allow_slow_non_contiguous=True), inc=16)
    wait("gpsimd", ev_a16d)
    ev_a16b = emit("gpsimd", lambda e: e.dma_start(
        out=a16b[:],
        in_=a16_d2.rearrange("n d -> (n d)").unsqueeze(0).partition_broadcast(128).squeeze(1)), inc=16)
    waits("gpsimd", ev_pre)
    ev_dbc = emit("gpsimd", lambda e: e.dma_start(
        out=dbc[:], in_=dpar_h.partition_broadcast(128).squeeze(1)), inc=16)
    ev_fbb = emit("gpsimd", lambda e: e.dma_start(
        out=fbb[:], in_=bf_h.partition_broadcast(128).squeeze(1)), inc=16)

    ev_bd = emit("sync", lambda e: e.dma_start(out=x_t[0:1, :], in_=bdt_h), inc=16)
    wait("vector", ev_bd)
    ev_bdt = emit("vector", lambda e: e.tensor_copy(out=bdt_row[:], in_=x_t[0:1, :]))

    if stage >= 2:
        waits("sync", [ev_a16d, ev_dbc, ev_fbb, ev_a16b])
        waits("tensor", ev_pre)
        ev_cp_prev = ev_bdt
        for i in range(8):
            waits("sync", [ev_cp_prev])
            ev1 = emit("sync", lambda e, i=i: e.dma_start(out=wslab[:], in_=wdt_h[i * 128:(i + 1) * 128, :]), inc=16)
            wait("vector", ev1)
            ev_cp_prev = emit("vector", lambda e, i=i: e.tensor_copy(
                out=wdtT[:, i * D:(i + 1) * D], in_=wslab[:]))
        for i in range(8):
            waits("sync", [ev_cp_prev])
            ev1 = emit("sync", lambda e, i=i: e.dma_start(out=wslab[:], in_=wf_h[i * 128:(i + 1) * 128, :]), inc=16)
            wait("scalar", ev1)
            ev_cp_prev = emit("scalar", lambda e, i=i: e.copy(
                out=wqT[:, i * D:(i + 1) * D], in_=wslab[:]))
        waits("sync", [ev_cp_prev])
        ev1 = emit("sync", lambda e: e.dma_start(
            out=wslab[:, 0:8 * N].rearrange("p (i n) -> p i n", n=N),
            in_=wb_h.rearrange("(i p) n -> p i n", p=128)), inc=16)
        ev1b = emit("sync", lambda e: e.dma_start(
            out=wslab[:, 8 * N:16 * N].rearrange("p (i n) -> p i n", n=N),
            in_=wc_h.rearrange("(i p) n -> p i n", p=128)), inc=16)
        waits("vector", [ev1, ev1b])
        emit("vector", lambda e: e.tensor_copy(out=wbT[:], in_=wslab[:, 0:8 * N]))
        ev_cp_prev = emit("vector", lambda e: e.tensor_copy(out=wcT[:], in_=wslab[:, 8 * N:16 * N]))

    emit("vector", lambda e: e.memset(h_t[:], 0.0))
    emit("vector", lambda e: e.memset(h0st[:], 0.0))
    emit("vector", lambda e: e.memset(s_blk[:], 0.0))
    emit("vector", lambda e: e.memset(chE[:], 0.0))
    emit("vector", lambda e: e.memset(chS[:], 0.0))
    if debug:
        emit("vector", lambda e: e.memset(dt2[:], 0.0))
        emit("vector", lambda e: e.memset(bp_full[:], 0.0))
        emit("vector", lambda e: e.memset(cp_full[:], 0.0))
        emit("vector", lambda e: e.memset(sfull[0][:], 0.0))
        emit("vector", lambda e: e.memset(sfull[1][:], 0.0))
        emit("vector", lambda e: e.memset(yt_blk[:], 0.0))
        emit("vector", lambda e: e.memset(xn_t[:], 0.0))
        barrier(snapshot())
    if debug:
        waits("gpsimd", [ev_a16b])
        emit("gpsimd", lambda e: e.dma_start(out=dbg["dbg_a16"], in_=a16b[:]), inc=16)

    a_sq = [None] * NBLK
    a_v2 = [None] * NBLK
    a_rstd = [None] * NBLK
    a_xn = [None] * NBLK
    a_trlast = [None] * NBLK
    a_xnTl = [None] * NBLK
    a_cp = [None] * NBLK
    a_e1 = [[None, None] for _ in range(NBLK)]
    a_e1s = [None] * NBLK
    a_dtf = [None] * NBLK
    NG = NBLK * NSLC
    ev_g = [None] * NG
    ev_garg = [None] * NG
    ev_ca = [None] * NG
    ev_E = [None] * NG
    ev_en = [None] * NG
    ev_r = [None] * NG
    ev_smm = [None] * NG
    ev_evac = [None] * NG
    ev_re = [None] * NG
    ev_stt = [None] * NG
    ev_yadd = [None] * NG
    ev_sq_b = [None] * NBLK
    ev_v2_b = [None] * NBLK
    ev_sr_b = [None] * NBLK
    ev_xn_b = [None] * NBLK
    ev_w_b = [None] * NBLK
    ev_ext_b = [None] * NBLK
    ev_chain_b = [None] * NBLK
    ev_wb_b = [None] * NBLK
    ev_basex_b = [None] * NBLK
    ev_baseff_b = [None] * NBLK
    ev_store_b = [None] * NBLK
    ev_hcarry_b = [None] * NBLK
    ev_ffn_b = [None] * NBLK

    def pe(lst, i):
        return lst[i] if i >= 0 else None

    base2 = [yt_blk, base_b]

    def emit_head(k):
        t0, bt, regions = blocks[k]
        kb = k % 2
        waits("sync", [pe(a_sq, k - 1), pe(a_xn, k - 1), pe(ev_basex_b, k - 1), ev_bdt])
        evx = emit("sync", lambda e, t0=t0, bt=bt: e.dma_start(
            out=x_t[0:bt, :], in_=x_h[t0:t0 + bt, :]), inc=16)
        wait("vector", evx)
        emit("vector", lambda e, bt=bt: e.tensor_reduce(
            out=mean_c[0:bt, :], in_=x_t[0:bt, :], axis=AX.X, op=ALU.add))
        waits("scalar", [evx, pe(a_v2, k - 1), pe(ev_r, k * NSLC - 1)])
        ev_sq_b[k] = a_sq[k] = emit("scalar", lambda e, bt=bt: e.activation(
            out=w_t[0:bt, :], in_=x_t[0:bt, :], func=AF.Square, accum_out=sq_c[0:bt, :]))
        waits("vector", [a_sq[k], pe(a_rstd, k - 1)])
        emit("vector", lambda e, bt=bt: e.tensor_scalar(
            out=mean_c[0:bt, :], in0=mean_c[0:bt, :], scalar1=1.0 / D, scalar2=None, op0=ALU.mult))
        emit("vector", lambda e, bt=bt: e.tensor_tensor(
            out=var_c[0:bt, :], in0=mean_c[0:bt, :], in1=mean_c[0:bt, :], op=ALU.mult))
        a_v2[k] = emit("vector", lambda e, bt=bt: e.scalar_tensor_tensor(
            out=var_c[0:bt, :], in0=sq_c[0:bt, :], scalar=1.0 / D, in1=var_c[0:bt, :],
            op0=ALU.mult, op1=ALU.subtract))
        waits("scalar", [a_v2[k], pe(a_xn, k - 1)])
        emit("scalar", lambda e, bt=bt: e.activation(
            out=rstd_c[0:bt, :], in_=var_c[0:bt, :], func=AF.Ln, bias=1e-5))
        a_rstd[k] = emit("scalar", lambda e, bt=bt: e.activation(
            out=rstd_c[0:bt, :], in_=rstd_c[0:bt, :], func=AF.Exp, scale=-0.5))
        waits("vector", [a_rstd[k], pe(a_trlast, k - 1)])
        a_xn[k] = emit("vector", lambda e, bt=bt: e.tensor_scalar(
            out=xn_t[0:bt, :], in0=x_t[0:bt, :], scalar1=mean_c[0:bt, :], scalar2=rstd_c[0:bt, :],
            op0=ALU.subtract, op1=ALU.mult))
        ev_tr = [None] * 8
        ev_evv = [None] * 8
        ev_evs = [None] * 8
        for j in range(9):
            if j < 8:
                waits("tensor", [a_xn[k], pe(a_cp, k - 1),
                                 pe(ev_en, k * NSLC - 1) if j < 2 else None,
                                 ev_evs[j - 2] if j >= 2 else None])
                ev_tr[j] = emit("tensor", lambda e, j=j, bt=bt: e.transpose(
                    out=p_ca[j % 2][:, 0:bt], in_=xn_t[0:bt, j * 128:(j + 1) * 128],
                    identity=id_f32[0:bt, 0:bt]))
            if j >= 1:
                jj = j - 1
                wait("vector", ev_tr[jj])
                ev_evv[jj] = emit("vector", lambda e, jj=jj, bt=bt: e.tensor_copy(
                    out=xnT_t[:, jj * BT: jj * BT + bt], in_=p_ca[jj % 2][:, 0:bt]))
                waits("scalar", [ev_tr[jj], ev_evv[jj],
                                 pe(ev_ffn_b, k - 2) if jj == 0 else None])
                ev_evs[jj] = emit("scalar", lambda e, jj=jj, bt=bt, kb=kb: e.copy(
                    out=xqs[kb][:, jj * BT: jj * BT + bt], in_=p_ca[jj % 2][:, 0:bt]))
        a_trlast[k] = ev_tr[7]
        a_xnTl[k] = ev_evv[7]
        ev_mmd = [None, None]
        for h in range(2):
            waits("tensor", [a_xnTl[k],
                             pe(ev_stt, (k - 1) * NSLC + NSLC - 1) if k >= 1 else None,
                             pe(ev_baseff_b, k - 1), pe(a_e1s, k - 1)])
            for j in range(8):
                emit("tensor", lambda e, j=j, h=h, bt=bt: e.matmul(
                    out=p_ffn[h][0:bt, :], lhsT=xnT_t[:, j * BT: j * BT + bt],
                    rhs=wdtT[:, j * D + h * SL: j * D + h * SL + SL],
                    start=(j == 0), stop=False))
            ev_mmd[h] = emit("tensor", lambda e, h=h, bt=bt: e.matmul(
                out=p_ffn[h][0:bt, :], lhsT=ones_bf[:, 0:bt],
                rhs=bdt_row[:, h * SL: h * SL + SL], start=False, stop=True))
        for h in range(2):
            waits("scalar", [ev_mmd[h], pe(a_dtf, k - 1)])
            a_e1[k][h] = emit("scalar", lambda e, h=h, bt=bt: e.activation(
                out=ep_e[0:bt, h * SL:(h + 1) * SL], in_=p_ffn[h][0:bt, :], func=AF.Exp))
        a_e1s[k] = a_e1[k][1]
        for h in range(2):
            emit("scalar", lambda e, h=h, bt=bt: e.activation(
                out=ep_e[0:bt, h * SL:(h + 1) * SL], in_=ep_e[0:bt, h * SL:(h + 1) * SL],
                func=AF.Ln, bias=1.0))
        ev_sg = [None, None]
        for h in range(2):
            ev_sg[h] = emit("scalar", lambda e, h=h, bt=bt: e.activation(
                out=ep_e[0:bt, h * SL:(h + 1) * SL], in_=ep_e[0:bt, h * SL:(h + 1) * SL],
                func=AF.Sigmoid, scale=1.0 / DT_R, bias=-DT_C / DT_R))
        for h in range(2):
            wait("vector", ev_sg[h])
            a_dtf[k] = emit("vector", lambda e, h=h, kb=kb, bt=bt: e.tensor_scalar(
                out=dt2[0:bt, kb * D + h * SL: kb * D + h * SL + SL],
                in0=ep_e[0:bt, h * SL:(h + 1) * SL],
                scalar1=4.0 * DT_R, scalar2=DT_C - 2.0 * DT_R, op0=ALU.mult, op1=ALU.add))
        waits("tensor", [a_xnTl[k], ev_evs[6], ev_evs[7]])
        for j in range(8):
            emit("tensor", lambda e, j=j, bt=bt: e.matmul(
                out=p_ca[0][0:bt, 0:N], lhsT=xnT_t[:, j * BT: j * BT + bt],
                rhs=wbT[:, j * N:(j + 1) * N], start=(j == 0), stop=(j == 7)))
        for j in range(8):
            ev_mmc = emit("tensor", lambda e, j=j, bt=bt: e.matmul(
                out=p_ca[0][0:bt, N:2 * N], lhsT=xnT_t[:, j * BT: j * BT + bt],
                rhs=wcT[:, j * N:(j + 1) * N], start=(j == 0), stop=(j == 7)))
        wait("vector", ev_mmc)
        emit("vector", lambda e, k=k, bt=bt: e.tensor_copy(
            out=bp_full[0:bt, k * N:(k + 1) * N], in_=p_ca[0][0:bt, 0:N]))
        a_cp[k] = emit("vector", lambda e, k=k, bt=bt: e.tensor_copy(
            out=cp_full[0:bt, k * N:(k + 1) * N], in_=p_ca[0][0:bt, N:2 * N]))
        ev_w_b[k] = emit("vector", lambda e, kb=kb, bt=bt: e.tensor_tensor(
            out=w_t[0:bt, :], in0=xn_t[0:bt, :], in1=dt2[0:bt, kb * D:(kb + 1) * D], op=ALU.mult))
        emit("vector", lambda e, kb=kb, bt=bt: e.memset(sfull[kb][0:bt, :], 1.0))
        ev_f = None
        waits("tensor", [a_e1[k][0], a_e1[k][1]])
        for h in range(2):
            for j in range(8):
                ev_f = emit("tensor", lambda e, j=j, h=h, bt=bt, kb=kb: e.matmul(
                    out=p_ffn[h][0:bt, :], lhsT=xqs[kb][:, j * BT: j * BT + bt],
                    rhs=wqT[:, j * D + h * SL: j * D + h * SL + SL],
                    start=(j == 0), stop=(j == 7)))
        ev_ffn_b[k] = ev_f
        bb = base2[kb]
        waits("vector", [pe(ev_store_b, k - 2)])
        emit("vector", lambda e, bt=bt, bb=bb: e.tensor_tensor(
            out=bb[0:bt, :], in0=xn_t[0:bt, :], in1=dbc[0:bt, :], op=ALU.mult))
        ev_basex_b[k] = emit("vector", lambda e, bt=bt, bb=bb: e.tensor_tensor(
            out=bb[0:bt, :], in0=bb[0:bt, :], in1=x_t[0:bt, :], op=ALU.add))
        waits("vector", [ev_ffn_b[k], ev_fbb])
        for h in range(2):
            emit("vector", lambda e, h=h, bt=bt, bb=bb: e.tensor_tensor(
                out=bb[0:bt, h * SL:(h + 1) * SL], in0=bb[0:bt, h * SL:(h + 1) * SL],
                in1=p_ffn[h][0:bt, :], op=ALU.add))
        ev_baseff_b[k] = emit("vector", lambda e, bt=bt, bb=bb: e.tensor_tensor(
            out=bb[0:bt, :], in0=bb[0:bt, :], in1=fbb[0:bt, :], op=ALU.add))

    def emit_2a_stage(k, i):
        t0, bt, regions = blocks[k]
        pat = pat_of(k)
        g0 = k * NSLC
        if i < NSLC:
            s = i
            gs = g0 + s
            b = s % 4
            dh = (s & 1) * SL
            geng = "vector"
            waits(geng, [pe(ev_g, gs - 4)])
            ev_garg[gs] = emit(geng, lambda e, kb=k % 2, bt=bt, dh=dh, s=s, b=b: e.tensor_tensor(
                out=garg[b][0:bt, :], in0=dt2[0:bt, kb * D + dh: kb * D + dh + SL],
                in1=a16b[0:bt, s * SL:(s + 1) * SL], op=ALU.mult))
        if 0 <= i - 1 < NSLC:
            s = i - 1
            gs = g0 + s
            waits("scalar", [ev_garg[gs], pe(ev_ca, gs - 2), ev_w_b[k]])
            ev_g[gs] = emit("scalar", lambda e, bt=bt, gb=s % 4, b=s % 2: e.activation(
                out=g_t[b][0:bt, :], in_=garg[gb][0:bt, :], func=AF.Tanh))
            waits("tensor", [ev_g[gs], pe(ev_en, gs - 4), a_cp[k] if s < 4 else None])
            ev_ca[gs] = emit("tensor", lambda e, pat=pat, bt=bt, s=s: e.matmul(
                out=p_ca[(s >> 1) % 2][:, (s & 1) * SL:((s & 1) + 1) * SL],
                lhsT=tri_c[0:bt, pat * 128:(pat + 1) * 128], rhs=g_t[s % 2][0:bt, :],
                start=True, stop=True))
        if 0 <= i - 2 < NSLC and (i - 2) % 2 == 1:
            s1 = i - 2
            s0 = s1 - 1
            gs1 = g0 + s1
            gs0 = g0 + s0
            pb = (s0 >> 1) % 2
            waits("scalar", [ev_ca[gs1], pe(ev_stt, gs0 - NSLC), pe(ev_stt, gs1 - NSLC), pe(ev_ext_b, k - 1)])
            ev_E[gs0] = ev_E[gs1] = emit("scalar", lambda e, s0=s0, pb=pb: e.activation(
                out=e_blk[:, s0 * SL:(s0 + 2) * SL], in_=p_ca[pb][:], func=AF.Exp, scale=16.0))
            waits("scalar", [pe(ev_r, gs0 - 2)])
            waits("scalar", [pe(ev_r, gs0 - 2)])
            ev_en[gs0] = ev_en[gs1] = emit("scalar", lambda e, bt=bt, pb=pb: e.activation(
                out=en2_t[0:bt, :], in_=p_ca[pb][0:bt, :], func=AF.Exp, scale=-16.0))
        if 0 <= i - 3 < NSLC and (i - 3) % 2 == 1:
            s1 = i - 3
            s0 = s1 - 1
            gs1 = g0 + s1
            gs0 = g0 + s0
            n_i = s0 >> 1
            waits("vector", [ev_en[gs1], pe(ev_smm, gs0 - 2), pe(ev_smm, gs1 - 2)])
            ev_r[gs0] = ev_r[gs1] = emit("vector", lambda e, k=k, bt=bt, n_i=n_i: e.scalar_tensor_tensor(
                out=r2_t[0][0:bt, :], in0=w_t[0:bt, :],
                scalar=bp_full[0:bt, k * N + n_i: k * N + n_i + 1],
                in1=en2_t[0:bt, :], op0=ALU.mult, op1=ALU.mult))
            for s in (s0, s1):
                gs = g0 + s
                waits("tensor", [ev_r[gs], pe(ev_evac, gs - 2)])
                ev_smm[gs] = emit("tensor", lambda e, pat=pat, bt=bt, b=s % 2, s=s: e.matmul(
                    out=p_sp[b][:], lhsT=tri_c[0:bt, pat * 128:(pat + 1) * 128],
                    rhs=r2_t[0][0:bt, (s & 1) * SL:((s & 1) + 1) * SL],
                    start=True, stop=True))
        if 0 <= i - 4 < NSLC:
            s = i - 4
            gs = g0 + s
            b = s % 2
            eng = "scalar" if (s % 2 == 0) else "vector"
            waits(eng, [ev_smm[gs], pe(ev_re, gs - NSLC), pe(ev_ext_b, k - 1)])
            if eng == "scalar":
                ev_evac[gs] = emit("scalar", lambda e, s=s, b=b: e.copy(
                    out=s_blk[:, s * SL:(s + 1) * SL], in_=p_sp[b][:]))
            else:
                ev_evac[gs] = emit("vector", lambda e, s=s, b=b: e.tensor_copy(
                    out=s_blk[:, s * SL:(s + 1) * SL], in_=p_sp[b][:]))

    def emit_2b_stage(k, i):
        t0, bt, regions = blocks[k]
        g0 = k * NSLC
        if i < NSLC:
            s = i
            gs = g0 + s
            waits("tensor", [ev_wb_b[k], ev_evac[gs], pe(ev_stt, gs - 2),
                             ev_baseff_b[k + 1] if k + 1 < NBLK and ev_baseff_b[k + 1] is not None else None])
            ev_re[gs] = emit("tensor", lambda e, k=k, bt=bt, s=s: e.matmul(
                out=p_ffn[s % 2][0:bt, :], lhsT=mre_c[:, pat_of(k) * BT: pat_of(k) * BT + bt],
                rhs=s_blk[:, s * SL:(s + 1) * SL], start=True, stop=True))
        if 0 <= i - 1 < NSLC:
            s = i - 1
            gs = g0 + s
            b = s % 2
            n_i = s >> 1
            dh = (s & 1) * SL
            waits("vector", [ev_re[gs]])
            if s < 2:
                ev_stt[gs] = emit("vector", lambda e, k=k, bt=bt, s=s, n_i=n_i, dh=dh, b=b: e.scalar_tensor_tensor(
                    out=yt_bf[0:bt, dh:dh + SL], in0=e_blk[0:bt, s * SL:(s + 1) * SL],
                    scalar=cp_full[0:bt, k * N + n_i: k * N + n_i + 1], in1=p_ffn[b][0:bt, :],
                    op0=ALU.mult, op1=ALU.mult))
                ev_yadd[gs] = ev_stt[gs]
            else:
                waits("vector", [pe(ev_yadd, gs - 4)])
                ev_stt[gs] = emit("vector", lambda e, k=k, bt=bt, s=s, n_i=n_i, cb=s % 4, b=b: e.scalar_tensor_tensor(
                    out=ces_t[0:bt, cb * SL:(cb + 1) * SL], in0=e_blk[0:bt, s * SL:(s + 1) * SL],
                    scalar=cp_full[0:bt, k * N + n_i: k * N + n_i + 1], in1=p_ffn[b][0:bt, :],
                    op0=ALU.mult, op1=ALU.mult))

                def _yadd(e, bt=bt, dh=dh, cb=s % 4):
                    with nc.allow_low_precision(reason="y partial sums kept in bf16; rvr-checked"):
                        return e.tensor_tensor(
                            out=yt_bf[0:bt, dh:dh + SL], in0=yt_bf[0:bt, dh:dh + SL],
                            in1=ces_t[0:bt, cb * SL:(cb + 1) * SL], op=ALU.add)
                ev_yadd[gs] = emit("vector", _yadd)

    def emit_finalize(k):
        t0, bt, regions = blocks[k]
        bb = base2[k % 2]
        waits("vector", [ev_yadd[k * NSLC + NSLC - 1]])
        ev_out = emit("vector", lambda e, k=k, bt=bt, bb=bb: e.scalar_tensor_tensor(
            out=bb[0:bt, :], in0=yt_bf[0:bt, :], scalar=sfull[k % 2][0:bt, :],
            in1=bb[0:bt, :], op0=ALU.mult, op1=ALU.add))
        wait("sync", ev_out)
        ev_store_b[k] = emit("sync", lambda e, t0=t0, bt=bt, bb=bb: e.dma_start(
            out=out_h[t0:t0 + bt, :], in_=bb[0:bt, :]), inc=16)

    def emit_chain(k):
        t0, bt, regions = blocks[k]
        nreg = len(regions)
        g0 = k * NSLC
        waits("gpsimd", [ev_E[g0 + NSLC - 1], ev_evac[g0 + NSLC - 1], ev_evac[g0 + NSLC - 2],
                         pe(ev_chain_b, k - 1)])
        emit("gpsimd", lambda e, nreg=nreg: e.dma_start(
            out=chE_d[0:nreg, :], in_=e_blk[114:114 + nreg, :]), inc=16)
        emit("gpsimd", lambda e, nreg=nreg: e.dma_start(
            out=chS_d[0:nreg, :], in_=s_blk[114:114 + nreg, :]), inc=16)
        emit("gpsimd", lambda e, nreg=nreg: e.dma_start(
            out=chE[:, 0:nreg * 128].rearrange("p (c j) -> p c j", j=128),
            in_=chE_d[0:nreg, :].rearrange("c (q j) -> q c j", j=128)), inc=16)
        ev_ext = [emit("gpsimd", lambda e, nreg=nreg: e.dma_start(
            out=chS[:, 0:nreg * 128].rearrange("p (c j) -> p c j", j=128),
            in_=chS_d[0:nreg, :].rearrange("c (q j) -> q c j", j=128)), inc=16)]
        ev_ext_b[k] = ev_ext[-1]
        waits("gpsimd", [pe(ev_hcarry_b, k - 1)])
        ev_h0w = [emit("gpsimd", lambda e: e.tensor_copy(out=h0st[:, 0:128], in_=h_t[:]))]
        waits("vector", ev_ext + [pe(ev_wb_b, k - 1)])
        emit("vector", lambda e: e.tensor_tensor(
            out=ht_tmp[:], in0=h_t[:], in1=chS[:, 0:128], op=ALU.add))
        pend_pool_h0 = None
        for c, (lo, hi, clamps) in enumerate(regions):
            last = (c == nreg - 1)
            waits("vector", [pend_pool_h0])
            pend_pool_h0 = None
            if clamps:
                emit("vector", lambda e, c=c: e.scalar_tensor_tensor(
                    out=ht_new[:], in0=chE[:, c * 128:(c + 1) * 128], scalar=1.0,
                    in1=ht_tmp[:], op0=ALU.mult, op1=ALU.mult))
                ev_q = emit("vector", lambda e: e.scalar_tensor_tensor(
                    out=ht_tmp[:], in0=ht_new[:], scalar=1.0, in1=ht_new[:],
                    op0=ALU.mult, op1=ALU.mult, accum_out=qcol[:]))
                wait("tensor", ev_q)
                ev_qs = emit("tensor", lambda e: e.matmul(
                    out=p_s, lhsT=ones_sq[:], rhs=qcol[:], start=True, stop=True))
                wait("scalar", ev_qs)
                ev_rt = emit("scalar", lambda e: e.activation(
                    out=scol[:], in_=p_s, func=AF.Sqrt))
                wait("vector", ev_rt)
                emit("vector", lambda e: e.reciprocal(out=scol[:], in_=scol[:]))
                emit("vector", lambda e: e.tensor_scalar(
                    out=scol[:], in0=scol[:], scalar1=20.0, scalar2=1.0, op0=ALU.mult, op1=ALU.min))
                if last:
                    waits("vector", [ev_h0w[0]])
                    ev_hcarry_b[k] = emit("vector", lambda e: e.tensor_scalar(
                        out=h_t[:], in0=ht_new[:], scalar1=scol[:], scalar2=None, op0=ALU.mult))
                else:
                    emit("vector", lambda e, c=c: e.scalar_tensor_tensor(
                        out=ht_tmp[:], in0=ht_new[:], scalar=scol[:],
                        in1=chS[:, (c + 1) * 128:(c + 2) * 128], op0=ALU.mult, op1=ALU.add))
                    ev_h0w.append(emit("vector", lambda e, c=c: e.tensor_scalar(
                        out=h0st[:, (c + 1) * 128:(c + 2) * 128], in0=ht_new[:],
                        scalar1=scol[:], scalar2=None, op0=ALU.mult)))
                emit("vector", lambda e: e.tensor_scalar(
                    out=sm1[:], in0=scol[:], scalar1=-1.0, scalar2=None, op0=ALU.add))
                ev_chain_b[k] = emit("vector", lambda e, hi=hi, bt=bt, k=k: e.scalar_tensor_tensor(
                    out=sfull[k % 2][0:bt, :], in0=id_f32[0:bt, hi:hi + 1], scalar=sm1[0:bt, :],
                    in1=sfull[k % 2][0:bt, :], op0=ALU.mult, op1=ALU.add))
            else:
                if last:
                    waits("vector", [ev_h0w[0]])
                    ev_chain_b[k] = ev_hcarry_b[k] = emit("vector", lambda e, c=c: e.tensor_tensor(
                        out=h_t[:], in0=chE[:, c * 128:(c + 1) * 128], in1=ht_tmp[:], op=ALU.mult))
                else:
                    ev_hn = emit("vector", lambda e, c=c: e.tensor_tensor(
                        out=ht_new[:], in0=chE[:, c * 128:(c + 1) * 128], in1=ht_tmp[:], op=ALU.mult))
                    ev_chain_b[k] = emit("vector", lambda e, c=c: e.tensor_tensor(
                        out=ht_tmp[:], in0=ht_new[:], in1=chS[:, (c + 1) * 128:(c + 2) * 128], op=ALU.add))
                    waits("gpsimd", [ev_chain_b[k]])
                    pend_pool_h0 = emit("gpsimd", lambda e, c=c: e.tensor_copy(
                        out=h0st[:, (c + 1) * 128:(c + 2) * 128], in_=ht_new[:]))
                    ev_h0w.append(pend_pool_h0)
        waits("gpsimd", ev_h0w + [pe(ev_hcarry_b, k)])
        emit("gpsimd", lambda e, nreg=nreg: e.dma_start(
            out=h0_d[0:nreg, :].rearrange("c (q j) -> q c j", j=128),
            in_=h0st[:, 0:nreg * 128].rearrange("p (c j) -> p c j", j=128)), inc=16)
        ev_wb_b[k] = emit("gpsimd", lambda e, nreg=nreg: e.dma_start(
            out=s_blk[114:114 + nreg, :], in_=h0_d[0:nreg, :]), inc=16)

    barrier(snapshot())
    NBL = len(blocks)
    if stage >= 4:
        emit_head(0)
        for k in range(NBL):
            for i in range(NSLC + 5):
                emit_2a_stage(k, i)
                if k >= 1 and stage >= 6:
                    emit_2b_stage(k - 1, i)
            if k >= 1 and stage >= 6:
                emit_finalize(k - 1)
            if k + 1 < NBL:
                emit_head(k + 1)
            if stage >= 5:
                emit_chain(k)
            if debug and k == 0:
                barrier(snapshot())
                emit("gpsimd", lambda e: e.dma_start(out=dbg["dbg_E0"], in_=e_blk[:]), inc=16)
                emit("gpsimd", lambda e: e.dma_start(out=dbg["dbg_S0"], in_=s_blk[:]), inc=16)
                emit("gpsimd", lambda e: e.dma_start(out=dbg["dbg_h0s"], in_=h0st[:]), inc=16)
                emit("gpsimd", lambda e: e.dma_start(out=dbg["dbg_xn0"], in_=xn_t[0:BT, :]), inc=16)
                barrier(snapshot())
        if stage >= 6:
            for i in range(NSLC + 2):
                emit_2b_stage(NBL - 1, i)
            emit_finalize(NBL - 1)
            if debug:
                barrier(snapshot())
                emit("gpsimd", lambda e: e.dma_start(out=dbg["dbg_y0"], in_=base2[(NBL - 1) % 2][0:BT, :]), inc=16)
                emit("gpsimd", lambda e, k=0: e.dma_start(out=dbg["dbg_sfull"], in_=sfull[0][:]), inc=16)
                barrier(snapshot())

    if debug:
        barrier(snapshot())
        emit("gpsimd", lambda e: e.dma_start(out=dbg["dbg_bp"], in_=bp_full[:]), inc=16)
        emit("gpsimd", lambda e: e.dma_start(out=dbg["dbg_cp"], in_=cp_full[:]), inc=16)

    def thunk_final_d(eng):
        eng.wait_ge(sem_d, cnt["d"])
    prog["sync"].append(thunk_final_d)

    def thunk_final_g(eng):
        eng.wait_ge(sem_g, cnt["g"])
        eng.wait_ge(sem_p, cnt["p"])
    prog["gpsimd"].append(thunk_final_g)

    with es, nc.Block() as block:
        @block.sync
        def _(eng):
            for f in prog["sync"]:
                f(eng)

        @block.gpsimd
        def _(eng):
            for f in prog["gpsimd"]:
                f(eng)

        @block.tensor
        def _(eng):
            for f in prog["tensor"]:
                f(eng)

        @block.vector
        def _(eng):
            for f in prog["vector"]:
                f(eng)

        @block.scalar
        def _(eng):
            for f in prog["scalar"]:
                f(eng)

    return nc


def kernel(L0_attn_x, L0_attn_W_dt, L0_attn_b_dt, L0_attn_W_B, L0_attn_W_C,
           L0_attn_A_log, L0_attn_D_param, L0_ffn_weight, L0_ffn_bias):
    _, consts = build_consts()
    nc = build_nc(debug=False)
    base = {
        "W_dtT": np.ascontiguousarray(np.asarray(L0_attn_W_dt, dtype=np.float32).T),
        "b_dt": np.ascontiguousarray(L0_attn_b_dt, dtype=np.float32).reshape(1, D),
        "W_BT": np.ascontiguousarray(np.asarray(L0_attn_W_B, dtype=np.float32).T),
        "W_CT": np.ascontiguousarray(np.asarray(L0_attn_W_C, dtype=np.float32).T),
        "A_log": np.ascontiguousarray(L0_attn_A_log, dtype=np.float32),
        "D_param": np.ascontiguousarray(L0_attn_D_param, dtype=np.float32).reshape(1, D),
        "W_ffnT": np.ascontiguousarray(np.asarray(L0_ffn_weight, dtype=np.float32).T),
        "b_ffn": np.ascontiguousarray(L0_ffn_bias, dtype=np.float32).reshape(1, D),
    }
    base.update(consts)
    x = np.ascontiguousarray(L0_attn_x, dtype=np.float32)
    in_maps = []
    for i in range(8):
        m = dict(base)
        m["x"] = x[i]
        in_maps.append(m)
    res = run_bass_kernel_spmd(nc, in_maps, core_ids=list(range(8)))
    out = np.stack([np.asarray(r["out"]) for r in res.results], axis=0).astype(np.float32)
    return out


# version 16
# speedup vs baseline: 1.1287x; 1.0332x over previous
import contextlib
import numpy as np
import ml_dtypes
import concourse.bass as bass
import concourse.mybir as mybir
from concourse.bass_utils import run_bass_kernel_spmd

F32 = mybir.dt.float32
BF16 = mybir.dt.bfloat16
FP8 = mybir.dt.float8e4
AF = mybir.ActivationFunctionType
ALU = mybir.AluOpType
AX = mybir.AxisListType

BFD = ml_dtypes.bfloat16
F8D = ml_dtypes.float8_e4m3fn

L = 2048
D = 1024
N = 16
DN = D * N
BT = 114
NBLK = 18
NSLC = 32
SL = 512
RMAX = 13
NPAT = 6

DT_C = (0.1 + 0.001) / 2.0
DT_R = (0.1 - 0.001) / 2.0


def block_structure():
    blocks = []
    for k in range(NBLK):
        t0 = k * BT
        bt = min(BT, L - t0)
        regions = []
        lo = 0
        for r in range(1, bt + 1):
            t = t0 + r
            if r == bt or t % 10 == 1:
                hi = r - 1
                clamps = ((t0 + hi) % 10 == 0)
                regions.append((lo, hi, clamps))
                lo = r
        blocks.append((t0, bt, regions))
    return blocks


def pat_of(k):
    return 5 if k == NBLK - 1 else k % 5


def build_consts():
    blocks = block_structure()
    tri = np.zeros((NPAT, BT, 128), dtype=np.float32)
    mre = np.zeros((NPAT, 128, BT), dtype=np.float32)
    done = set()
    for k, (t0, bt, regions) in enumerate(blocks):
        p = pat_of(k)
        if p in done:
            continue
        done.add(p)
        for ci, (lo, hi, cl) in enumerate(regions):
            for tp in range(lo, hi + 1):
                tri[p, lo:tp + 1, tp] = 1.0
                tri[p, tp, 114 + ci] = 1.0
                mre[p, tp, tp] = 1.0
                mre[p, 114 + ci, tp] = 1.0
    ident = np.eye(128, dtype=np.float32)
    c = {
        "c_tri": tri.reshape(NPAT * BT, 128).astype(BFD),
        "c_mre": mre.reshape(NPAT * 128, BT).astype(BFD),
        "c_id_bf": ident.astype(BFD),
        "c_id_f32": ident.copy(),
        "c_id_f8": ident.astype(F8D),
        "c_ones_bf": np.ones((1, BT), dtype=np.float32).astype(BFD),
        "c_ones_row": np.ones((1, 128), dtype=np.float32),
        "c_ones_p": np.ones((128, 1), dtype=np.float32),
        "c_ones_sq": np.ones((128, 128), dtype=np.float32),
        "c_bias": np.tile(np.array([[1e-5, 1.0, -DT_C / DT_R]], dtype=np.float32), (128, 1)),
    }
    return blocks, c


def build_nc(debug=False, nblk=NBLK, stage=9):
    blocks, _ = build_consts()
    blocks = blocks[:nblk]
    nc = bass.Bass("TRN2", target_bir_lowering=False)

    x_h = nc.dram_tensor("x", [L, D], F32, kind="ExternalInput").ap()
    wdt_h = nc.dram_tensor("W_dtT", [D, D], F32, kind="ExternalInput").ap()
    bdt_h = nc.dram_tensor("b_dt", [1, D], F32, kind="ExternalInput").ap()
    wb_h = nc.dram_tensor("W_BT", [D, N], F32, kind="ExternalInput").ap()
    wc_h = nc.dram_tensor("W_CT", [D, N], F32, kind="ExternalInput").ap()
    alog_h = nc.dram_tensor("A_logT", [N, D], F32, kind="ExternalInput").ap()
    dpar_h = nc.dram_tensor("D_param", [1, D], F32, kind="ExternalInput").ap()
    wf_h = nc.dram_tensor("W_ffnT", [D, D], F32, kind="ExternalInput").ap()
    bf_h = nc.dram_tensor("b_ffn", [1, D], F32, kind="ExternalInput").ap()
    tri_h = nc.dram_tensor("c_tri", [NPAT * BT, 128], BF16, kind="ExternalInput").ap()
    mre_h = nc.dram_tensor("c_mre", [NPAT * 128, BT], BF16, kind="ExternalInput").ap()
    idb_h = nc.dram_tensor("c_id_bf", [128, 128], BF16, kind="ExternalInput").ap()
    idf_h = nc.dram_tensor("c_id_f32", [128, 128], F32, kind="ExternalInput").ap()
    id8_h = nc.dram_tensor("c_id_f8", [128, 128], FP8, kind="ExternalInput").ap()
    onesb_h = nc.dram_tensor("c_ones_bf", [1, BT], BF16, kind="ExternalInput").ap()
    onesr_h = nc.dram_tensor("c_ones_row", [1, 128], F32, kind="ExternalInput").ap()
    onesp_h = nc.dram_tensor("c_ones_p", [128, 1], F32, kind="ExternalInput").ap()
    bias_h = nc.dram_tensor("c_bias", [128, 3], F32, kind="ExternalInput").ap()
    onessq_h = nc.dram_tensor("c_ones_sq", [128, 128], F32, kind="ExternalInput").ap()
    a16_d3 = nc.dram_tensor("a16_scratch3", [128, 128], BF16, kind="Internal").ap()
    chE_d = nc.dram_tensor("chE_scratch", [RMAX, DN], BF16, kind="Internal").ap()
    chS_d = nc.dram_tensor("chS_scratch", [RMAX, DN], BF16, kind="Internal").ap()
    h0_d = nc.dram_tensor("h0_scratch", [RMAX, DN], BF16, kind="Internal").ap()
    xq_d = nc.dram_tensor("xq_scratch", [128, NBLK * 8 * BT], FP8, kind="Internal").ap()

    out_h = nc.dram_tensor("out", [L, D], F32, kind="ExternalOutput").ap()
    dbg = {}
    if debug:
        for nm, shp, dt_ in [("dbg_E0", [128, DN], F32),
                             ("dbg_S0", [128, DN], F32), ("dbg_y0", [BT, D], F32),
                             ("dbg_h0s", [128, RMAX * 128], F32),
                             ("dbg_sfull", [BT, 1], F32), ("dbg_bp", [128, NBLK * N], F32),
                             ("dbg_cp", [128, NBLK * N], F32), ("dbg_xn0", [BT, D], F32),
                             ("dbg_a16", [128, DN], F32)]:
            dbg[nm] = nc.dram_tensor(nm, shp, dt_, kind="ExternalOutput").ap()

    es = contextlib.ExitStack()

    def sb(name, shape, dtype):
        return es.enter_context(nc.sbuf_tensor(name, shape, dtype))

    def ps(name, shape, dtype):
        return es.enter_context(nc.psum_tensor(name, shape, dtype))

    dt2 = sb("dt2", [128, 2 * D], BF16)
    wdtT = sb("wdtT", [128, 8 * D], BF16)
    xnT_t = sb("xnT_t", [128, 8 * BT], BF16)
    ep_e = sb("ep_e", [128, 2 * SL], F32)
    wbT = sb("wbT", [128, 8 * N], BF16)
    wcT = sb("wcT", [128, 8 * N], BF16)
    bdt_row = sb("bdt_row", [1, D], BF16)
    xqs = [sb(f"xqs{i}", [128, 8 * BT], FP8) for i in range(2)]
    a16b = sb("a16b", [128, DN], BF16)
    s_blk = sb("s_blk", [128, DN], BF16)
    wqT = sb("wqT", [128, 8 * D], FP8)
    tri_c = sb("tri_c", [BT, NPAT * 128], BF16)
    mre_c = sb("mre_c", [128, NPAT * BT], BF16)
    bp_full = sb("bp_full", [128, NBLK * N], BF16)
    cp_full = sb("cp_full", [128, NBLK * N], F32)
    dbc = sb("dbc", [128, D], BF16)
    fbb = sb("fbb", [128, D], BF16)
    id_bf = sb("id_bf", [128, 128], BF16)
    id_f32 = sb("id_f32", [128, 128], F32)
    id_f8 = sb("id_f8", [128, 128], FP8)
    ones_bf = sb("ones_bf", [1, BT], BF16)
    ones_sq = sb("ones_sq", [128, 128], F32)
    bias_c = sb("bias_c", [128, 3], F32)
    chE = sb("chE", [128, RMAX * 128], BF16)
    chS = sb("chS", [128, RMAX * 128], BF16)
    h0st = sb("h0st", [128, RMAX * 128], BF16)
    h_t = sb("h_t", [128, 128], F32)
    ht_tmp = sb("ht_tmp", [128, 128], F32)
    ht_new = sb("ht_new", [128, 128], F32)
    sfull = [sb(f"sfull{i}", [BT, 1], F32) for i in range(2)]
    qcol = sb("qcol", [128, 1], F32)
    scol = sb("scol", [128, 1], F32)
    sm1 = sb("sm1", [128, 1], F32)
    x_t = sb("x_t", [128, D], F32)
    xn_t = sb("xn_t", [128, D], F32)
    w_t = sb("w_t", [128, D], BF16)
    mean_c = sb("mean_c", [128, 1], F32)
    sq_c = sb("sq_c", [128, 1], F32)
    var_c = sb("var_c", [128, 1], F32)
    rstd_c = sb("rstd_c", [128, 1], F32)
    garg = [sb(f"garg{i}", [BT, 2 * SL], BF16) for i in range(2)]
    g_t = [sb(f"g_t{i}", [BT, 2 * SL], BF16) for i in range(2)]
    en2_t = sb("en2_t", [BT, 2 * SL], BF16)
    r2_t = [sb(f"r2_t{i}", [BT, 2 * SL], BF16) for i in range(1)]
    ces_t = sb("ces_t", [BT, 4 * SL], BF16)
    yt_bf = sb("yt_bf", [BT, D], BF16)
    yt_blk = sb("yt_blk", [BT, D], F32)
    base_b = sb("base_b", [BT, D], F32)

    off0 = (nc.sbuf_base + 31) & ~31
    arena = sb("arena", [128, 16384], BF16)

    def at(name, shape, dtype, byte_off):
        return nc.alloc_sbuf_tensor_at(name, shape, dtype, offset=off0 + byte_off)

    e_blk = arena
    wslab = at("wslab", [128, D], F32, 16384)
    wslabB = at("wslabB", [128, D], F32, 20480)
    wsl = [wslab, wslabB]
    atile = at("atile", [128, 128], F32, 0)
    a16t = at("a16t", [128, 128], BF16, 512)

    p_ca = [ps(f"p_ca{i}", [128, 2 * SL], F32) for i in range(2)]
    p_sp = [ps(f"p_sp{i}", [128, SL], F32) for i in range(2)]
    p_ffn = [ps(f"p_ffn{i}", [128, SL], F32) for i in range(2)]
    p_trb = [p_ca[i][:].bitcast(BF16)[:, 0:128] for i in range(2)]
    p_tr8 = [p_ca[i][:].bitcast(FP8)[:, 512:768].rearrange("p (j t) -> p j t", t=2)[:, :, 0:1].squeeze(2) for i in range(2)]
    p_s = p_sp[0][:, 300:301]

    sem_d = es.enter_context(nc.semaphore("sem_d"))
    sem_g = es.enter_context(nc.semaphore("sem_g"))
    sem_p = es.enter_context(nc.semaphore("sem_p"))
    sem_t = es.enter_context(nc.semaphore("sem_t"))
    sem_v = es.enter_context(nc.semaphore("sem_v"))
    sem_s = es.enter_context(nc.semaphore("sem_s"))

    cnt = {"d": 0, "g": 0, "p": 0, "t": 0, "v": 0, "s": 0}
    sems = {"d": sem_d, "g": sem_g, "p": sem_p, "t": sem_t, "v": sem_v, "s": sem_s}
    prog = {"sync": [], "gpsimd": [], "tensor": [], "vector": [], "scalar": []}
    ENG_SEM = {"sync": "d", "gpsimd": "g", "tensor": "t", "vector": "v", "scalar": "s"}
    ENG_KEYS = {"sync": ("d",), "gpsimd": ("g", "p"), "tensor": ("t",),
                "vector": ("v",), "scalar": ("s",)}

    def emit(engine, fn, inc=1):
        if engine == "gpsimd":
            key = "g" if inc == 16 else "p"
        else:
            key = ENG_SEM[engine]
        befores = [(kk, cnt[kk]) for kk in ENG_KEYS[engine]]
        cnt[key] += inc
        sem = sems[key]
        if engine != "tensor":
            for kk, before in befores:
                if before > 0:
                    def wthunk(eng, sem=sems[kk], before=before):
                        eng.wait_ge(sem, before)
                    prog[engine].append(wthunk)

        def thunk(eng, fn=fn, sem=sem, inc=inc):
            fn(eng).then_inc(sem, inc)
        prog[engine].append(thunk)
        return (key, cnt[key])

    def wait(engine, ev):
        if ev is None:
            return
        key, val = ev
        if key in ENG_KEYS[engine]:
            return

        def thunk(eng, key=key, val=val):
            eng.wait_ge(sems[key], val)
        prog[engine].append(thunk)

    def waits(engine, evs):
        best = {}
        for ev in evs:
            if ev is None:
                continue
            kk, vv = ev
            if kk not in ENG_KEYS[engine] and best.get(kk, -1) < vv:
                best[kk] = vv
        for kk, vv in best.items():
            wait(engine, (kk, vv))

    def snapshot():
        return dict(cnt)

    def barrier(snap):
        evs = [(kk, vv) for kk, vv in snap.items()]
        for eng in ("tensor", "vector", "scalar", "gpsimd", "sync"):
            waits(eng, evs)

    ev_pre = []
    for dst, src in [(id_bf[:], idb_h), (id_f32[:], idf_h), (id_f8[:], id8_h),
                     (ones_bf[:], onesb_h),
                     (bias_c[:], bias_h), (ones_sq[:], onessq_h),
                     (atile[:], alog_h.rearrange("n d -> (n d)").rearrange("(p q) -> p q", q=128))]:
        ev_pre.append(emit("sync", lambda e, d=dst, s=src: e.dma_start(out=d, in_=s), inc=16))
    for p in range(NPAT):
        ev_pre.append(emit("sync", lambda e, p=p: e.dma_start(
            out=tri_c[:, p * 128:(p + 1) * 128], in_=tri_h[p * BT:(p + 1) * BT, :]), inc=16))
        ev_pre.append(emit("sync", lambda e, p=p: e.dma_start(
            out=mre_c[:, p * BT:(p + 1) * BT], in_=mre_h[p * 128:(p + 1) * 128, :]), inc=16))

    nc.const_aps.register(F32, 1e-5, bias_c[:, 0:1])
    nc.const_aps.register(F32, 1.0, bias_c[:, 1:2])
    nc.const_aps.register(F32, -DT_C / DT_R, bias_c[:, 2:3])
    waits("scalar", ev_pre)
    emit("scalar", lambda e: e.activation(out=atile[:], in_=atile[:], func=AF.Tanh, scale=0.1))
    ev = emit("scalar", lambda e: e.activation(out=atile[:], in_=atile[:], func=AF.Exp, scale=10.0))
    wait("vector", ev)
    ev_a16 = emit("vector", lambda e: e.tensor_scalar(
        out=a16t[:], in0=atile[:], scalar1=-1.0 / 16.0, scalar2=None, op0=ALU.mult))
    wait("gpsimd", ev_a16)
    ev_a16d = emit("gpsimd", lambda e: e.dma_start(out=a16_d3, in_=a16t[:]), inc=16)
    wait("gpsimd", ev_a16d)
    ev_a16b = emit("gpsimd", lambda e: e.dma_start(
        out=a16b[:],
        in_=a16_d3.rearrange("p q -> (p q)").unsqueeze(0).partition_broadcast(128).squeeze(1)), inc=16)
    waits("gpsimd", ev_pre)
    ev_dbc = emit("gpsimd", lambda e: e.dma_start(
        out=dbc[:], in_=dpar_h.partition_broadcast(128).squeeze(1)), inc=16)
    ev_fbb = emit("gpsimd", lambda e: e.dma_start(
        out=fbb[:], in_=bf_h.partition_broadcast(128).squeeze(1)), inc=16)

    ev_bd = emit("sync", lambda e: e.dma_start(out=x_t[0:1, :], in_=bdt_h), inc=16)
    wait("vector", ev_bd)
    ev_bdt = emit("vector", lambda e: e.tensor_copy(out=bdt_row[:], in_=x_t[0:1, :]))

    if stage >= 2:
        waits("tensor", ev_pre)
        ev_cps = [ev_bdt, ev_bdt]
        for i in range(8):
            waits("sync", [ev_cps[i % 2]])
            ev1 = emit("sync", lambda e, i=i: e.dma_start(out=wsl[i % 2][:], in_=wdt_h[i * 128:(i + 1) * 128, :]), inc=16)
            wait("vector", ev1)
            ev_cps[i % 2] = emit("vector", lambda e, i=i: e.tensor_copy(
                out=wdtT[:, i * D:(i + 1) * D], in_=wsl[i % 2][:]))
        for i in range(8):
            waits("sync", [ev_cps[i % 2]])
            ev1 = emit("sync", lambda e, i=i: e.dma_start(out=wsl[i % 2][:], in_=wf_h[i * 128:(i + 1) * 128, :]), inc=16)
            wait("scalar", ev1)
            ev_cps[i % 2] = emit("scalar", lambda e, i=i: e.copy(
                out=wqT[:, i * D:(i + 1) * D], in_=wsl[i % 2][:]))
        waits("sync", [ev_cps[0], ev_cps[1]])
        ev1 = emit("sync", lambda e: e.dma_start(
            out=wslab[:, 0:8 * N].rearrange("p (i n) -> p i n", n=N),
            in_=wb_h.rearrange("(i p) n -> p i n", p=128)), inc=16)
        ev1b = emit("sync", lambda e: e.dma_start(
            out=wslab[:, 8 * N:16 * N].rearrange("p (i n) -> p i n", n=N),
            in_=wc_h.rearrange("(i p) n -> p i n", p=128)), inc=16)
        waits("vector", [ev1, ev1b])
        emit("vector", lambda e: e.tensor_copy(out=wbT[:], in_=wslab[:, 0:8 * N]))
        ev_cp_prev = emit("vector", lambda e: e.tensor_copy(out=wcT[:], in_=wslab[:, 8 * N:16 * N]))

    emit("vector", lambda e: e.memset(h_t[:], 0.0))
    emit("vector", lambda e: e.memset(h0st[:], 0.0))
    emit("vector", lambda e: e.memset(s_blk[:], 0.0))
    emit("vector", lambda e: e.memset(chE[:], 0.0))
    emit("vector", lambda e: e.memset(chS[:], 0.0))
    if debug:
        emit("vector", lambda e: e.memset(dt2[:], 0.0))
        emit("vector", lambda e: e.memset(bp_full[:], 0.0))
        emit("vector", lambda e: e.memset(cp_full[:], 0.0))
        emit("vector", lambda e: e.memset(sfull[0][:], 0.0))
        emit("vector", lambda e: e.memset(sfull[1][:], 0.0))
        emit("vector", lambda e: e.memset(yt_blk[:], 0.0))
        emit("vector", lambda e: e.memset(xn_t[:], 0.0))
        barrier(snapshot())
    if debug:
        waits("gpsimd", [ev_a16b])
        emit("gpsimd", lambda e: e.dma_start(out=dbg["dbg_a16"], in_=a16b[:]), inc=16)

    a_sq = [None] * NBLK
    a_v2 = [None] * NBLK
    a_rstd = [None] * NBLK
    a_xn = [None] * NBLK
    a_trlast = [None] * NBLK
    a_xnTl = [None] * NBLK
    a_cp = [None] * NBLK
    a_e1 = [[None, None] for _ in range(NBLK)]
    a_e1s = [None] * NBLK
    a_dtf = [None] * NBLK
    NG = NBLK * NSLC
    ev_g = [None] * NG
    ev_garg = [None] * NG
    ev_ca = [None] * NG
    ev_E = [None] * NG
    ev_en = [None] * NG
    ev_r = [None] * NG
    ev_smm = [None] * NG
    ev_evac = [None] * NG
    ev_re = [None] * NG
    ev_stt = [None] * NG
    ev_yadd = [None] * NG
    ev_sq_b = [None] * NBLK
    ev_v2_b = [None] * NBLK
    ev_sr_b = [None] * NBLK
    ev_xn_b = [None] * NBLK
    ev_w_b = [None] * NBLK
    ev_ext_b = [None] * NBLK
    ev_chain_b = [None] * NBLK
    ev_wb_b = [None] * NBLK
    ev_basex_b = [None] * NBLK
    ev_baseff_b = [None] * NBLK
    ev_store_b = [None] * NBLK
    ev_hcarry_b = [None] * NBLK
    ev_ffn_b = [None] * NBLK

    def pe(lst, i):
        return lst[i] if i >= 0 else None

    base2 = [yt_blk, base_b]

    def emit_head(k):
        t0, bt, regions = blocks[k]
        kb = k % 2
        waits("sync", [pe(a_sq, k - 1), pe(a_xn, k - 1), pe(ev_basex_b, k - 1), ev_bdt])
        evx = emit("sync", lambda e, t0=t0, bt=bt: e.dma_start(
            out=x_t[0:bt, :], in_=x_h[t0:t0 + bt, :]), inc=16)
        wait("vector", evx)
        emit("vector", lambda e, bt=bt: e.tensor_reduce(
            out=mean_c[0:bt, :], in_=x_t[0:bt, :], axis=AX.X, op=ALU.add))
        waits("scalar", [evx, pe(a_v2, k - 1), pe(ev_r, k * NSLC - 1)])
        ev_sq_b[k] = a_sq[k] = emit("scalar", lambda e, bt=bt: e.activation(
            out=w_t[0:bt, :], in_=x_t[0:bt, :], func=AF.Square, accum_out=sq_c[0:bt, :]))
        waits("vector", [a_sq[k], pe(a_rstd, k - 1)])
        emit("vector", lambda e, bt=bt: e.tensor_scalar(
            out=mean_c[0:bt, :], in0=mean_c[0:bt, :], scalar1=1.0 / D, scalar2=None, op0=ALU.mult))
        emit("vector", lambda e, bt=bt: e.tensor_tensor(
            out=var_c[0:bt, :], in0=mean_c[0:bt, :], in1=mean_c[0:bt, :], op=ALU.mult))
        a_v2[k] = emit("vector", lambda e, bt=bt: e.scalar_tensor_tensor(
            out=var_c[0:bt, :], in0=sq_c[0:bt, :], scalar=1.0 / D, in1=var_c[0:bt, :],
            op0=ALU.mult, op1=ALU.subtract))
        waits("scalar", [a_v2[k], pe(a_xn, k - 1)])
        emit("scalar", lambda e, bt=bt: e.activation(
            out=rstd_c[0:bt, :], in_=var_c[0:bt, :], func=AF.Ln, bias=1e-5))
        a_rstd[k] = emit("scalar", lambda e, bt=bt: e.activation(
            out=rstd_c[0:bt, :], in_=rstd_c[0:bt, :], func=AF.Exp, scale=-0.5))
        waits("vector", [a_rstd[k], pe(a_trlast, k - 1)])
        a_xn[k] = emit("vector", lambda e, bt=bt: e.tensor_scalar(
            out=xn_t[0:bt, :], in0=x_t[0:bt, :], scalar1=mean_c[0:bt, :], scalar2=rstd_c[0:bt, :],
            op0=ALU.subtract, op1=ALU.mult))
        ev_tr = [None] * 8
        ev_evv = [None] * 8
        ev_evs = [None] * 8
        for j in range(9):
            if j < 8:
                waits("tensor", [a_xn[k], pe(a_cp, k - 1),
                                 pe(ev_en, k * NSLC - 1) if j < 2 else None,
                                 ev_evs[j - 2] if j >= 2 else None])
                ev_tr[j] = emit("tensor", lambda e, j=j, bt=bt: e.transpose(
                    out=p_ca[j % 2][:, 0:bt], in_=xn_t[0:bt, j * 128:(j + 1) * 128],
                    identity=id_f32[0:bt, 0:bt]))
            if j >= 1:
                jj = j - 1
                waits("scalar", [ev_tr[jj], pe(ev_ffn_b, k - 2) if jj == 0 else None])
                ev_evv[jj] = emit("scalar", lambda e, jj=jj, bt=bt: e.copy(
                    out=xnT_t[:, jj * BT: jj * BT + bt], in_=p_ca[jj % 2][:, 0:bt]))
                ev_evs[jj] = emit("scalar", lambda e, jj=jj, bt=bt, kb=kb: e.copy(
                    out=xqs[kb][:, jj * BT: jj * BT + bt], in_=p_ca[jj % 2][:, 0:bt]))
        a_trlast[k] = ev_tr[7]
        a_xnTl[k] = ev_evv[7]
        ev_mmd = [None, None]
        for h in range(2):
            waits("tensor", [a_xnTl[k],
                             pe(ev_stt, (k - 1) * NSLC + NSLC - 1) if k >= 1 else None,
                             pe(ev_baseff_b, k - 1), pe(a_e1s, k - 1)])
            for j in range(8):
                emit("tensor", lambda e, j=j, h=h, bt=bt: e.matmul(
                    out=p_ffn[h][0:bt, :], lhsT=xnT_t[:, j * BT: j * BT + bt],
                    rhs=wdtT[:, j * D + h * SL: j * D + h * SL + SL],
                    start=(j == 0), stop=False))
            ev_mmd[h] = emit("tensor", lambda e, h=h, bt=bt: e.matmul(
                out=p_ffn[h][0:bt, :], lhsT=ones_bf[:, 0:bt],
                rhs=bdt_row[:, h * SL: h * SL + SL], start=False, stop=True))
        for h in range(2):
            waits("scalar", [ev_mmd[h], pe(a_dtf, k - 1)])
            a_e1[k][h] = emit("scalar", lambda e, h=h, bt=bt: e.activation(
                out=ep_e[0:bt, h * SL:(h + 1) * SL], in_=p_ffn[h][0:bt, :], func=AF.Exp))
        a_e1s[k] = a_e1[k][1]
        for h in range(2):
            emit("scalar", lambda e, h=h, bt=bt: e.activation(
                out=ep_e[0:bt, h * SL:(h + 1) * SL], in_=ep_e[0:bt, h * SL:(h + 1) * SL],
                func=AF.Ln, bias=1.0))
        ev_sg = [None, None]
        for h in range(2):
            ev_sg[h] = emit("scalar", lambda e, h=h, bt=bt: e.activation(
                out=ep_e[0:bt, h * SL:(h + 1) * SL], in_=ep_e[0:bt, h * SL:(h + 1) * SL],
                func=AF.Sigmoid, scale=1.0 / DT_R, bias=-DT_C / DT_R))
        for h in range(2):
            wait("vector", ev_sg[h])
            a_dtf[k] = emit("vector", lambda e, h=h, kb=kb, bt=bt: e.tensor_scalar(
                out=dt2[0:bt, kb * D + h * SL: kb * D + h * SL + SL],
                in0=ep_e[0:bt, h * SL:(h + 1) * SL],
                scalar1=4.0 * DT_R, scalar2=DT_C - 2.0 * DT_R, op0=ALU.mult, op1=ALU.add))
        waits("tensor", [a_xnTl[k], ev_evs[6], ev_evs[7]])
        for j in range(8):
            emit("tensor", lambda e, j=j, bt=bt: e.matmul(
                out=p_ca[0][0:bt, 0:N], lhsT=xnT_t[:, j * BT: j * BT + bt],
                rhs=wbT[:, j * N:(j + 1) * N], start=(j == 0), stop=(j == 7)))
        for j in range(8):
            ev_mmc = emit("tensor", lambda e, j=j, bt=bt: e.matmul(
                out=p_ca[0][0:bt, N:2 * N], lhsT=xnT_t[:, j * BT: j * BT + bt],
                rhs=wcT[:, j * N:(j + 1) * N], start=(j == 0), stop=(j == 7)))
        wait("vector", ev_mmc)
        emit("vector", lambda e, k=k, bt=bt: e.tensor_copy(
            out=bp_full[0:bt, k * N:(k + 1) * N], in_=p_ca[0][0:bt, 0:N]))
        a_cp[k] = emit("vector", lambda e, k=k, bt=bt: e.tensor_copy(
            out=cp_full[0:bt, k * N:(k + 1) * N], in_=p_ca[0][0:bt, N:2 * N]))
        ev_w_b[k] = emit("vector", lambda e, kb=kb, bt=bt: e.tensor_tensor(
            out=w_t[0:bt, :], in0=xn_t[0:bt, :], in1=dt2[0:bt, kb * D:(kb + 1) * D], op=ALU.mult))
        emit("vector", lambda e, kb=kb, bt=bt: e.memset(sfull[kb][0:bt, :], 1.0))
        ev_f = None
        waits("tensor", [a_e1[k][0], a_e1[k][1]])
        for h in range(2):
            for j in range(8):
                ev_f = emit("tensor", lambda e, j=j, h=h, bt=bt, kb=kb: e.matmul(
                    out=p_ffn[h][0:bt, :], lhsT=xqs[kb][:, j * BT: j * BT + bt],
                    rhs=wqT[:, j * D + h * SL: j * D + h * SL + SL],
                    start=(j == 0), stop=(j == 7)))
        ev_ffn_b[k] = ev_f
        bb = base2[kb]
        waits("vector", [pe(ev_store_b, k - 2)])
        emit("vector", lambda e, bt=bt, bb=bb: e.tensor_tensor(
            out=bb[0:bt, :], in0=xn_t[0:bt, :], in1=dbc[0:bt, :], op=ALU.mult))
        ev_basex_b[k] = emit("vector", lambda e, bt=bt, bb=bb: e.tensor_tensor(
            out=bb[0:bt, :], in0=bb[0:bt, :], in1=x_t[0:bt, :], op=ALU.add))
        waits("vector", [ev_ffn_b[k], ev_fbb])
        for h in range(2):
            emit("vector", lambda e, h=h, bt=bt, bb=bb: e.tensor_tensor(
                out=bb[0:bt, h * SL:(h + 1) * SL], in0=bb[0:bt, h * SL:(h + 1) * SL],
                in1=p_ffn[h][0:bt, :], op=ALU.add))
        ev_baseff_b[k] = emit("vector", lambda e, bt=bt, bb=bb: e.tensor_tensor(
            out=bb[0:bt, :], in0=bb[0:bt, :], in1=fbb[0:bt, :], op=ALU.add))

    def emit_2a_stage(k, i):
        t0, bt, regions = blocks[k]
        pat = pat_of(k)
        g0 = k * NSLC
        if i < NSLC and i % 2 == 0:
            s0, s1 = i, i + 1
            gs0 = g0 + s0
            pb = (s0 >> 1) % 2
            waits("vector", [pe(ev_g, gs0 - 4)])
            ev_garg[gs0] = ev_garg[gs0 + 1] = emit("vector", lambda e, kb=k % 2, bt=bt, s0=s0, pb=pb: e.tensor_tensor(
                out=garg[pb][0:bt, :], in0=dt2[0:bt, kb * D: kb * D + 2 * SL],
                in1=a16b[0:bt, s0 * SL:(s0 + 2) * SL], op=ALU.mult))
        if 0 <= i - 1 < NSLC and (i - 1) % 2 == 0:
            s0 = i - 1
            gs0 = g0 + s0
            pb = (s0 >> 1) % 2
            waits("scalar", [ev_garg[gs0 + 1], pe(ev_ca, gs0 - 3), ev_w_b[k]])
            ev_g[gs0] = ev_g[gs0 + 1] = emit("scalar", lambda e, bt=bt, pb=pb: e.activation(
                out=g_t[pb][0:bt, :], in_=garg[pb][0:bt, :], func=AF.Tanh))
        if 0 <= i - 1 < NSLC:
            s = i - 1
            gs = g0 + s
            waits("tensor", [ev_g[gs], pe(ev_en, gs - 4), a_cp[k] if s < 4 else None])
            ev_ca[gs] = emit("tensor", lambda e, pat=pat, bt=bt, s=s: e.matmul(
                out=p_ca[(s >> 1) % 2][:, (s & 1) * SL:((s & 1) + 1) * SL],
                lhsT=tri_c[0:bt, pat * 128:(pat + 1) * 128],
                rhs=g_t[(s >> 1) % 2][0:bt, (s & 1) * SL:(s & 1) * SL + SL],
                start=True, stop=True))
        if 0 <= i - 2 < NSLC and (i - 2) % 2 == 1:
            s1 = i - 2
            s0 = s1 - 1
            gs1 = g0 + s1
            gs0 = g0 + s0
            pb = (s0 >> 1) % 2
            waits("scalar", [ev_ca[gs1], pe(ev_stt, gs0 - NSLC), pe(ev_stt, gs1 - NSLC), pe(ev_ext_b, k - 1)])
            ev_E[gs0] = ev_E[gs1] = emit("scalar", lambda e, s0=s0, pb=pb: e.activation(
                out=e_blk[:, s0 * SL:(s0 + 2) * SL], in_=p_ca[pb][:], func=AF.Exp, scale=16.0))
            waits("scalar", [pe(ev_r, gs0 - 2)])
            waits("scalar", [pe(ev_r, gs0 - 2)])
            ev_en[gs0] = ev_en[gs1] = emit("scalar", lambda e, bt=bt, pb=pb: e.activation(
                out=en2_t[0:bt, :], in_=p_ca[pb][0:bt, :], func=AF.Exp, scale=-16.0))
        if 0 <= i - 3 < NSLC and (i - 3) % 2 == 1:
            s1 = i - 3
            s0 = s1 - 1
            gs1 = g0 + s1
            gs0 = g0 + s0
            n_i = s0 >> 1
            waits("vector", [ev_en[gs1], pe(ev_smm, gs0 - 2), pe(ev_smm, gs1 - 2)])
            ev_r[gs0] = ev_r[gs1] = emit("vector", lambda e, k=k, bt=bt, n_i=n_i: e.scalar_tensor_tensor(
                out=r2_t[0][0:bt, :], in0=w_t[0:bt, :],
                scalar=bp_full[0:bt, k * N + n_i: k * N + n_i + 1],
                in1=en2_t[0:bt, :], op0=ALU.mult, op1=ALU.mult))
            for s in (s0, s1):
                gs = g0 + s
                waits("tensor", [ev_r[gs], pe(ev_evac, gs - 2)])
                ev_smm[gs] = emit("tensor", lambda e, pat=pat, bt=bt, b=s % 2, s=s: e.matmul(
                    out=p_sp[b][:], lhsT=tri_c[0:bt, pat * 128:(pat + 1) * 128],
                    rhs=r2_t[0][0:bt, (s & 1) * SL:((s & 1) + 1) * SL],
                    start=True, stop=True))
        if 0 <= i - 4 < NSLC:
            s = i - 4
            gs = g0 + s
            b = s % 2
            eng = "scalar" if (s % 2 == 0) else "vector"
            waits(eng, [ev_smm[gs], pe(ev_re, gs - NSLC), pe(ev_ext_b, k - 1)])
            if eng == "scalar":
                ev_evac[gs] = emit("scalar", lambda e, s=s, b=b: e.copy(
                    out=s_blk[:, s * SL:(s + 1) * SL], in_=p_sp[b][:]))
            else:
                ev_evac[gs] = emit("vector", lambda e, s=s, b=b: e.tensor_copy(
                    out=s_blk[:, s * SL:(s + 1) * SL], in_=p_sp[b][:]))

    def emit_2b_stage(k, i):
        t0, bt, regions = blocks[k]
        g0 = k * NSLC
        if i < NSLC:
            s = i
            gs = g0 + s
            waits("tensor", [ev_wb_b[k], ev_evac[gs], pe(ev_stt, gs - 2),
                             ev_baseff_b[k + 1] if k + 1 < NBLK and ev_baseff_b[k + 1] is not None else None])
            ev_re[gs] = emit("tensor", lambda e, k=k, bt=bt, s=s: e.matmul(
                out=p_ffn[s % 2][0:bt, :], lhsT=mre_c[:, pat_of(k) * BT: pat_of(k) * BT + bt],
                rhs=s_blk[:, s * SL:(s + 1) * SL], start=True, stop=True))
        if 0 <= i - 1 < NSLC:
            s = i - 1
            gs = g0 + s
            b = s % 2
            n_i = s >> 1
            dh = (s & 1) * SL
            waits("vector", [ev_re[gs]])
            if s < 2:
                ev_stt[gs] = emit("vector", lambda e, k=k, bt=bt, s=s, n_i=n_i, dh=dh, b=b: e.scalar_tensor_tensor(
                    out=yt_bf[0:bt, dh:dh + SL], in0=e_blk[0:bt, s * SL:(s + 1) * SL],
                    scalar=cp_full[0:bt, k * N + n_i: k * N + n_i + 1], in1=p_ffn[b][0:bt, :],
                    op0=ALU.mult, op1=ALU.mult))
                ev_yadd[gs] = ev_stt[gs]
            else:
                waits("vector", [pe(ev_yadd, gs - 4)])
                ev_stt[gs] = emit("vector", lambda e, k=k, bt=bt, s=s, n_i=n_i, cb=s % 4, b=b: e.scalar_tensor_tensor(
                    out=ces_t[0:bt, cb * SL:(cb + 1) * SL], in0=e_blk[0:bt, s * SL:(s + 1) * SL],
                    scalar=cp_full[0:bt, k * N + n_i: k * N + n_i + 1], in1=p_ffn[b][0:bt, :],
                    op0=ALU.mult, op1=ALU.mult))

                def _yadd(e, bt=bt, dh=dh, cb=s % 4):
                    with nc.allow_low_precision(reason="y partial sums kept in bf16; rvr-checked"):
                        return e.tensor_tensor(
                            out=yt_bf[0:bt, dh:dh + SL], in0=yt_bf[0:bt, dh:dh + SL],
                            in1=ces_t[0:bt, cb * SL:(cb + 1) * SL], op=ALU.add)
                ev_yadd[gs] = emit("vector", _yadd)

    def emit_finalize(k):
        t0, bt, regions = blocks[k]
        bb = base2[k % 2]
        waits("vector", [ev_yadd[k * NSLC + NSLC - 1]])
        ev_out = emit("vector", lambda e, k=k, bt=bt, bb=bb: e.scalar_tensor_tensor(
            out=bb[0:bt, :], in0=yt_bf[0:bt, :], scalar=sfull[k % 2][0:bt, :],
            in1=bb[0:bt, :], op0=ALU.mult, op1=ALU.add))
        wait("sync", ev_out)
        ev_store_b[k] = emit("sync", lambda e, t0=t0, bt=bt, bb=bb: e.dma_start(
            out=out_h[t0:t0 + bt, :], in_=bb[0:bt, :]), inc=16)

    def emit_chain(k):
        t0, bt, regions = blocks[k]
        nreg = len(regions)
        g0 = k * NSLC
        waits("gpsimd", [ev_E[g0 + NSLC - 1], ev_evac[g0 + NSLC - 1], ev_evac[g0 + NSLC - 2],
                         pe(ev_chain_b, k - 1)])
        emit("gpsimd", lambda e, nreg=nreg: e.dma_start(
            out=chE_d[0:nreg, :], in_=e_blk[114:114 + nreg, :]), inc=16)
        emit("gpsimd", lambda e, nreg=nreg: e.dma_start(
            out=chS_d[0:nreg, :], in_=s_blk[114:114 + nreg, :]), inc=16)
        emit("gpsimd", lambda e, nreg=nreg: e.dma_start(
            out=chE[:, 0:nreg * 128].rearrange("p (c j) -> p c j", j=128),
            in_=chE_d[0:nreg, :].rearrange("c (q j) -> q c j", j=128)), inc=16)
        ev_ext = [emit("gpsimd", lambda e, nreg=nreg: e.dma_start(
            out=chS[:, 0:nreg * 128].rearrange("p (c j) -> p c j", j=128),
            in_=chS_d[0:nreg, :].rearrange("c (q j) -> q c j", j=128)), inc=16)]
        ev_ext_b[k] = ev_ext[-1]
        waits("gpsimd", [pe(ev_hcarry_b, k - 1)])
        ev_h0w = [emit("gpsimd", lambda e: e.tensor_copy(out=h0st[:, 0:128], in_=h_t[:]))]
        waits("vector", ev_ext + [pe(ev_wb_b, k - 1)])
        emit("vector", lambda e: e.tensor_tensor(
            out=ht_tmp[:], in0=h_t[:], in1=chS[:, 0:128], op=ALU.add))
        pend_pool_h0 = None
        for c, (lo, hi, clamps) in enumerate(regions):
            last = (c == nreg - 1)
            waits("vector", [pend_pool_h0])
            pend_pool_h0 = None
            if clamps:
                emit("vector", lambda e, c=c: e.scalar_tensor_tensor(
                    out=ht_new[:], in0=chE[:, c * 128:(c + 1) * 128], scalar=1.0,
                    in1=ht_tmp[:], op0=ALU.mult, op1=ALU.mult))
                ev_q = emit("vector", lambda e: e.scalar_tensor_tensor(
                    out=ht_tmp[:], in0=ht_new[:], scalar=1.0, in1=ht_new[:],
                    op0=ALU.mult, op1=ALU.mult, accum_out=qcol[:]))
                wait("tensor", ev_q)
                ev_qs = emit("tensor", lambda e: e.matmul(
                    out=p_s, lhsT=ones_sq[:], rhs=qcol[:], start=True, stop=True))
                wait("scalar", ev_qs)
                ev_rt = emit("scalar", lambda e: e.activation(
                    out=scol[:], in_=p_s, func=AF.Sqrt))
                wait("vector", ev_rt)
                emit("vector", lambda e: e.reciprocal(out=scol[:], in_=scol[:]))
                emit("vector", lambda e: e.tensor_scalar(
                    out=scol[:], in0=scol[:], scalar1=20.0, scalar2=1.0, op0=ALU.mult, op1=ALU.min))
                if last:
                    waits("vector", [ev_h0w[0]])
                    ev_hcarry_b[k] = emit("vector", lambda e: e.tensor_scalar(
                        out=h_t[:], in0=ht_new[:], scalar1=scol[:], scalar2=None, op0=ALU.mult))
                else:
                    emit("vector", lambda e, c=c: e.scalar_tensor_tensor(
                        out=ht_tmp[:], in0=ht_new[:], scalar=scol[:],
                        in1=chS[:, (c + 1) * 128:(c + 2) * 128], op0=ALU.mult, op1=ALU.add))
                    ev_h0w.append(emit("vector", lambda e, c=c: e.tensor_scalar(
                        out=h0st[:, (c + 1) * 128:(c + 2) * 128], in0=ht_new[:],
                        scalar1=scol[:], scalar2=None, op0=ALU.mult)))
                emit("vector", lambda e: e.tensor_scalar(
                    out=sm1[:], in0=scol[:], scalar1=-1.0, scalar2=None, op0=ALU.add))
                ev_chain_b[k] = emit("vector", lambda e, hi=hi, bt=bt, k=k: e.scalar_tensor_tensor(
                    out=sfull[k % 2][0:bt, :], in0=id_f32[0:bt, hi:hi + 1], scalar=sm1[0:bt, :],
                    in1=sfull[k % 2][0:bt, :], op0=ALU.mult, op1=ALU.add))
            else:
                if last:
                    waits("vector", [ev_h0w[0]])
                    ev_chain_b[k] = ev_hcarry_b[k] = emit("vector", lambda e, c=c: e.tensor_tensor(
                        out=h_t[:], in0=chE[:, c * 128:(c + 1) * 128], in1=ht_tmp[:], op=ALU.mult))
                else:
                    ev_hn = emit("vector", lambda e, c=c: e.tensor_tensor(
                        out=ht_new[:], in0=chE[:, c * 128:(c + 1) * 128], in1=ht_tmp[:], op=ALU.mult))
                    ev_chain_b[k] = emit("vector", lambda e, c=c: e.tensor_tensor(
                        out=ht_tmp[:], in0=ht_new[:], in1=chS[:, (c + 1) * 128:(c + 2) * 128], op=ALU.add))
                    waits("gpsimd", [ev_chain_b[k]])
                    pend_pool_h0 = emit("gpsimd", lambda e, c=c: e.tensor_copy(
                        out=h0st[:, (c + 1) * 128:(c + 2) * 128], in_=ht_new[:]))
                    ev_h0w.append(pend_pool_h0)
        waits("gpsimd", ev_h0w + [pe(ev_hcarry_b, k)])
        emit("gpsimd", lambda e, nreg=nreg: e.dma_start(
            out=h0_d[0:nreg, :].rearrange("c (q j) -> q c j", j=128),
            in_=h0st[:, 0:nreg * 128].rearrange("p (c j) -> p c j", j=128)), inc=16)
        ev_wb_b[k] = emit("gpsimd", lambda e, nreg=nreg: e.dma_start(
            out=s_blk[114:114 + nreg, :], in_=h0_d[0:nreg, :]), inc=16)

    barrier(snapshot())
    NBL = len(blocks)
    if stage >= 4:
        emit_head(0)
        for k in range(NBL):
            for i in range(NSLC + 5):
                emit_2a_stage(k, i)
                if k >= 1 and stage >= 6:
                    emit_2b_stage(k - 1, i)
            if k >= 1 and stage >= 6:
                emit_finalize(k - 1)
            if k + 1 < NBL:
                emit_head(k + 1)
            if stage >= 5:
                emit_chain(k)
            if debug and k == 0:
                barrier(snapshot())
                emit("gpsimd", lambda e: e.dma_start(out=dbg["dbg_E0"], in_=e_blk[:]), inc=16)
                emit("gpsimd", lambda e: e.dma_start(out=dbg["dbg_S0"], in_=s_blk[:]), inc=16)
                emit("gpsimd", lambda e: e.dma_start(out=dbg["dbg_h0s"], in_=h0st[:]), inc=16)
                emit("gpsimd", lambda e: e.dma_start(out=dbg["dbg_xn0"], in_=xn_t[0:BT, :]), inc=16)
                barrier(snapshot())
        if stage >= 6:
            for i in range(NSLC + 2):
                emit_2b_stage(NBL - 1, i)
            emit_finalize(NBL - 1)
            if debug:
                barrier(snapshot())
                emit("gpsimd", lambda e: e.dma_start(out=dbg["dbg_y0"], in_=base2[(NBL - 1) % 2][0:BT, :]), inc=16)
                emit("gpsimd", lambda e, k=0: e.dma_start(out=dbg["dbg_sfull"], in_=sfull[0][:]), inc=16)
                barrier(snapshot())

    if debug:
        barrier(snapshot())
        emit("gpsimd", lambda e: e.dma_start(out=dbg["dbg_bp"], in_=bp_full[:]), inc=16)
        emit("gpsimd", lambda e: e.dma_start(out=dbg["dbg_cp"], in_=cp_full[:]), inc=16)

    def thunk_final_d(eng):
        eng.wait_ge(sem_d, cnt["d"])
    prog["sync"].append(thunk_final_d)

    def thunk_final_g(eng):
        eng.wait_ge(sem_g, cnt["g"])
        eng.wait_ge(sem_p, cnt["p"])
    prog["gpsimd"].append(thunk_final_g)

    with es, nc.Block() as block:
        @block.sync
        def _(eng):
            for f in prog["sync"]:
                f(eng)

        @block.gpsimd
        def _(eng):
            for f in prog["gpsimd"]:
                f(eng)

        @block.tensor
        def _(eng):
            for f in prog["tensor"]:
                f(eng)

        @block.vector
        def _(eng):
            for f in prog["vector"]:
                f(eng)

        @block.scalar
        def _(eng):
            for f in prog["scalar"]:
                f(eng)

    return nc


def kernel(L0_attn_x, L0_attn_W_dt, L0_attn_b_dt, L0_attn_W_B, L0_attn_W_C,
           L0_attn_A_log, L0_attn_D_param, L0_ffn_weight, L0_ffn_bias):
    _, consts = build_consts()
    nc = build_nc(debug=False)
    base = {
        "W_dtT": np.ascontiguousarray(np.asarray(L0_attn_W_dt, dtype=np.float32).T),
        "b_dt": np.ascontiguousarray(L0_attn_b_dt, dtype=np.float32).reshape(1, D),
        "W_BT": np.ascontiguousarray(np.asarray(L0_attn_W_B, dtype=np.float32).T),
        "W_CT": np.ascontiguousarray(np.asarray(L0_attn_W_C, dtype=np.float32).T),
        "A_logT": np.ascontiguousarray(L0_attn_A_log.T, dtype=np.float32),
        "D_param": np.ascontiguousarray(L0_attn_D_param, dtype=np.float32).reshape(1, D),
        "W_ffnT": np.ascontiguousarray(np.asarray(L0_ffn_weight, dtype=np.float32).T),
        "b_ffn": np.ascontiguousarray(L0_ffn_bias, dtype=np.float32).reshape(1, D),
    }
    base.update(consts)
    x = np.ascontiguousarray(L0_attn_x, dtype=np.float32)
    in_maps = []
    for i in range(8):
        m = dict(base)
        m["x"] = x[i]
        in_maps.append(m)
    res = run_bass_kernel_spmd(nc, in_maps, core_ids=list(range(8)))
    out = np.stack([np.asarray(r["out"]) for r in res.results], axis=0).astype(np.float32)
    return out


# version 17
# speedup vs baseline: 1.1669x; 1.0339x over previous
import contextlib
import numpy as np
import ml_dtypes
import concourse.bass as bass
import concourse.mybir as mybir
from concourse.bass_utils import run_bass_kernel_spmd

F32 = mybir.dt.float32
BF16 = mybir.dt.bfloat16
FP8 = mybir.dt.float8e4
AF = mybir.ActivationFunctionType
ALU = mybir.AluOpType
AX = mybir.AxisListType

BFD = ml_dtypes.bfloat16
F8D = ml_dtypes.float8_e4m3fn

L = 2048
D = 1024
N = 16
DN = D * N
BT = 114
NBLK = 18
NSLC = 32
SL = 512
RMAX = 13
NPAT = 6

DT_C = (0.1 + 0.001) / 2.0
DT_R = (0.1 - 0.001) / 2.0


def block_structure():
    blocks = []
    for k in range(NBLK):
        t0 = k * BT
        bt = min(BT, L - t0)
        regions = []
        lo = 0
        for r in range(1, bt + 1):
            t = t0 + r
            if r == bt or t % 10 == 1:
                hi = r - 1
                clamps = ((t0 + hi) % 10 == 0)
                regions.append((lo, hi, clamps))
                lo = r
        blocks.append((t0, bt, regions))
    return blocks


def pat_of(k):
    return 5 if k == NBLK - 1 else k % 5


def build_consts():
    blocks = block_structure()
    tri = np.zeros((NPAT, BT, 128), dtype=np.float32)
    mre = np.zeros((NPAT, 128, BT), dtype=np.float32)
    done = set()
    for k, (t0, bt, regions) in enumerate(blocks):
        p = pat_of(k)
        if p in done:
            continue
        done.add(p)
        for ci, (lo, hi, cl) in enumerate(regions):
            for tp in range(lo, hi + 1):
                tri[p, lo:tp + 1, tp] = 1.0
                tri[p, tp, 114 + ci] = 1.0
                mre[p, tp, tp] = 1.0
                mre[p, 114 + ci, tp] = 1.0
    ident = np.eye(128, dtype=np.float32)
    c = {
        "c_tri": tri.reshape(NPAT * BT, 128).astype(BFD),
        "c_mre": mre.reshape(NPAT * 128, BT).astype(BFD),
        "c_id_bf": ident.astype(BFD),
        "c_id_f32": ident.copy(),
        "c_id_f8": ident.astype(F8D),
        "c_ones_bf": np.ones((1, BT), dtype=np.float32).astype(BFD),
        "c_ones_row": np.ones((1, 128), dtype=np.float32),
        "c_ones_p": np.ones((128, 1), dtype=np.float32),
        "c_ones_sq": np.ones((128, 128), dtype=np.float32),
        "c_bias": np.tile(np.array([[1e-5, 1.0, -DT_C / DT_R]], dtype=np.float32), (128, 1)),
    }
    return blocks, c


def build_nc(debug=False, nblk=NBLK, stage=9):
    blocks, _ = build_consts()
    blocks = blocks[:nblk]
    nc = bass.Bass("TRN2", target_bir_lowering=False)

    x_h = nc.dram_tensor("x", [L, D], F32, kind="ExternalInput").ap()
    wdt_h = nc.dram_tensor("W_dtT", [D, D], F32, kind="ExternalInput").ap()
    bdt_h = nc.dram_tensor("b_dt", [1, D], F32, kind="ExternalInput").ap()
    wb_h = nc.dram_tensor("W_BT", [D, N], F32, kind="ExternalInput").ap()
    wc_h = nc.dram_tensor("W_CT", [D, N], F32, kind="ExternalInput").ap()
    alog_h = nc.dram_tensor("A_logT", [N, D], F32, kind="ExternalInput").ap()
    dpar_h = nc.dram_tensor("D_param", [1, D], F32, kind="ExternalInput").ap()
    wf_h = nc.dram_tensor("W_ffnT", [D, D], F32, kind="ExternalInput").ap()
    bf_h = nc.dram_tensor("b_ffn", [1, D], F32, kind="ExternalInput").ap()
    tri_h = nc.dram_tensor("c_tri", [NPAT * BT, 128], BF16, kind="ExternalInput").ap()
    mre_h = nc.dram_tensor("c_mre", [NPAT * 128, BT], BF16, kind="ExternalInput").ap()
    idb_h = nc.dram_tensor("c_id_bf", [128, 128], BF16, kind="ExternalInput").ap()
    idf_h = nc.dram_tensor("c_id_f32", [128, 128], F32, kind="ExternalInput").ap()
    id8_h = nc.dram_tensor("c_id_f8", [128, 128], FP8, kind="ExternalInput").ap()
    onesb_h = nc.dram_tensor("c_ones_bf", [1, BT], BF16, kind="ExternalInput").ap()
    onesr_h = nc.dram_tensor("c_ones_row", [1, 128], F32, kind="ExternalInput").ap()
    onesp_h = nc.dram_tensor("c_ones_p", [128, 1], F32, kind="ExternalInput").ap()
    bias_h = nc.dram_tensor("c_bias", [128, 3], F32, kind="ExternalInput").ap()
    onessq_h = nc.dram_tensor("c_ones_sq", [128, 128], F32, kind="ExternalInput").ap()
    a16_d3 = nc.dram_tensor("a16_scratch3", [128, 128], BF16, kind="Internal").ap()
    chE_d = nc.dram_tensor("chE_scratch", [RMAX, DN], BF16, kind="Internal").ap()
    chS_d = nc.dram_tensor("chS_scratch", [RMAX, DN], BF16, kind="Internal").ap()
    h0_d = nc.dram_tensor("h0_scratch", [RMAX, DN], BF16, kind="Internal").ap()
    xq_d = nc.dram_tensor("xq_scratch", [128, NBLK * 8 * BT], FP8, kind="Internal").ap()

    out_h = nc.dram_tensor("out", [L, D], F32, kind="ExternalOutput").ap()
    dbg = {}
    if debug:
        for nm, shp, dt_ in [("dbg_E0", [128, DN], F32),
                             ("dbg_S0", [128, DN], F32), ("dbg_y0", [BT, D], F32),
                             ("dbg_h0s", [128, RMAX * 128], F32),
                             ("dbg_sfull", [BT, 1], F32), ("dbg_bp", [128, NBLK * N], F32),
                             ("dbg_cp", [128, NBLK * N], F32), ("dbg_xn0", [BT, D], F32),
                             ("dbg_a16", [128, DN], F32)]:
            dbg[nm] = nc.dram_tensor(nm, shp, dt_, kind="ExternalOutput").ap()

    es = contextlib.ExitStack()

    def sb(name, shape, dtype):
        return es.enter_context(nc.sbuf_tensor(name, shape, dtype))

    def ps(name, shape, dtype):
        return es.enter_context(nc.psum_tensor(name, shape, dtype))

    dt2 = sb("dt2", [128, 2 * D], BF16)
    wdtT = sb("wdtT", [128, 8 * D], BF16)
    xnT_t = sb("xnT_t", [128, 8 * BT], BF16)
    ep_e = sb("ep_e", [128, 2 * SL], F32)
    wbT = sb("wbT", [128, 8 * N], BF16)
    wcT = sb("wcT", [128, 8 * N], BF16)
    bdt_row = sb("bdt_row", [1, D], BF16)
    xqs = [sb(f"xqs{i}", [128, 8 * BT], FP8) for i in range(2)]
    a16b = sb("a16b", [128, DN], BF16)
    s_blk = sb("s_blk", [128, DN], BF16)
    wqT = sb("wqT", [128, 8 * D], FP8)
    tri_c = sb("tri_c", [BT, NPAT * 128], BF16)
    mre_c = sb("mre_c", [128, NPAT * BT], BF16)
    bp_full = sb("bp_full", [128, NBLK * N], BF16)
    cp_full = sb("cp_full", [128, NBLK * N], F32)
    dbc = sb("dbc", [128, D], BF16)
    fbb = sb("fbb", [128, D], BF16)
    id_bf = sb("id_bf", [128, 128], BF16)
    id_f32 = sb("id_f32", [128, 128], F32)
    id_f8 = sb("id_f8", [128, 128], FP8)
    ones_bf = sb("ones_bf", [1, BT], BF16)
    ones_sq = sb("ones_sq", [128, 128], F32)
    bias_c = sb("bias_c", [128, 3], F32)
    chE = sb("chE", [128, RMAX * 128], BF16)
    chS = sb("chS", [128, RMAX * 128], BF16)
    h0st = sb("h0st", [128, RMAX * 128], BF16)
    h_t = sb("h_t", [128, 128], F32)
    ht_tmp = sb("ht_tmp", [128, 128], F32)
    ht_new = sb("ht_new", [128, 128], F32)
    sfull = [sb(f"sfull{i}", [BT, 1], F32) for i in range(2)]
    qcol = sb("qcol", [128, 1], F32)
    scol = sb("scol", [128, 1], F32)
    sm1 = sb("sm1", [128, 1], F32)
    x_t = sb("x_t", [128, D], F32)
    xn_t = sb("xn_t", [128, D], F32)
    w_t = sb("w_t", [128, D], BF16)
    mean_c = sb("mean_c", [128, 1], F32)
    sq_c = sb("sq_c", [128, 1], F32)
    var_c = sb("var_c", [128, 1], F32)
    rstd_c = sb("rstd_c", [128, 1], F32)
    garg = [sb(f"garg{i}", [BT, 2 * SL], BF16) for i in range(2)]
    g_t = [sb(f"g_t{i}", [BT, 2 * SL], BF16) for i in range(2)]
    en2_t = sb("en2_t", [BT, 2 * SL], BF16)
    r2_t = [sb(f"r2_t{i}", [BT, 2 * SL], BF16) for i in range(1)]
    ces_t = sb("ces_t", [BT, 4 * SL], BF16)
    yt_bf = sb("yt_bf", [BT, D], BF16)
    yt_blk = sb("yt_blk", [BT, D], F32)
    base_b = sb("base_b", [BT, D], F32)

    off0 = (nc.sbuf_base + 31) & ~31
    arena = sb("arena", [128, 16384], BF16)

    def at(name, shape, dtype, byte_off):
        return nc.alloc_sbuf_tensor_at(name, shape, dtype, offset=off0 + byte_off)

    e_blk = arena
    wslab = at("wslab", [128, D], F32, 16384)
    wslabB = at("wslabB", [128, D], F32, 20480)
    wsl = [wslab, wslabB]
    atile = at("atile", [128, 128], F32, 0)
    a16t = at("a16t", [128, 128], BF16, 512)

    p_ca = [ps(f"p_ca{i}", [128, 2 * SL], F32) for i in range(2)]
    p_sp = [ps(f"p_sp{i}", [128, SL], F32) for i in range(2)]
    p_ffn = [ps(f"p_ffn{i}", [128, SL], F32) for i in range(2)]
    p_trb = [p_ca[i][:].bitcast(BF16)[:, 0:128] for i in range(2)]
    p_tr8 = [p_ca[i][:].bitcast(FP8)[:, 512:768].rearrange("p (j t) -> p j t", t=2)[:, :, 0:1].squeeze(2) for i in range(2)]
    p_s = p_sp[0][:, 300:301]

    sem_d = es.enter_context(nc.semaphore("sem_d"))
    sem_g = es.enter_context(nc.semaphore("sem_g"))
    sem_p = es.enter_context(nc.semaphore("sem_p"))
    sem_t = es.enter_context(nc.semaphore("sem_t"))
    sem_v = es.enter_context(nc.semaphore("sem_v"))
    sem_s = es.enter_context(nc.semaphore("sem_s"))

    cnt = {"d": 0, "g": 0, "p": 0, "t": 0, "v": 0, "s": 0}
    sems = {"d": sem_d, "g": sem_g, "p": sem_p, "t": sem_t, "v": sem_v, "s": sem_s}
    prog = {"sync": [], "gpsimd": [], "tensor": [], "vector": [], "scalar": []}
    ENG_SEM = {"sync": "d", "gpsimd": "g", "tensor": "t", "vector": "v", "scalar": "s"}
    ENG_KEYS = {"sync": ("d",), "gpsimd": ("g", "p"), "tensor": ("t",),
                "vector": ("v",), "scalar": ("s",)}

    def emit(engine, fn, inc=1):
        if engine == "gpsimd":
            key = "g" if inc == 16 else "p"
        else:
            key = ENG_SEM[engine]
        befores = [(kk, cnt[kk]) for kk in ENG_KEYS[engine]]
        cnt[key] += inc
        sem = sems[key]
        if engine != "tensor":
            for kk, before in befores:
                if before > 0:
                    def wthunk(eng, sem=sems[kk], before=before):
                        eng.wait_ge(sem, before)
                    prog[engine].append(wthunk)

        def thunk(eng, fn=fn, sem=sem, inc=inc):
            fn(eng).then_inc(sem, inc)
        prog[engine].append(thunk)
        return (key, cnt[key])

    def wait(engine, ev):
        if ev is None:
            return
        key, val = ev
        if key in ENG_KEYS[engine]:
            return

        def thunk(eng, key=key, val=val):
            eng.wait_ge(sems[key], val)
        prog[engine].append(thunk)

    def waits(engine, evs):
        best = {}
        for ev in evs:
            if ev is None:
                continue
            kk, vv = ev
            if kk not in ENG_KEYS[engine] and best.get(kk, -1) < vv:
                best[kk] = vv
        for kk, vv in best.items():
            wait(engine, (kk, vv))

    def snapshot():
        return dict(cnt)

    def barrier(snap):
        evs = [(kk, vv) for kk, vv in snap.items()]
        for eng in ("tensor", "vector", "scalar", "gpsimd", "sync"):
            waits(eng, evs)

    ev_pre = []
    for dst, src in [(id_bf[:], idb_h), (id_f32[:], idf_h), (id_f8[:], id8_h),
                     (ones_bf[:], onesb_h),
                     (bias_c[:], bias_h), (ones_sq[:], onessq_h),
                     (atile[:], alog_h.rearrange("n d -> (n d)").rearrange("(p q) -> p q", q=128))]:
        ev_pre.append(emit("sync", lambda e, d=dst, s=src: e.dma_start(out=d, in_=s), inc=16))
    for p in range(NPAT):
        ev_pre.append(emit("sync", lambda e, p=p: e.dma_start(
            out=tri_c[:, p * 128:(p + 1) * 128], in_=tri_h[p * BT:(p + 1) * BT, :]), inc=16))
        ev_pre.append(emit("sync", lambda e, p=p: e.dma_start(
            out=mre_c[:, p * BT:(p + 1) * BT], in_=mre_h[p * 128:(p + 1) * 128, :]), inc=16))

    nc.const_aps.register(F32, 1e-5, bias_c[:, 0:1])
    nc.const_aps.register(F32, 1.0, bias_c[:, 1:2])
    nc.const_aps.register(F32, -DT_C / DT_R, bias_c[:, 2:3])
    waits("scalar", ev_pre)
    emit("scalar", lambda e: e.activation(out=atile[:], in_=atile[:], func=AF.Tanh, scale=0.1))
    ev = emit("scalar", lambda e: e.activation(out=atile[:], in_=atile[:], func=AF.Exp, scale=10.0))
    wait("vector", ev)
    ev_a16 = emit("vector", lambda e: e.tensor_scalar(
        out=a16t[:], in0=atile[:], scalar1=-1.0 / 16.0, scalar2=None, op0=ALU.mult))
    wait("gpsimd", ev_a16)
    ev_a16d = emit("gpsimd", lambda e: e.dma_start(out=a16_d3, in_=a16t[:]), inc=16)
    wait("gpsimd", ev_a16d)
    ev_a16b = emit("gpsimd", lambda e: e.dma_start(
        out=a16b[:],
        in_=a16_d3.rearrange("p q -> (p q)").unsqueeze(0).partition_broadcast(128).squeeze(1)), inc=16)
    waits("gpsimd", ev_pre)
    ev_dbc = emit("gpsimd", lambda e: e.dma_start(
        out=dbc[:], in_=dpar_h.partition_broadcast(128).squeeze(1)), inc=16)
    ev_fbb = emit("gpsimd", lambda e: e.dma_start(
        out=fbb[:], in_=bf_h.partition_broadcast(128).squeeze(1)), inc=16)

    ev_bd = emit("sync", lambda e: e.dma_start(out=x_t[0:1, :], in_=bdt_h), inc=16)
    wait("vector", ev_bd)
    ev_bdt = emit("vector", lambda e: e.tensor_copy(out=bdt_row[:], in_=x_t[0:1, :]))

    if stage >= 2:
        waits("tensor", ev_pre)
        ev_cps = [ev_bdt, ev_bdt]
        for i in range(8):
            waits("sync", [ev_cps[i % 2]])
            ev1 = emit("sync", lambda e, i=i: e.dma_start(out=wsl[i % 2][:], in_=wdt_h[i * 128:(i + 1) * 128, :]), inc=16)
            wait("vector", ev1)
            ev_cps[i % 2] = emit("vector", lambda e, i=i: e.tensor_copy(
                out=wdtT[:, i * D:(i + 1) * D], in_=wsl[i % 2][:]))
        for i in range(8):
            waits("sync", [ev_cps[i % 2]])
            ev1 = emit("sync", lambda e, i=i: e.dma_start(out=wsl[i % 2][:], in_=wf_h[i * 128:(i + 1) * 128, :]), inc=16)
            wait("scalar", ev1)
            ev_cps[i % 2] = emit("scalar", lambda e, i=i: e.copy(
                out=wqT[:, i * D:(i + 1) * D], in_=wsl[i % 2][:]))
        waits("sync", [ev_cps[0], ev_cps[1]])
        ev1 = emit("sync", lambda e: e.dma_start(
            out=wslab[:, 0:8 * N].rearrange("p (i n) -> p i n", n=N),
            in_=wb_h.rearrange("(i p) n -> p i n", p=128)), inc=16)
        ev1b = emit("sync", lambda e: e.dma_start(
            out=wslab[:, 8 * N:16 * N].rearrange("p (i n) -> p i n", n=N),
            in_=wc_h.rearrange("(i p) n -> p i n", p=128)), inc=16)
        waits("vector", [ev1, ev1b])
        emit("vector", lambda e: e.tensor_copy(out=wbT[:], in_=wslab[:, 0:8 * N]))
        ev_cp_prev = emit("vector", lambda e: e.tensor_copy(out=wcT[:], in_=wslab[:, 8 * N:16 * N]))

    emit("vector", lambda e: e.memset(h_t[:], 0.0))
    emit("vector", lambda e: e.memset(h0st[:], 0.0))
    emit("vector", lambda e: e.memset(s_blk[:], 0.0))
    emit("vector", lambda e: e.memset(chE[:], 0.0))
    emit("vector", lambda e: e.memset(chS[:], 0.0))
    if debug:
        emit("vector", lambda e: e.memset(dt2[:], 0.0))
        emit("vector", lambda e: e.memset(bp_full[:], 0.0))
        emit("vector", lambda e: e.memset(cp_full[:], 0.0))
        emit("vector", lambda e: e.memset(sfull[0][:], 0.0))
        emit("vector", lambda e: e.memset(sfull[1][:], 0.0))
        emit("vector", lambda e: e.memset(yt_blk[:], 0.0))
        emit("vector", lambda e: e.memset(xn_t[:], 0.0))
        barrier(snapshot())
    if debug:
        waits("gpsimd", [ev_a16b])
        emit("gpsimd", lambda e: e.dma_start(out=dbg["dbg_a16"], in_=a16b[:]), inc=16)

    a_sq = [None] * NBLK
    a_v2 = [None] * NBLK
    a_rstd = [None] * NBLK
    a_xn = [None] * NBLK
    a_trlast = [None] * NBLK
    a_xnTl = [None] * NBLK
    a_cp = [None] * NBLK
    a_e1 = [[None, None] for _ in range(NBLK)]
    a_e1s = [None] * NBLK
    a_dtf = [None] * NBLK
    NG = NBLK * NSLC
    ev_g = [None] * NG
    ev_garg = [None] * NG
    ev_ca = [None] * NG
    ev_E = [None] * NG
    ev_en = [None] * NG
    ev_r = [None] * NG
    ev_smm = [None] * NG
    ev_evac = [None] * NG
    ev_re = [None] * NG
    ev_stt = [None] * NG
    ev_yadd = [None] * NG
    ev_sq_b = [None] * NBLK
    ev_v2_b = [None] * NBLK
    ev_sr_b = [None] * NBLK
    ev_xn_b = [None] * NBLK
    ev_w_b = [None] * NBLK
    ev_ext_b = [None] * NBLK
    ev_chain_b = [None] * NBLK
    ev_wb_b = [None] * NBLK
    ev_basex_b = [None] * NBLK
    ev_baseff_b = [None] * NBLK
    ev_store_b = [None] * NBLK
    ev_hcarry_b = [None] * NBLK
    ev_ffn_b = [None] * NBLK

    def pe(lst, i):
        return lst[i] if i >= 0 else None

    base2 = [yt_blk, base_b]

    def head_chunks(k):
        t0, bt, regions = blocks[k]
        kb = k % 2
        st = {"tr": [None] * 8, "evv": [None] * 8, "evs": [None] * 8,
              "mmd": [None, None], "sg": [None, None], "mmc": None}
        ch = []

        def c_ln1():
            waits("sync", [pe(a_sq, k - 1), pe(a_xn, k - 1), pe(ev_basex_b, k - 1), ev_bdt])
            evx = emit("sync", lambda e, t0=t0, bt=bt: e.dma_start(
                out=x_t[0:bt, :], in_=x_h[t0:t0 + bt, :]), inc=16)
            wait("vector", evx)
            emit("vector", lambda e, bt=bt: e.tensor_reduce(
                out=mean_c[0:bt, :], in_=x_t[0:bt, :], axis=AX.X, op=ALU.add))
            waits("scalar", [evx, pe(a_v2, k - 1), pe(ev_r, k * NSLC - 1)])
            ev_sq_b[k] = a_sq[k] = emit("scalar", lambda e, bt=bt: e.activation(
                out=w_t[0:bt, :], in_=x_t[0:bt, :], func=AF.Square, accum_out=sq_c[0:bt, :]))
        ch.append(c_ln1)

        def c_ln2():
            waits("vector", [a_sq[k], pe(a_rstd, k - 1)])
            emit("vector", lambda e, bt=bt: e.tensor_scalar(
                out=mean_c[0:bt, :], in0=mean_c[0:bt, :], scalar1=1.0 / D, scalar2=None, op0=ALU.mult))
            emit("vector", lambda e, bt=bt: e.tensor_tensor(
                out=var_c[0:bt, :], in0=mean_c[0:bt, :], in1=mean_c[0:bt, :], op=ALU.mult))
            a_v2[k] = emit("vector", lambda e, bt=bt: e.scalar_tensor_tensor(
                out=var_c[0:bt, :], in0=sq_c[0:bt, :], scalar=1.0 / D, in1=var_c[0:bt, :],
                op0=ALU.mult, op1=ALU.subtract))
            waits("scalar", [a_v2[k], pe(a_xn, k - 1)])
            emit("scalar", lambda e, bt=bt: e.activation(
                out=rstd_c[0:bt, :], in_=var_c[0:bt, :], func=AF.Ln, bias=1e-5))
            a_rstd[k] = emit("scalar", lambda e, bt=bt: e.activation(
                out=rstd_c[0:bt, :], in_=rstd_c[0:bt, :], func=AF.Exp, scale=-0.5))
            waits("vector", [a_rstd[k], pe(a_trlast, k - 1)])
            a_xn[k] = emit("vector", lambda e, bt=bt: e.tensor_scalar(
                out=xn_t[0:bt, :], in0=x_t[0:bt, :], scalar1=mean_c[0:bt, :], scalar2=rstd_c[0:bt, :],
                op0=ALU.subtract, op1=ALU.mult))
        ch.append(c_ln2)

        def c_tr(j):
            if j < 8:
                waits("tensor", [a_xn[k], pe(a_cp, k - 1),
                                 pe(ev_en, k * NSLC - 1) if j < 2 else None,
                                 st["evs"][j - 2] if j >= 2 else None])
                st["tr"][j] = emit("tensor", lambda e, j=j, bt=bt: e.transpose(
                    out=p_ca[j % 2][:, 0:bt], in_=xn_t[0:bt, j * 128:(j + 1) * 128],
                    identity=id_f32[0:bt, 0:bt]))
            if j >= 1:
                jj = j - 1
                waits("scalar", [st["tr"][jj], pe(ev_ffn_b, k - 2) if jj == 0 else None])
                st["evv"][jj] = emit("scalar", lambda e, jj=jj, bt=bt: e.copy(
                    out=xnT_t[:, jj * BT: jj * BT + bt], in_=p_ca[jj % 2][:, 0:bt]))
                st["evs"][jj] = emit("scalar", lambda e, jj=jj, bt=bt, kb=kb: e.copy(
                    out=xqs[kb][:, jj * BT: jj * BT + bt], in_=p_ca[jj % 2][:, 0:bt]))
            if j == 8:
                a_trlast[k] = st["tr"][7]
                a_xnTl[k] = st["evv"][7]
        for j in range(9):
            ch.append(lambda j=j: c_tr(j))

        def c_dtmm(h):
            waits("tensor", [a_xnTl[k],
                             pe(ev_stt, (k - 2) * NSLC + NSLC - 1) if k >= 2 else None,
                             pe(ev_baseff_b, k - 1), pe(a_e1s, k - 1)])
            for j in range(8):
                emit("tensor", lambda e, j=j, h=h, bt=bt: e.matmul(
                    out=p_ffn[h][0:bt, :], lhsT=xnT_t[:, j * BT: j * BT + bt],
                    rhs=wdtT[:, j * D + h * SL: j * D + h * SL + SL],
                    start=(j == 0), stop=False))
            st["mmd"][h] = emit("tensor", lambda e, h=h, bt=bt: e.matmul(
                out=p_ffn[h][0:bt, :], lhsT=ones_bf[:, 0:bt],
                rhs=bdt_row[:, h * SL: h * SL + SL], start=False, stop=True))
        ch.append(lambda: c_dtmm(0))
        ch.append(lambda: c_dtmm(1))

        def c_epi():
            for h in range(2):
                waits("scalar", [st["mmd"][h], pe(a_dtf, k - 1)])
                a_e1[k][h] = emit("scalar", lambda e, h=h, bt=bt: e.activation(
                    out=ep_e[0:bt, h * SL:(h + 1) * SL], in_=p_ffn[h][0:bt, :], func=AF.Exp))
            a_e1s[k] = a_e1[k][1]
            for h in range(2):
                emit("scalar", lambda e, h=h, bt=bt: e.activation(
                    out=ep_e[0:bt, h * SL:(h + 1) * SL], in_=ep_e[0:bt, h * SL:(h + 1) * SL],
                    func=AF.Ln, bias=1.0))
            for h in range(2):
                st["sg"][h] = emit("scalar", lambda e, h=h, bt=bt: e.activation(
                    out=ep_e[0:bt, h * SL:(h + 1) * SL], in_=ep_e[0:bt, h * SL:(h + 1) * SL],
                    func=AF.Sigmoid, scale=1.0 / DT_R, bias=-DT_C / DT_R))
        ch.append(c_epi)

        def c_dt2():
            for h in range(2):
                wait("vector", st["sg"][h])
                a_dtf[k] = emit("vector", lambda e, h=h, kb=kb, bt=bt: e.tensor_scalar(
                    out=dt2[0:bt, kb * D + h * SL: kb * D + h * SL + SL],
                    in0=ep_e[0:bt, h * SL:(h + 1) * SL],
                    scalar1=4.0 * DT_R, scalar2=DT_C - 2.0 * DT_R, op0=ALU.mult, op1=ALU.add))
        ch.append(c_dt2)

        def c_bmm():
            waits("tensor", [a_xnTl[k], st["evs"][6], st["evs"][7]])
            for j in range(8):
                emit("tensor", lambda e, j=j, bt=bt: e.matmul(
                    out=p_ca[0][0:bt, 0:N], lhsT=xnT_t[:, j * BT: j * BT + bt],
                    rhs=wbT[:, j * N:(j + 1) * N], start=(j == 0), stop=(j == 7)))
        ch.append(c_bmm)

        def c_cmm():
            for j in range(8):
                st["mmc"] = emit("tensor", lambda e, j=j, bt=bt: e.matmul(
                    out=p_ca[0][0:bt, N:2 * N], lhsT=xnT_t[:, j * BT: j * BT + bt],
                    rhs=wcT[:, j * N:(j + 1) * N], start=(j == 0), stop=(j == 7)))
            wait("vector", st["mmc"])
            emit("vector", lambda e, k=k, bt=bt: e.tensor_copy(
                out=bp_full[0:bt, k * N:(k + 1) * N], in_=p_ca[0][0:bt, 0:N]))
            a_cp[k] = emit("vector", lambda e, k=k, bt=bt: e.tensor_copy(
                out=cp_full[0:bt, k * N:(k + 1) * N], in_=p_ca[0][0:bt, N:2 * N]))
        ch.append(c_cmm)

        def c_w():
            ev_w_b[k] = emit("vector", lambda e, kb=kb, bt=bt: e.tensor_tensor(
                out=w_t[0:bt, :], in0=xn_t[0:bt, :], in1=dt2[0:bt, kb * D:(kb + 1) * D], op=ALU.mult))
            emit("vector", lambda e, kb=kb, bt=bt: e.memset(sfull[kb][0:bt, :], 1.0))
        ch.append(c_w)

        def c_ffn(h):
            if h == 0:
                waits("tensor", [a_e1[k][0], a_e1[k][1]])
            for j in range(8):
                st["mmc"] = emit("tensor", lambda e, j=j, h=h, bt=bt, kb=kb: e.matmul(
                    out=p_ffn[h][0:bt, :], lhsT=xqs[kb][:, j * BT: j * BT + bt],
                    rhs=wqT[:, j * D + h * SL: j * D + h * SL + SL],
                    start=(j == 0), stop=(j == 7)))
            if h == 1:
                ev_ffn_b[k] = st["mmc"]
        ch.append(lambda: c_ffn(0))
        ch.append(lambda: c_ffn(1))

        def c_base1():
            bb = base2[kb]
            waits("vector", [pe(ev_store_b, k - 2)])
            emit("vector", lambda e, bt=bt, bb=bb: e.tensor_tensor(
                out=bb[0:bt, :], in0=xn_t[0:bt, :], in1=dbc[0:bt, :], op=ALU.mult))
            ev_basex_b[k] = emit("vector", lambda e, bt=bt, bb=bb: e.tensor_tensor(
                out=bb[0:bt, :], in0=bb[0:bt, :], in1=x_t[0:bt, :], op=ALU.add))
        ch.append(c_base1)

        def c_base2():
            bb = base2[kb]
            waits("vector", [ev_ffn_b[k], ev_fbb])
            for h in range(2):
                emit("vector", lambda e, h=h, bt=bt, bb=bb: e.tensor_tensor(
                    out=bb[0:bt, h * SL:(h + 1) * SL], in0=bb[0:bt, h * SL:(h + 1) * SL],
                    in1=p_ffn[h][0:bt, :], op=ALU.add))
            ev_baseff_b[k] = emit("vector", lambda e, bt=bt, bb=bb: e.tensor_tensor(
                out=bb[0:bt, :], in0=bb[0:bt, :], in1=fbb[0:bt, :], op=ALU.add))
        ch.append(c_base2)
        return ch

    def emit_2a_stage(k, i):
        t0, bt, regions = blocks[k]
        pat = pat_of(k)
        g0 = k * NSLC
        if i < NSLC and i % 2 == 0:
            s0, s1 = i, i + 1
            gs0 = g0 + s0
            pb = (s0 >> 1) % 2
            waits("vector", [pe(ev_g, gs0 - 4)])
            ev_garg[gs0] = ev_garg[gs0 + 1] = emit("vector", lambda e, kb=k % 2, bt=bt, s0=s0, pb=pb: e.tensor_tensor(
                out=garg[pb][0:bt, :], in0=dt2[0:bt, kb * D: kb * D + 2 * SL],
                in1=a16b[0:bt, s0 * SL:(s0 + 2) * SL], op=ALU.mult))
        if 0 <= i - 1 < NSLC and (i - 1) % 2 == 0:
            s0 = i - 1
            gs0 = g0 + s0
            pb = (s0 >> 1) % 2
            waits("scalar", [ev_garg[gs0 + 1], pe(ev_ca, gs0 - 3), ev_w_b[k]])
            ev_g[gs0] = ev_g[gs0 + 1] = emit("scalar", lambda e, bt=bt, pb=pb: e.activation(
                out=g_t[pb][0:bt, :], in_=garg[pb][0:bt, :], func=AF.Tanh))
        if 0 <= i - 1 < NSLC:
            s = i - 1
            gs = g0 + s
            waits("tensor", [ev_g[gs], pe(ev_en, gs - 4), a_cp[k] if s < 4 else None])
            ev_ca[gs] = emit("tensor", lambda e, pat=pat, bt=bt, s=s: e.matmul(
                out=p_ca[(s >> 1) % 2][:, (s & 1) * SL:((s & 1) + 1) * SL],
                lhsT=tri_c[0:bt, pat * 128:(pat + 1) * 128],
                rhs=g_t[(s >> 1) % 2][0:bt, (s & 1) * SL:(s & 1) * SL + SL],
                start=True, stop=True))
        if 0 <= i - 2 < NSLC and (i - 2) % 2 == 1:
            s1 = i - 2
            s0 = s1 - 1
            gs1 = g0 + s1
            gs0 = g0 + s0
            pb = (s0 >> 1) % 2
            waits("scalar", [ev_ca[gs1], pe(ev_stt, gs0 - NSLC), pe(ev_stt, gs1 - NSLC), pe(ev_ext_b, k - 1)])
            ev_E[gs0] = ev_E[gs1] = emit("scalar", lambda e, s0=s0, pb=pb: e.activation(
                out=e_blk[:, s0 * SL:(s0 + 2) * SL], in_=p_ca[pb][:], func=AF.Exp, scale=16.0))
            waits("scalar", [pe(ev_r, gs0 - 2)])
            waits("scalar", [pe(ev_r, gs0 - 2)])
            ev_en[gs0] = ev_en[gs1] = emit("scalar", lambda e, bt=bt, pb=pb: e.activation(
                out=en2_t[0:bt, :], in_=p_ca[pb][0:bt, :], func=AF.Exp, scale=-16.0))
        if 0 <= i - 3 < NSLC and (i - 3) % 2 == 1:
            s1 = i - 3
            s0 = s1 - 1
            gs1 = g0 + s1
            gs0 = g0 + s0
            n_i = s0 >> 1
            waits("vector", [ev_en[gs1], pe(ev_smm, gs0 - 2), pe(ev_smm, gs1 - 2)])
            ev_r[gs0] = ev_r[gs1] = emit("vector", lambda e, k=k, bt=bt, n_i=n_i: e.scalar_tensor_tensor(
                out=r2_t[0][0:bt, :], in0=w_t[0:bt, :],
                scalar=bp_full[0:bt, k * N + n_i: k * N + n_i + 1],
                in1=en2_t[0:bt, :], op0=ALU.mult, op1=ALU.mult))
            for s in (s0, s1):
                gs = g0 + s
                waits("tensor", [ev_r[gs], pe(ev_evac, gs - 2)])
                ev_smm[gs] = emit("tensor", lambda e, pat=pat, bt=bt, b=s % 2, s=s: e.matmul(
                    out=p_sp[b][:], lhsT=tri_c[0:bt, pat * 128:(pat + 1) * 128],
                    rhs=r2_t[0][0:bt, (s & 1) * SL:((s & 1) + 1) * SL],
                    start=True, stop=True))
        if 0 <= i - 4 < NSLC:
            s = i - 4
            gs = g0 + s
            b = s % 2
            eng = "scalar" if (s % 2 == 0) else "vector"
            waits(eng, [ev_smm[gs], pe(ev_re, gs - NSLC), pe(ev_ext_b, k - 1)])
            if eng == "scalar":
                ev_evac[gs] = emit("scalar", lambda e, s=s, b=b: e.copy(
                    out=s_blk[:, s * SL:(s + 1) * SL], in_=p_sp[b][:]))
            else:
                ev_evac[gs] = emit("vector", lambda e, s=s, b=b: e.tensor_copy(
                    out=s_blk[:, s * SL:(s + 1) * SL], in_=p_sp[b][:]))

    def emit_2b_stage(k, i):
        t0, bt, regions = blocks[k]
        g0 = k * NSLC
        if i < NSLC:
            s = i
            gs = g0 + s
            waits("tensor", [ev_wb_b[k], ev_evac[gs], pe(ev_stt, gs - 2),
                             ev_baseff_b[k + 1] if k + 1 < NBLK and ev_baseff_b[k + 1] is not None else None])
            ev_re[gs] = emit("tensor", lambda e, k=k, bt=bt, s=s: e.matmul(
                out=p_ffn[s % 2][0:bt, :], lhsT=mre_c[:, pat_of(k) * BT: pat_of(k) * BT + bt],
                rhs=s_blk[:, s * SL:(s + 1) * SL], start=True, stop=True))
        if 0 <= i - 1 < NSLC:
            s = i - 1
            gs = g0 + s
            b = s % 2
            n_i = s >> 1
            dh = (s & 1) * SL
            waits("vector", [ev_re[gs]])
            if s < 2:
                ev_stt[gs] = emit("vector", lambda e, k=k, bt=bt, s=s, n_i=n_i, dh=dh, b=b: e.scalar_tensor_tensor(
                    out=yt_bf[0:bt, dh:dh + SL], in0=e_blk[0:bt, s * SL:(s + 1) * SL],
                    scalar=cp_full[0:bt, k * N + n_i: k * N + n_i + 1], in1=p_ffn[b][0:bt, :],
                    op0=ALU.mult, op1=ALU.mult))
                ev_yadd[gs] = ev_stt[gs]
            else:
                waits("vector", [pe(ev_yadd, gs - 4)])
                ev_stt[gs] = emit("vector", lambda e, k=k, bt=bt, s=s, n_i=n_i, cb=s % 4, b=b: e.scalar_tensor_tensor(
                    out=ces_t[0:bt, cb * SL:(cb + 1) * SL], in0=e_blk[0:bt, s * SL:(s + 1) * SL],
                    scalar=cp_full[0:bt, k * N + n_i: k * N + n_i + 1], in1=p_ffn[b][0:bt, :],
                    op0=ALU.mult, op1=ALU.mult))

                def _yadd(e, bt=bt, dh=dh, cb=s % 4):
                    with nc.allow_low_precision(reason="y partial sums kept in bf16; rvr-checked"):
                        return e.tensor_tensor(
                            out=yt_bf[0:bt, dh:dh + SL], in0=yt_bf[0:bt, dh:dh + SL],
                            in1=ces_t[0:bt, cb * SL:(cb + 1) * SL], op=ALU.add)
                ev_yadd[gs] = emit("vector", _yadd)

    def emit_finalize(k):
        t0, bt, regions = blocks[k]
        bb = base2[k % 2]
        waits("vector", [ev_yadd[k * NSLC + NSLC - 1]])
        ev_out = emit("vector", lambda e, k=k, bt=bt, bb=bb: e.scalar_tensor_tensor(
            out=bb[0:bt, :], in0=yt_bf[0:bt, :], scalar=sfull[k % 2][0:bt, :],
            in1=bb[0:bt, :], op0=ALU.mult, op1=ALU.add))
        wait("sync", ev_out)
        ev_store_b[k] = emit("sync", lambda e, t0=t0, bt=bt, bb=bb: e.dma_start(
            out=out_h[t0:t0 + bt, :], in_=bb[0:bt, :]), inc=16)

    def emit_chain(k, feeder=None):
        def feed(n=1):
            for _ in range(n):
                if feeder:
                    feeder.pop(0)()
        t0, bt, regions = blocks[k]
        nreg = len(regions)
        g0 = k * NSLC
        waits("gpsimd", [ev_E[g0 + NSLC - 1], ev_evac[g0 + NSLC - 1], ev_evac[g0 + NSLC - 2],
                         pe(ev_chain_b, k - 1)])
        emit("gpsimd", lambda e, nreg=nreg: e.dma_start(
            out=chE_d[0:nreg, :], in_=e_blk[114:114 + nreg, :]), inc=16)
        emit("gpsimd", lambda e, nreg=nreg: e.dma_start(
            out=chS_d[0:nreg, :], in_=s_blk[114:114 + nreg, :]), inc=16)
        emit("gpsimd", lambda e, nreg=nreg: e.dma_start(
            out=chE[:, 0:nreg * 128].rearrange("p (c j) -> p c j", j=128),
            in_=chE_d[0:nreg, :].rearrange("c (q j) -> q c j", j=128)), inc=16)
        ev_ext = [emit("gpsimd", lambda e, nreg=nreg: e.dma_start(
            out=chS[:, 0:nreg * 128].rearrange("p (c j) -> p c j", j=128),
            in_=chS_d[0:nreg, :].rearrange("c (q j) -> q c j", j=128)), inc=16)]
        ev_ext_b[k] = ev_ext[-1]
        feed(2)
        waits("gpsimd", [pe(ev_hcarry_b, k - 1)])
        ev_h0w = [emit("gpsimd", lambda e: e.tensor_copy(out=h0st[:, 0:128], in_=h_t[:]))]
        waits("vector", ev_ext + [pe(ev_wb_b, k - 1)])
        emit("vector", lambda e: e.tensor_tensor(
            out=ht_tmp[:], in0=h_t[:], in1=chS[:, 0:128], op=ALU.add))
        pend_pool_h0 = None
        for c, (lo, hi, clamps) in enumerate(regions):
            feed(2)
            last = (c == nreg - 1)
            waits("vector", [pend_pool_h0])
            pend_pool_h0 = None
            if clamps:
                emit("vector", lambda e, c=c: e.scalar_tensor_tensor(
                    out=ht_new[:], in0=chE[:, c * 128:(c + 1) * 128], scalar=1.0,
                    in1=ht_tmp[:], op0=ALU.mult, op1=ALU.mult))
                ev_q = emit("vector", lambda e: e.scalar_tensor_tensor(
                    out=ht_tmp[:], in0=ht_new[:], scalar=1.0, in1=ht_new[:],
                    op0=ALU.mult, op1=ALU.mult, accum_out=qcol[:]))
                wait("tensor", ev_q)
                ev_qs = emit("tensor", lambda e: e.matmul(
                    out=p_s, lhsT=ones_sq[:], rhs=qcol[:], start=True, stop=True))
                wait("scalar", ev_qs)
                ev_rt = emit("scalar", lambda e: e.activation(
                    out=scol[:], in_=p_s, func=AF.Sqrt))
                wait("vector", ev_rt)
                emit("vector", lambda e: e.reciprocal(out=scol[:], in_=scol[:]))
                emit("vector", lambda e: e.tensor_scalar(
                    out=scol[:], in0=scol[:], scalar1=20.0, scalar2=1.0, op0=ALU.mult, op1=ALU.min))
                if last:
                    waits("vector", [ev_h0w[0]])
                    ev_hcarry_b[k] = emit("vector", lambda e: e.tensor_scalar(
                        out=h_t[:], in0=ht_new[:], scalar1=scol[:], scalar2=None, op0=ALU.mult))
                else:
                    emit("vector", lambda e, c=c: e.scalar_tensor_tensor(
                        out=ht_tmp[:], in0=ht_new[:], scalar=scol[:],
                        in1=chS[:, (c + 1) * 128:(c + 2) * 128], op0=ALU.mult, op1=ALU.add))
                    ev_h0w.append(emit("vector", lambda e, c=c: e.tensor_scalar(
                        out=h0st[:, (c + 1) * 128:(c + 2) * 128], in0=ht_new[:],
                        scalar1=scol[:], scalar2=None, op0=ALU.mult)))
                emit("vector", lambda e: e.tensor_scalar(
                    out=sm1[:], in0=scol[:], scalar1=-1.0, scalar2=None, op0=ALU.add))
                ev_chain_b[k] = emit("vector", lambda e, hi=hi, bt=bt, k=k: e.scalar_tensor_tensor(
                    out=sfull[k % 2][0:bt, :], in0=id_f32[0:bt, hi:hi + 1], scalar=sm1[0:bt, :],
                    in1=sfull[k % 2][0:bt, :], op0=ALU.mult, op1=ALU.add))
            else:
                if last:
                    waits("vector", [ev_h0w[0]])
                    ev_chain_b[k] = ev_hcarry_b[k] = emit("vector", lambda e, c=c: e.tensor_tensor(
                        out=h_t[:], in0=chE[:, c * 128:(c + 1) * 128], in1=ht_tmp[:], op=ALU.mult))
                else:
                    ev_hn = emit("vector", lambda e, c=c: e.tensor_tensor(
                        out=ht_new[:], in0=chE[:, c * 128:(c + 1) * 128], in1=ht_tmp[:], op=ALU.mult))
                    ev_chain_b[k] = emit("vector", lambda e, c=c: e.tensor_tensor(
                        out=ht_tmp[:], in0=ht_new[:], in1=chS[:, (c + 1) * 128:(c + 2) * 128], op=ALU.add))
                    waits("gpsimd", [ev_chain_b[k]])
                    pend_pool_h0 = emit("gpsimd", lambda e, c=c: e.tensor_copy(
                        out=h0st[:, (c + 1) * 128:(c + 2) * 128], in_=ht_new[:]))
                    ev_h0w.append(pend_pool_h0)
        waits("gpsimd", ev_h0w + [pe(ev_hcarry_b, k)])
        emit("gpsimd", lambda e, nreg=nreg: e.dma_start(
            out=h0_d[0:nreg, :].rearrange("c (q j) -> q c j", j=128),
            in_=h0st[:, 0:nreg * 128].rearrange("p (c j) -> p c j", j=128)), inc=16)
        ev_wb_b[k] = emit("gpsimd", lambda e, nreg=nreg: e.dma_start(
            out=s_blk[114:114 + nreg, :], in_=h0_d[0:nreg, :]), inc=16)

    barrier(snapshot())
    NBL = len(blocks)
    if stage >= 4:
        for c in head_chunks(0):
            c()
        for k in range(NBL):
            for i in range(NSLC + 5):
                emit_2a_stage(k, i)
                if k >= 1 and stage >= 6:
                    emit_2b_stage(k - 1, i)
            if k >= 1 and stage >= 6:
                emit_finalize(k - 1)
            hc = head_chunks(k + 1) if k + 1 < NBL else []
            if stage >= 5:
                emit_chain(k, hc)
            for c in hc:
                c()
            if debug and k == 0:
                barrier(snapshot())
                emit("gpsimd", lambda e: e.dma_start(out=dbg["dbg_E0"], in_=e_blk[:]), inc=16)
                emit("gpsimd", lambda e: e.dma_start(out=dbg["dbg_S0"], in_=s_blk[:]), inc=16)
                emit("gpsimd", lambda e: e.dma_start(out=dbg["dbg_h0s"], in_=h0st[:]), inc=16)
                emit("gpsimd", lambda e: e.dma_start(out=dbg["dbg_xn0"], in_=xn_t[0:BT, :]), inc=16)
                barrier(snapshot())
        if stage >= 6:
            for i in range(NSLC + 2):
                emit_2b_stage(NBL - 1, i)
            emit_finalize(NBL - 1)
            if debug:
                barrier(snapshot())
                emit("gpsimd", lambda e: e.dma_start(out=dbg["dbg_y0"], in_=base2[(NBL - 1) % 2][0:BT, :]), inc=16)
                emit("gpsimd", lambda e, k=0: e.dma_start(out=dbg["dbg_sfull"], in_=sfull[0][:]), inc=16)
                barrier(snapshot())

    if debug:
        barrier(snapshot())
        emit("gpsimd", lambda e: e.dma_start(out=dbg["dbg_bp"], in_=bp_full[:]), inc=16)
        emit("gpsimd", lambda e: e.dma_start(out=dbg["dbg_cp"], in_=cp_full[:]), inc=16)

    def thunk_final_d(eng):
        eng.wait_ge(sem_d, cnt["d"])
    prog["sync"].append(thunk_final_d)

    def thunk_final_g(eng):
        eng.wait_ge(sem_g, cnt["g"])
        eng.wait_ge(sem_p, cnt["p"])
    prog["gpsimd"].append(thunk_final_g)

    with es, nc.Block() as block:
        @block.sync
        def _(eng):
            for f in prog["sync"]:
                f(eng)

        @block.gpsimd
        def _(eng):
            for f in prog["gpsimd"]:
                f(eng)

        @block.tensor
        def _(eng):
            for f in prog["tensor"]:
                f(eng)

        @block.vector
        def _(eng):
            for f in prog["vector"]:
                f(eng)

        @block.scalar
        def _(eng):
            for f in prog["scalar"]:
                f(eng)

    return nc


def kernel(L0_attn_x, L0_attn_W_dt, L0_attn_b_dt, L0_attn_W_B, L0_attn_W_C,
           L0_attn_A_log, L0_attn_D_param, L0_ffn_weight, L0_ffn_bias):
    _, consts = build_consts()
    nc = build_nc(debug=False)
    base = {
        "W_dtT": np.ascontiguousarray(np.asarray(L0_attn_W_dt, dtype=np.float32).T),
        "b_dt": np.ascontiguousarray(L0_attn_b_dt, dtype=np.float32).reshape(1, D),
        "W_BT": np.ascontiguousarray(np.asarray(L0_attn_W_B, dtype=np.float32).T),
        "W_CT": np.ascontiguousarray(np.asarray(L0_attn_W_C, dtype=np.float32).T),
        "A_logT": np.ascontiguousarray(L0_attn_A_log.T, dtype=np.float32),
        "D_param": np.ascontiguousarray(L0_attn_D_param, dtype=np.float32).reshape(1, D),
        "W_ffnT": np.ascontiguousarray(np.asarray(L0_ffn_weight, dtype=np.float32).T),
        "b_ffn": np.ascontiguousarray(L0_ffn_bias, dtype=np.float32).reshape(1, D),
    }
    base.update(consts)
    x = np.ascontiguousarray(L0_attn_x, dtype=np.float32)
    in_maps = []
    for i in range(8):
        m = dict(base)
        m["x"] = x[i]
        in_maps.append(m)
    res = run_bass_kernel_spmd(nc, in_maps, core_ids=list(range(8)))
    out = np.stack([np.asarray(r["out"]) for r in res.results], axis=0).astype(np.float32)
    return out


# version 18 (final)
# speedup vs baseline: 1.1756x; 1.0074x over previous
import contextlib
import numpy as np
import ml_dtypes
import concourse.bass as bass
import concourse.mybir as mybir
from concourse.bass_utils import run_bass_kernel_spmd

F32 = mybir.dt.float32
BF16 = mybir.dt.bfloat16
FP8 = mybir.dt.float8e4
AF = mybir.ActivationFunctionType
ALU = mybir.AluOpType
AX = mybir.AxisListType

BFD = ml_dtypes.bfloat16
F8D = ml_dtypes.float8_e4m3fn

L = 2048
D = 1024
N = 16
DN = D * N
BT = 114
NBLK = 18
NSLC = 32
SL = 512
RMAX = 13
NPAT = 6

DT_C = (0.1 + 0.001) / 2.0
DT_R = (0.1 - 0.001) / 2.0


def block_structure():
    blocks = []
    for k in range(NBLK):
        t0 = k * BT
        bt = min(BT, L - t0)
        regions = []
        lo = 0
        for r in range(1, bt + 1):
            t = t0 + r
            if r == bt or t % 10 == 1:
                hi = r - 1
                clamps = ((t0 + hi) % 10 == 0)
                regions.append((lo, hi, clamps))
                lo = r
        blocks.append((t0, bt, regions))
    return blocks


def pat_of(k):
    return 5 if k == NBLK - 1 else k % 5


def build_consts():
    blocks = block_structure()
    tri = np.zeros((NPAT, BT, 128), dtype=np.float32)
    mre = np.zeros((NPAT, 128, BT), dtype=np.float32)
    done = set()
    for k, (t0, bt, regions) in enumerate(blocks):
        p = pat_of(k)
        if p in done:
            continue
        done.add(p)
        for ci, (lo, hi, cl) in enumerate(regions):
            for tp in range(lo, hi + 1):
                tri[p, lo:tp + 1, tp] = 1.0
                tri[p, tp, 114 + ci] = 1.0
                mre[p, tp, tp] = 1.0
                mre[p, 114 + ci, tp] = 1.0
    ident = np.eye(128, dtype=np.float32)
    c = {
        "c_tri": tri.reshape(NPAT * BT, 128).astype(BFD),
        "c_mre": mre.reshape(NPAT * 128, BT).astype(BFD),
        "c_id_bf": ident.astype(BFD),
        "c_id_f32": ident.copy(),
        "c_id_f8": ident.astype(F8D),
        "c_ones_bf": np.ones((1, BT), dtype=np.float32).astype(BFD),
        "c_ones_row": np.ones((1, 128), dtype=np.float32),
        "c_ones_p": np.ones((128, 1), dtype=np.float32),
        "c_ones_sq": np.ones((128, 128), dtype=np.float32),
        "c_bias": np.tile(np.array([[1e-5, 1.0, -DT_C / DT_R]], dtype=np.float32), (128, 1)),
    }
    return blocks, c


def build_nc(debug=False, nblk=NBLK, stage=9):
    blocks, _ = build_consts()
    blocks = blocks[:nblk]
    nc = bass.Bass("TRN2", target_bir_lowering=False)

    x_h = nc.dram_tensor("x", [L, D], F32, kind="ExternalInput").ap()
    wdt_h = nc.dram_tensor("W_dtT", [D, D], F32, kind="ExternalInput").ap()
    bdt_h = nc.dram_tensor("b_dt", [1, D], F32, kind="ExternalInput").ap()
    wb_h = nc.dram_tensor("W_BT", [D, N], F32, kind="ExternalInput").ap()
    wc_h = nc.dram_tensor("W_CT", [D, N], F32, kind="ExternalInput").ap()
    alog_h = nc.dram_tensor("A_logT", [N, D], F32, kind="ExternalInput").ap()
    dpar_h = nc.dram_tensor("D_param", [1, D], F32, kind="ExternalInput").ap()
    wf_h = nc.dram_tensor("W_ffnT", [D, D], F32, kind="ExternalInput").ap()
    bf_h = nc.dram_tensor("b_ffn", [1, D], F32, kind="ExternalInput").ap()
    tri_h = nc.dram_tensor("c_tri", [NPAT * BT, 128], BF16, kind="ExternalInput").ap()
    mre_h = nc.dram_tensor("c_mre", [NPAT * 128, BT], BF16, kind="ExternalInput").ap()
    idb_h = nc.dram_tensor("c_id_bf", [128, 128], BF16, kind="ExternalInput").ap()
    idf_h = nc.dram_tensor("c_id_f32", [128, 128], F32, kind="ExternalInput").ap()
    id8_h = nc.dram_tensor("c_id_f8", [128, 128], FP8, kind="ExternalInput").ap()
    onesb_h = nc.dram_tensor("c_ones_bf", [1, BT], BF16, kind="ExternalInput").ap()
    onesr_h = nc.dram_tensor("c_ones_row", [1, 128], F32, kind="ExternalInput").ap()
    onesp_h = nc.dram_tensor("c_ones_p", [128, 1], F32, kind="ExternalInput").ap()
    bias_h = nc.dram_tensor("c_bias", [128, 3], F32, kind="ExternalInput").ap()
    onessq_h = nc.dram_tensor("c_ones_sq", [128, 128], F32, kind="ExternalInput").ap()
    a16_d3 = nc.dram_tensor("a16_scratch3", [128, 128], BF16, kind="Internal").ap()
    chE_d = nc.dram_tensor("chE_scratch", [RMAX, DN], BF16, kind="Internal").ap()
    chS_d = nc.dram_tensor("chS_scratch", [RMAX, DN], BF16, kind="Internal").ap()
    h0_d = nc.dram_tensor("h0_scratch", [RMAX, DN], BF16, kind="Internal").ap()
    xq_d = nc.dram_tensor("xq_scratch", [128, NBLK * 8 * BT], FP8, kind="Internal").ap()

    out_h = nc.dram_tensor("out", [L, D], F32, kind="ExternalOutput").ap()
    dbg = {}
    if debug:
        for nm, shp, dt_ in [("dbg_E0", [128, DN], F32),
                             ("dbg_S0", [128, DN], F32), ("dbg_y0", [BT, D], F32),
                             ("dbg_h0s", [128, RMAX * 128], F32),
                             ("dbg_sfull", [BT, 1], F32), ("dbg_bp", [128, NBLK * N], F32),
                             ("dbg_cp", [128, NBLK * N], F32), ("dbg_xn0", [BT, D], F32),
                             ("dbg_a16", [128, DN], F32)]:
            dbg[nm] = nc.dram_tensor(nm, shp, dt_, kind="ExternalOutput").ap()

    es = contextlib.ExitStack()

    def sb(name, shape, dtype):
        return es.enter_context(nc.sbuf_tensor(name, shape, dtype))

    def ps(name, shape, dtype):
        return es.enter_context(nc.psum_tensor(name, shape, dtype))

    dt2 = sb("dt2", [128, 2 * D], BF16)
    wdtT = sb("wdtT", [128, 8 * D], BF16)
    xnT_t = sb("xnT_t", [128, 8 * BT], BF16)
    ep_e = sb("ep_e", [128, 2 * SL], F32)
    wbT = sb("wbT", [128, 8 * N], BF16)
    wcT = sb("wcT", [128, 8 * N], BF16)
    bdt_row = sb("bdt_row", [1, D], BF16)
    xqs = [sb(f"xqs{i}", [128, 8 * BT], FP8) for i in range(2)]
    a16b = sb("a16b", [128, DN], BF16)
    s_blk = sb("s_blk", [128, DN], BF16)
    wqT = sb("wqT", [128, 8 * D], FP8)
    tri_c = sb("tri_c", [BT, NPAT * 128], BF16)
    mre_c = sb("mre_c", [128, NPAT * BT], BF16)
    bp_full = sb("bp_full", [128, NBLK * N], BF16)
    cp_full = sb("cp_full", [128, NBLK * N], F32)
    dbc = sb("dbc", [128, D], BF16)
    fbb = sb("fbb", [128, D], BF16)
    id_bf = sb("id_bf", [128, 128], BF16)
    id_f32 = sb("id_f32", [128, 128], F32)
    id_f8 = sb("id_f8", [128, 128], FP8)
    ones_bf = sb("ones_bf", [1, BT], BF16)
    ones_sq = sb("ones_sq", [128, 128], F32)
    bias_c = sb("bias_c", [128, 3], F32)
    chE = sb("chE", [128, RMAX * 128], BF16)
    chS = sb("chS", [128, RMAX * 128], BF16)
    h0st = sb("h0st", [128, RMAX * 128], BF16)
    h_t = sb("h_t", [128, 128], F32)
    ht_tmp = sb("ht_tmp", [128, 128], F32)
    ht_new = sb("ht_new", [128, 128], F32)
    sfull = [sb(f"sfull{i}", [BT, 1], F32) for i in range(2)]
    qcol = sb("qcol", [128, 1], F32)
    scol = sb("scol", [128, 1], F32)
    sm1 = sb("sm1", [128, 1], F32)
    x_t = sb("x_t", [128, D], F32)
    xn_t = sb("xn_t", [128, D], F32)
    w_t = sb("w_t", [128, D], BF16)
    mean_c = sb("mean_c", [128, 1], F32)
    sq_c = sb("sq_c", [128, 1], F32)
    var_c = sb("var_c", [128, 1], F32)
    rstd_c = sb("rstd_c", [128, 1], F32)
    garg = [sb(f"garg{i}", [BT, 2 * SL], BF16) for i in range(2)]
    g_t = [sb(f"g_t{i}", [BT, 2 * SL], BF16) for i in range(2)]
    en2_t = sb("en2_t", [BT, 2 * SL], BF16)
    r2_t = [sb(f"r2_t{i}", [BT, 2 * SL], BF16) for i in range(1)]
    ces_t = sb("ces_t", [BT, 4 * SL], BF16)
    yt_bf = sb("yt_bf", [BT, D], BF16)
    yt_blk = sb("yt_blk", [BT, D], F32)
    base_b = sb("base_b", [BT, D], F32)

    off0 = (nc.sbuf_base + 31) & ~31
    arena = sb("arena", [128, 16384], BF16)

    def at(name, shape, dtype, byte_off):
        return nc.alloc_sbuf_tensor_at(name, shape, dtype, offset=off0 + byte_off)

    e_blk = arena
    wslab = at("wslab", [128, D], F32, 16384)
    wslabB = at("wslabB", [128, D], F32, 20480)
    wsl = [wslab, wslabB]
    atile = at("atile", [128, 128], F32, 0)
    a16t = at("a16t", [128, 128], BF16, 512)

    p_ca = [ps(f"p_ca{i}", [128, 2 * SL], F32) for i in range(2)]
    p_sp = [ps(f"p_sp{i}", [128, SL], F32) for i in range(2)]
    p_ffn = [ps(f"p_ffn{i}", [128, SL], F32) for i in range(2)]
    p_trb = [p_ca[i][:].bitcast(BF16)[:, 0:128] for i in range(2)]
    p_tr8 = [p_ca[i][:].bitcast(FP8)[:, 512:768].rearrange("p (j t) -> p j t", t=2)[:, :, 0:1].squeeze(2) for i in range(2)]
    p_s = p_sp[0][:, 300:301]

    sem_d = es.enter_context(nc.semaphore("sem_d"))
    sem_g = es.enter_context(nc.semaphore("sem_g"))
    sem_p = es.enter_context(nc.semaphore("sem_p"))
    sem_t = es.enter_context(nc.semaphore("sem_t"))
    sem_v = es.enter_context(nc.semaphore("sem_v"))
    sem_s = es.enter_context(nc.semaphore("sem_s"))

    cnt = {"d": 0, "g": 0, "p": 0, "t": 0, "v": 0, "s": 0}
    sems = {"d": sem_d, "g": sem_g, "p": sem_p, "t": sem_t, "v": sem_v, "s": sem_s}
    prog = {"sync": [], "gpsimd": [], "tensor": [], "vector": [], "scalar": []}
    ENG_SEM = {"sync": "d", "gpsimd": "g", "tensor": "t", "vector": "v", "scalar": "s"}
    ENG_KEYS = {"sync": ("d",), "gpsimd": ("g", "p"), "tensor": ("t",),
                "vector": ("v",), "scalar": ("s",)}

    def emit(engine, fn, inc=1):
        if engine == "gpsimd":
            key = "g" if inc == 16 else "p"
        else:
            key = ENG_SEM[engine]
        befores = [(kk, cnt[kk]) for kk in ENG_KEYS[engine]]
        cnt[key] += inc
        sem = sems[key]
        if engine != "tensor":
            for kk, before in befores:
                if before > 0:
                    def wthunk(eng, sem=sems[kk], before=before):
                        eng.wait_ge(sem, before)
                    prog[engine].append(wthunk)

        def thunk(eng, fn=fn, sem=sem, inc=inc):
            fn(eng).then_inc(sem, inc)
        prog[engine].append(thunk)
        return (key, cnt[key])

    def wait(engine, ev):
        if ev is None:
            return
        key, val = ev
        if key in ENG_KEYS[engine]:
            return

        def thunk(eng, key=key, val=val):
            eng.wait_ge(sems[key], val)
        prog[engine].append(thunk)

    def waits(engine, evs):
        best = {}
        for ev in evs:
            if ev is None:
                continue
            kk, vv = ev
            if kk not in ENG_KEYS[engine] and best.get(kk, -1) < vv:
                best[kk] = vv
        for kk, vv in best.items():
            wait(engine, (kk, vv))

    def snapshot():
        return dict(cnt)

    def barrier(snap):
        evs = [(kk, vv) for kk, vv in snap.items()]
        for eng in ("tensor", "vector", "scalar", "gpsimd", "sync"):
            waits(eng, evs)

    ev_pre = []
    for dst, src in [(id_bf[:], idb_h), (id_f32[:], idf_h), (id_f8[:], id8_h),
                     (ones_bf[:], onesb_h),
                     (bias_c[:], bias_h), (ones_sq[:], onessq_h),
                     (atile[:], alog_h.rearrange("n d -> (n d)").rearrange("(p q) -> p q", q=128))]:
        ev_pre.append(emit("sync", lambda e, d=dst, s=src: e.dma_start(out=d, in_=s), inc=16))
    for p in range(NPAT):
        ev_pre.append(emit("sync", lambda e, p=p: e.dma_start(
            out=tri_c[:, p * 128:(p + 1) * 128], in_=tri_h[p * BT:(p + 1) * BT, :]), inc=16))
        ev_pre.append(emit("sync", lambda e, p=p: e.dma_start(
            out=mre_c[:, p * BT:(p + 1) * BT], in_=mre_h[p * 128:(p + 1) * 128, :]), inc=16))

    nc.const_aps.register(F32, 1e-5, bias_c[:, 0:1])
    nc.const_aps.register(F32, 1.0, bias_c[:, 1:2])
    nc.const_aps.register(F32, -DT_C / DT_R, bias_c[:, 2:3])
    waits("scalar", ev_pre)
    emit("scalar", lambda e: e.activation(out=atile[:], in_=atile[:], func=AF.Tanh, scale=0.1))
    ev = emit("scalar", lambda e: e.activation(out=atile[:], in_=atile[:], func=AF.Exp, scale=10.0))
    wait("vector", ev)
    ev_a16 = emit("vector", lambda e: e.tensor_scalar(
        out=a16t[:], in0=atile[:], scalar1=-1.0 / 16.0, scalar2=None, op0=ALU.mult))
    wait("gpsimd", ev_a16)
    ev_a16d = emit("gpsimd", lambda e: e.dma_start(out=a16_d3, in_=a16t[:]), inc=16)
    wait("gpsimd", ev_a16d)
    ev_a16b = emit("gpsimd", lambda e: e.dma_start(
        out=a16b[:],
        in_=a16_d3.rearrange("p q -> (p q)").unsqueeze(0).partition_broadcast(128).squeeze(1)), inc=16)
    waits("gpsimd", ev_pre)
    ev_dbc = emit("gpsimd", lambda e: e.dma_start(
        out=dbc[:], in_=dpar_h.partition_broadcast(128).squeeze(1)), inc=16)
    ev_fbb = emit("gpsimd", lambda e: e.dma_start(
        out=fbb[:], in_=bf_h.partition_broadcast(128).squeeze(1)), inc=16)

    ev_bd = emit("sync", lambda e: e.dma_start(out=x_t[0:1, :], in_=bdt_h), inc=16)
    wait("vector", ev_bd)
    ev_bdt = emit("vector", lambda e: e.tensor_copy(out=bdt_row[:], in_=x_t[0:1, :]))

    if stage >= 2:
        waits("tensor", ev_pre)
        ev_cps = [ev_bdt, ev_bdt]
        for i in range(8):
            waits("sync", [ev_cps[i % 2]])
            ev1 = emit("sync", lambda e, i=i: e.dma_start(out=wsl[i % 2][:], in_=wdt_h[i * 128:(i + 1) * 128, :]), inc=16)
            wait("vector", ev1)
            ev_cps[i % 2] = emit("vector", lambda e, i=i: e.tensor_copy(
                out=wdtT[:, i * D:(i + 1) * D], in_=wsl[i % 2][:]))
        for i in range(8):
            waits("sync", [ev_cps[i % 2]])
            ev1 = emit("sync", lambda e, i=i: e.dma_start(out=wsl[i % 2][:], in_=wf_h[i * 128:(i + 1) * 128, :]), inc=16)
            wait("scalar", ev1)
            ev_cps[i % 2] = emit("scalar", lambda e, i=i: e.copy(
                out=wqT[:, i * D:(i + 1) * D], in_=wsl[i % 2][:]))
        waits("sync", [ev_cps[0], ev_cps[1]])
        ev1 = emit("sync", lambda e: e.dma_start(
            out=wslab[:, 0:8 * N].rearrange("p (i n) -> p i n", n=N),
            in_=wb_h.rearrange("(i p) n -> p i n", p=128)), inc=16)
        ev1b = emit("sync", lambda e: e.dma_start(
            out=wslab[:, 8 * N:16 * N].rearrange("p (i n) -> p i n", n=N),
            in_=wc_h.rearrange("(i p) n -> p i n", p=128)), inc=16)
        waits("vector", [ev1, ev1b])
        emit("vector", lambda e: e.tensor_copy(out=wbT[:], in_=wslab[:, 0:8 * N]))
        ev_cp_prev = emit("vector", lambda e: e.tensor_copy(out=wcT[:], in_=wslab[:, 8 * N:16 * N]))

    emit("vector", lambda e: e.memset(h_t[:], 0.0))
    emit("vector", lambda e: e.memset(h0st[:], 0.0))
    emit("vector", lambda e: e.memset(s_blk[:], 0.0))
    emit("vector", lambda e: e.memset(chE[:], 0.0))
    emit("vector", lambda e: e.memset(chS[:], 0.0))
    if debug:
        emit("vector", lambda e: e.memset(dt2[:], 0.0))
        emit("vector", lambda e: e.memset(bp_full[:], 0.0))
        emit("vector", lambda e: e.memset(cp_full[:], 0.0))
        emit("vector", lambda e: e.memset(sfull[0][:], 0.0))
        emit("vector", lambda e: e.memset(sfull[1][:], 0.0))
        emit("vector", lambda e: e.memset(yt_blk[:], 0.0))
        emit("vector", lambda e: e.memset(xn_t[:], 0.0))
        barrier(snapshot())
    if debug:
        waits("gpsimd", [ev_a16b])
        emit("gpsimd", lambda e: e.dma_start(out=dbg["dbg_a16"], in_=a16b[:]), inc=16)

    a_sq = [None] * NBLK
    a_v2 = [None] * NBLK
    a_rstd = [None] * NBLK
    a_xn = [None] * NBLK
    a_trlast = [None] * NBLK
    a_xnTl = [None] * NBLK
    a_cp = [None] * NBLK
    a_e1 = [[None, None] for _ in range(NBLK)]
    a_e1s = [None] * NBLK
    a_dtf = [None] * NBLK
    NG = NBLK * NSLC
    ev_g = [None] * NG
    ev_garg = [None] * NG
    ev_ca = [None] * NG
    ev_E = [None] * NG
    ev_en = [None] * NG
    ev_r = [None] * NG
    ev_smm = [None] * NG
    ev_evac = [None] * NG
    ev_re = [None] * NG
    ev_stt = [None] * NG
    ev_yadd = [None] * NG
    ev_sq_b = [None] * NBLK
    ev_v2_b = [None] * NBLK
    ev_sr_b = [None] * NBLK
    ev_xn_b = [None] * NBLK
    ev_w_b = [None] * NBLK
    ev_ext_b = [None] * NBLK
    ev_chain_b = [None] * NBLK
    ev_wb_b = [None] * NBLK
    ev_basex_b = [None] * NBLK
    ev_baseff_b = [None] * NBLK
    ev_store_b = [None] * NBLK
    ev_hcarry_b = [None] * NBLK
    ev_ffn_b = [None] * NBLK

    def pe(lst, i):
        return lst[i] if i >= 0 else None

    base2 = [yt_blk, base_b]

    def head_chunks(k):
        t0, bt, regions = blocks[k]
        kb = k % 2
        st = {"tr": [None] * 8, "evv": [None] * 8, "evs": [None] * 8,
              "mmd": [None, None], "sg": [None, None], "mmc": None}
        ch = []

        def c_ln1():
            waits("sync", [pe(a_sq, k - 1), pe(a_xn, k - 1), pe(ev_basex_b, k - 1), ev_bdt])
            evx = emit("sync", lambda e, t0=t0, bt=bt: e.dma_start(
                out=x_t[0:bt, :], in_=x_h[t0:t0 + bt, :]), inc=16)
            wait("vector", evx)
            emit("vector", lambda e, bt=bt: e.tensor_reduce(
                out=mean_c[0:bt, :], in_=x_t[0:bt, :], axis=AX.X, op=ALU.add))
            waits("scalar", [evx, pe(a_v2, k - 1), pe(ev_r, k * NSLC - 1)])
            ev_sq_b[k] = a_sq[k] = emit("scalar", lambda e, bt=bt: e.activation(
                out=w_t[0:bt, :], in_=x_t[0:bt, :], func=AF.Square, accum_out=sq_c[0:bt, :]))
        ch.append(c_ln1)

        def c_ln2():
            waits("vector", [a_sq[k], pe(a_rstd, k - 1)])
            emit("vector", lambda e, bt=bt: e.tensor_scalar(
                out=mean_c[0:bt, :], in0=mean_c[0:bt, :], scalar1=1.0 / D, scalar2=None, op0=ALU.mult))
            emit("vector", lambda e, bt=bt: e.tensor_tensor(
                out=var_c[0:bt, :], in0=mean_c[0:bt, :], in1=mean_c[0:bt, :], op=ALU.mult))
            a_v2[k] = emit("vector", lambda e, bt=bt: e.scalar_tensor_tensor(
                out=var_c[0:bt, :], in0=sq_c[0:bt, :], scalar=1.0 / D, in1=var_c[0:bt, :],
                op0=ALU.mult, op1=ALU.subtract))
            waits("scalar", [a_v2[k], pe(a_xn, k - 1)])
            emit("scalar", lambda e, bt=bt: e.activation(
                out=rstd_c[0:bt, :], in_=var_c[0:bt, :], func=AF.Ln, bias=1e-5))
            a_rstd[k] = emit("scalar", lambda e, bt=bt: e.activation(
                out=rstd_c[0:bt, :], in_=rstd_c[0:bt, :], func=AF.Exp, scale=-0.5))
            waits("vector", [a_rstd[k], pe(a_trlast, k - 1)])
            a_xn[k] = emit("vector", lambda e, bt=bt: e.tensor_scalar(
                out=xn_t[0:bt, :], in0=x_t[0:bt, :], scalar1=mean_c[0:bt, :], scalar2=rstd_c[0:bt, :],
                op0=ALU.subtract, op1=ALU.mult))
        ch.append(c_ln2)

        def c_tr(j):
            if j < 8:
                waits("tensor", [a_xn[k], pe(a_cp, k - 1),
                                 pe(ev_en, k * NSLC - 1) if j < 2 else None,
                                 st["evs"][j - 2] if j >= 2 else None])
                st["tr"][j] = emit("tensor", lambda e, j=j, bt=bt: e.transpose(
                    out=p_ca[j % 2][:, 0:bt], in_=xn_t[0:bt, j * 128:(j + 1) * 128],
                    identity=id_f32[0:bt, 0:bt]))
            if j >= 1:
                jj = j - 1
                waits("scalar", [st["tr"][jj], pe(ev_ffn_b, k - 2) if jj == 0 else None])
                st["evv"][jj] = emit("scalar", lambda e, jj=jj, bt=bt: e.copy(
                    out=xnT_t[:, jj * BT: jj * BT + bt], in_=p_ca[jj % 2][:, 0:bt]))
                st["evs"][jj] = emit("scalar", lambda e, jj=jj, bt=bt, kb=kb: e.copy(
                    out=xqs[kb][:, jj * BT: jj * BT + bt], in_=p_ca[jj % 2][:, 0:bt]))
            if j == 8:
                a_trlast[k] = st["tr"][7]
                a_xnTl[k] = st["evv"][7]
        for j in range(9):
            ch.append(lambda j=j: c_tr(j))

        def c_dtmm(h):
            waits("tensor", [a_xnTl[k],
                             pe(ev_stt, (k - 2) * NSLC + NSLC - 1) if k >= 2 else None,
                             pe(ev_baseff_b, k - 1), pe(a_e1s, k - 1)])
            for j in range(8):
                emit("tensor", lambda e, j=j, h=h, bt=bt: e.matmul(
                    out=p_ffn[h][0:bt, :], lhsT=xnT_t[:, j * BT: j * BT + bt],
                    rhs=wdtT[:, j * D + h * SL: j * D + h * SL + SL],
                    start=(j == 0), stop=False))
            st["mmd"][h] = emit("tensor", lambda e, h=h, bt=bt: e.matmul(
                out=p_ffn[h][0:bt, :], lhsT=ones_bf[:, 0:bt],
                rhs=bdt_row[:, h * SL: h * SL + SL], start=False, stop=True))
        ch.append(lambda: c_dtmm(0))
        ch.append(lambda: c_dtmm(1))

        def c_epi():
            for h in range(2):
                waits("scalar", [st["mmd"][h], pe(a_dtf, k - 1)])
                a_e1[k][h] = emit("scalar", lambda e, h=h, bt=bt: e.activation(
                    out=ep_e[0:bt, h * SL:(h + 1) * SL], in_=p_ffn[h][0:bt, :], func=AF.Exp))
            a_e1s[k] = a_e1[k][1]
            for h in range(2):
                emit("scalar", lambda e, h=h, bt=bt: e.activation(
                    out=ep_e[0:bt, h * SL:(h + 1) * SL], in_=ep_e[0:bt, h * SL:(h + 1) * SL],
                    func=AF.Ln, bias=1.0))
            for h in range(2):
                st["sg"][h] = emit("scalar", lambda e, h=h, bt=bt: e.activation(
                    out=ep_e[0:bt, h * SL:(h + 1) * SL], in_=ep_e[0:bt, h * SL:(h + 1) * SL],
                    func=AF.Sigmoid, scale=1.0 / DT_R, bias=-DT_C / DT_R))
        ch.append(c_epi)

        def c_dt2():
            for h in range(2):
                wait("vector", st["sg"][h])
                a_dtf[k] = emit("vector", lambda e, h=h, kb=kb, bt=bt: e.tensor_scalar(
                    out=dt2[0:bt, kb * D + h * SL: kb * D + h * SL + SL],
                    in0=ep_e[0:bt, h * SL:(h + 1) * SL],
                    scalar1=4.0 * DT_R, scalar2=DT_C - 2.0 * DT_R, op0=ALU.mult, op1=ALU.add))
        ch.append(c_dt2)

        def c_bmm():
            waits("tensor", [a_xnTl[k], st["evs"][6], st["evs"][7]])
            for j in range(8):
                emit("tensor", lambda e, j=j, bt=bt: e.matmul(
                    out=p_ca[0][0:bt, 0:N], lhsT=xnT_t[:, j * BT: j * BT + bt],
                    rhs=wbT[:, j * N:(j + 1) * N], start=(j == 0), stop=(j == 7)))
        ch.append(c_bmm)

        def c_cmm():
            for j in range(8):
                st["mmc"] = emit("tensor", lambda e, j=j, bt=bt: e.matmul(
                    out=p_ca[0][0:bt, N:2 * N], lhsT=xnT_t[:, j * BT: j * BT + bt],
                    rhs=wcT[:, j * N:(j + 1) * N], start=(j == 0), stop=(j == 7)))
            wait("vector", st["mmc"])
            emit("vector", lambda e, k=k, bt=bt: e.tensor_copy(
                out=bp_full[0:bt, k * N:(k + 1) * N], in_=p_ca[0][0:bt, 0:N]))
            a_cp[k] = emit("vector", lambda e, k=k, bt=bt: e.tensor_copy(
                out=cp_full[0:bt, k * N:(k + 1) * N], in_=p_ca[0][0:bt, N:2 * N]))
        ch.append(c_cmm)

        def c_w():
            ev_w_b[k] = emit("vector", lambda e, kb=kb, bt=bt: e.tensor_tensor(
                out=w_t[0:bt, :], in0=xn_t[0:bt, :], in1=dt2[0:bt, kb * D:(kb + 1) * D], op=ALU.mult))
            emit("vector", lambda e, kb=kb, bt=bt: e.memset(sfull[kb][0:bt, :], 1.0))
        ch.append(c_w)

        def c_ffn(h):
            if h == 0:
                waits("tensor", [a_e1[k][0], a_e1[k][1]])
            for j in range(8):
                st["mmc"] = emit("tensor", lambda e, j=j, h=h, bt=bt, kb=kb: e.matmul(
                    out=p_ffn[h][0:bt, :], lhsT=xqs[kb][:, j * BT: j * BT + bt],
                    rhs=wqT[:, j * D + h * SL: j * D + h * SL + SL],
                    start=(j == 0), stop=(j == 7)))
            if h == 1:
                ev_ffn_b[k] = st["mmc"]
        ch.append(lambda: c_ffn(0))
        ch.append(lambda: c_ffn(1))

        def c_base1():
            bb = base2[kb]
            waits("vector", [pe(ev_store_b, k - 2)])
            emit("vector", lambda e, bt=bt, bb=bb: e.tensor_tensor(
                out=bb[0:bt, :], in0=xn_t[0:bt, :], in1=dbc[0:bt, :], op=ALU.mult))
            ev_basex_b[k] = emit("vector", lambda e, bt=bt, bb=bb: e.tensor_tensor(
                out=bb[0:bt, :], in0=bb[0:bt, :], in1=x_t[0:bt, :], op=ALU.add))
        ch.append(c_base1)

        def c_base2():
            bb = base2[kb]
            waits("vector", [ev_ffn_b[k], ev_fbb])
            for h in range(2):
                emit("vector", lambda e, h=h, bt=bt, bb=bb: e.tensor_tensor(
                    out=bb[0:bt, h * SL:(h + 1) * SL], in0=bb[0:bt, h * SL:(h + 1) * SL],
                    in1=p_ffn[h][0:bt, :], op=ALU.add))
            ev_baseff_b[k] = emit("vector", lambda e, bt=bt, bb=bb: e.tensor_tensor(
                out=bb[0:bt, :], in0=bb[0:bt, :], in1=fbb[0:bt, :], op=ALU.add))
        ch.append(c_base2)
        return ch

    def emit_2a_stage(k, i):
        t0, bt, regions = blocks[k]
        pat = pat_of(k)
        g0 = k * NSLC
        if i < NSLC and i % 2 == 0:
            s0, s1 = i, i + 1
            gs0 = g0 + s0
            pb = (s0 >> 1) % 2
            waits("vector", [pe(ev_g, gs0 - 4)])
            ev_garg[gs0] = ev_garg[gs0 + 1] = emit("vector", lambda e, kb=k % 2, bt=bt, s0=s0, pb=pb: e.tensor_tensor(
                out=garg[pb][0:bt, :], in0=dt2[0:bt, kb * D: kb * D + 2 * SL],
                in1=a16b[0:bt, s0 * SL:(s0 + 2) * SL], op=ALU.mult))
        if 0 <= i - 1 < NSLC and (i - 1) % 2 == 0:
            s0 = i - 1
            gs0 = g0 + s0
            pb = (s0 >> 1) % 2
            waits("scalar", [ev_garg[gs0 + 1], pe(ev_ca, gs0 - 3), ev_w_b[k]])
            ev_g[gs0] = ev_g[gs0 + 1] = emit("scalar", lambda e, bt=bt, pb=pb: e.activation(
                out=g_t[pb][0:bt, :], in_=garg[pb][0:bt, :], func=AF.Tanh))
        if 0 <= i - 1 < NSLC:
            s = i - 1
            gs = g0 + s
            waits("tensor", [ev_g[gs], pe(ev_en, gs - 4), a_cp[k] if s < 4 else None])
            ev_ca[gs] = emit("tensor", lambda e, pat=pat, bt=bt, s=s: e.matmul(
                out=p_ca[(s >> 1) % 2][:, (s & 1) * SL:((s & 1) + 1) * SL],
                lhsT=tri_c[0:bt, pat * 128:(pat + 1) * 128],
                rhs=g_t[(s >> 1) % 2][0:bt, (s & 1) * SL:(s & 1) * SL + SL],
                start=True, stop=True))
        if 0 <= i - 2 < NSLC and (i - 2) % 2 == 1:
            s1 = i - 2
            s0 = s1 - 1
            gs1 = g0 + s1
            gs0 = g0 + s0
            pb = (s0 >> 1) % 2
            waits("scalar", [ev_ca[gs1], pe(ev_stt, gs0 - NSLC), pe(ev_stt, gs1 - NSLC), pe(ev_ext_b, k - 1)])
            ev_E[gs0] = ev_E[gs1] = emit("scalar", lambda e, s0=s0, pb=pb: e.activation(
                out=e_blk[:, s0 * SL:(s0 + 2) * SL], in_=p_ca[pb][:], func=AF.Exp, scale=16.0))
            waits("scalar", [pe(ev_r, gs0 - 2)])
            waits("scalar", [pe(ev_r, gs0 - 2)])
            ev_en[gs0] = ev_en[gs1] = emit("scalar", lambda e, bt=bt, pb=pb: e.activation(
                out=en2_t[0:bt, :], in_=p_ca[pb][0:bt, :], func=AF.Exp, scale=-16.0))
        if 0 <= i - 3 < NSLC and (i - 3) % 2 == 1:
            s1 = i - 3
            s0 = s1 - 1
            gs1 = g0 + s1
            gs0 = g0 + s0
            n_i = s0 >> 1
            waits("vector", [ev_en[gs1], pe(ev_smm, gs0 - 2), pe(ev_smm, gs1 - 2)])
            ev_r[gs0] = ev_r[gs1] = emit("vector", lambda e, k=k, bt=bt, n_i=n_i: e.scalar_tensor_tensor(
                out=r2_t[0][0:bt, :], in0=w_t[0:bt, :],
                scalar=bp_full[0:bt, k * N + n_i: k * N + n_i + 1],
                in1=en2_t[0:bt, :], op0=ALU.mult, op1=ALU.mult))
            for s in (s0, s1):
                gs = g0 + s
                waits("tensor", [ev_r[gs], pe(ev_evac, gs - 2)])
                ev_smm[gs] = emit("tensor", lambda e, pat=pat, bt=bt, b=s % 2, s=s: e.matmul(
                    out=p_sp[b][:], lhsT=tri_c[0:bt, pat * 128:(pat + 1) * 128],
                    rhs=r2_t[0][0:bt, (s & 1) * SL:((s & 1) + 1) * SL],
                    start=True, stop=True))
        if 0 <= i - 4 < NSLC:
            s = i - 4
            gs = g0 + s
            b = s % 2
            eng = "scalar" if (s % 2 == 0) else "vector"
            waits(eng, [ev_smm[gs], pe(ev_re, gs - NSLC), pe(ev_ext_b, k - 1)])
            if eng == "scalar":
                ev_evac[gs] = emit("scalar", lambda e, s=s, b=b: e.copy(
                    out=s_blk[:, s * SL:(s + 1) * SL], in_=p_sp[b][:]))
            else:
                ev_evac[gs] = emit("vector", lambda e, s=s, b=b: e.tensor_copy(
                    out=s_blk[:, s * SL:(s + 1) * SL], in_=p_sp[b][:]))

    def emit_2b_stage(k, i):
        t0, bt, regions = blocks[k]
        g0 = k * NSLC
        if i < NSLC:
            s = i
            gs = g0 + s
            waits("tensor", [ev_wb_b[k], ev_evac[gs], pe(ev_stt, gs - 2),
                             ev_baseff_b[k + 1] if k + 1 < NBLK and ev_baseff_b[k + 1] is not None else None])
            ev_re[gs] = emit("tensor", lambda e, k=k, bt=bt, s=s: e.matmul(
                out=p_ffn[s % 2][0:bt, :], lhsT=mre_c[:, pat_of(k) * BT: pat_of(k) * BT + bt],
                rhs=s_blk[:, s * SL:(s + 1) * SL], start=True, stop=True))
        if 0 <= i - 1 < NSLC:
            s = i - 1
            gs = g0 + s
            b = s % 2
            n_i = s >> 1
            dh = (s & 1) * SL
            waits("vector", [ev_re[gs]])
            if s < 2:
                ev_stt[gs] = emit("vector", lambda e, k=k, bt=bt, s=s, n_i=n_i, dh=dh, b=b: e.scalar_tensor_tensor(
                    out=yt_bf[0:bt, dh:dh + SL], in0=e_blk[0:bt, s * SL:(s + 1) * SL],
                    scalar=cp_full[0:bt, k * N + n_i: k * N + n_i + 1], in1=p_ffn[b][0:bt, :],
                    op0=ALU.mult, op1=ALU.mult))
                ev_yadd[gs] = ev_stt[gs]
            else:
                waits("vector", [pe(ev_yadd, gs - 4)])
                ev_stt[gs] = emit("vector", lambda e, k=k, bt=bt, s=s, n_i=n_i, cb=s % 4, b=b: e.scalar_tensor_tensor(
                    out=ces_t[0:bt, cb * SL:(cb + 1) * SL], in0=e_blk[0:bt, s * SL:(s + 1) * SL],
                    scalar=cp_full[0:bt, k * N + n_i: k * N + n_i + 1], in1=p_ffn[b][0:bt, :],
                    op0=ALU.mult, op1=ALU.mult))

                def _yadd(e, bt=bt, dh=dh, cb=s % 4):
                    with nc.allow_low_precision(reason="y partial sums kept in bf16; rvr-checked"):
                        return e.tensor_tensor(
                            out=yt_bf[0:bt, dh:dh + SL], in0=yt_bf[0:bt, dh:dh + SL],
                            in1=ces_t[0:bt, cb * SL:(cb + 1) * SL], op=ALU.add)
                ev_yadd[gs] = emit("vector", _yadd)

    def emit_finalize(k):
        t0, bt, regions = blocks[k]
        bb = base2[k % 2]
        waits("vector", [ev_yadd[k * NSLC + NSLC - 1]])
        ev_out = emit("vector", lambda e, k=k, bt=bt, bb=bb: e.scalar_tensor_tensor(
            out=bb[0:bt, :], in0=yt_bf[0:bt, :], scalar=sfull[k % 2][0:bt, :],
            in1=bb[0:bt, :], op0=ALU.mult, op1=ALU.add))
        wait("sync", ev_out)
        ev_store_b[k] = emit("sync", lambda e, t0=t0, bt=bt, bb=bb: e.dma_start(
            out=out_h[t0:t0 + bt, :], in_=bb[0:bt, :]), inc=16)

    def emit_chain(k, feeder=None):
        def feed(n=1):
            for _ in range(n):
                if feeder:
                    feeder.pop(0)()
        t0, bt, regions = blocks[k]
        nreg = len(regions)
        g0 = k * NSLC
        waits("gpsimd", [ev_E[g0 + NSLC - 1], pe(ev_chain_b, k - 1)])
        emit("gpsimd", lambda e, nreg=nreg: e.dma_start(
            out=chE_d[0:nreg, :], in_=e_blk[114:114 + nreg, :]), inc=16)
        ev_ext = [emit("gpsimd", lambda e, nreg=nreg: e.dma_start(
            out=chE[:, 0:nreg * 128].rearrange("p (c j) -> p c j", j=128),
            in_=chE_d[0:nreg, :].rearrange("c (q j) -> q c j", j=128)), inc=16)]
        waits("sync", [ev_evac[g0 + NSLC - 1], ev_evac[g0 + NSLC - 2], pe(ev_chain_b, k - 1)])
        ev_chS_out = emit("sync", lambda e, nreg=nreg: e.dma_start(
            out=chS_d[0:nreg, :], in_=s_blk[114:114 + nreg, :]), inc=16)
        ev_ext.append(emit("sync", lambda e, nreg=nreg: e.dma_start(
            out=chS[:, 0:nreg * 128].rearrange("p (c j) -> p c j", j=128),
            in_=chS_d[0:nreg, :].rearrange("c (q j) -> q c j", j=128)), inc=16))
        ev_ext_b[k] = ev_ext[0]
        feed(2)
        waits("gpsimd", [pe(ev_hcarry_b, k - 1)])
        ev_h0w = [emit("gpsimd", lambda e: e.tensor_copy(out=h0st[:, 0:128], in_=h_t[:]))]
        waits("vector", ev_ext + [pe(ev_wb_b, k - 1)])
        emit("vector", lambda e: e.tensor_tensor(
            out=ht_tmp[:], in0=h_t[:], in1=chS[:, 0:128], op=ALU.add))
        pend_pool_h0 = None

        ev_wb_last = [None]

        def wb_col(c):
            emit("gpsimd", lambda e, c=c: e.dma_start(
                out=h0_d[c:c + 1, :].rearrange("c (q j) -> q c j", j=128),
                in_=h0st[:, c * 128:(c + 1) * 128].rearrange("p (c j) -> p c j", j=128)), inc=16)
            return emit("gpsimd", lambda e, c=c: e.dma_start(
                out=s_blk[114 + c:115 + c, :], in_=h0_d[c:c + 1, :]), inc=16)

        waits("gpsimd", [ev_h0w[0], ev_evac[g0 + NSLC - 1], ev_evac[g0 + NSLC - 2], ev_chS_out])
        ev_wb_last[0] = wb_col(0)
        for c, (lo, hi, clamps) in enumerate(regions):
            feed(2)
            last = (c == nreg - 1)
            waits("vector", [pend_pool_h0])
            pend_pool_h0 = None
            if clamps:
                emit("vector", lambda e, c=c: e.scalar_tensor_tensor(
                    out=ht_new[:], in0=chE[:, c * 128:(c + 1) * 128], scalar=1.0,
                    in1=ht_tmp[:], op0=ALU.mult, op1=ALU.mult))
                ev_q = emit("vector", lambda e: e.scalar_tensor_tensor(
                    out=ht_tmp[:], in0=ht_new[:], scalar=1.0, in1=ht_new[:],
                    op0=ALU.mult, op1=ALU.mult, accum_out=qcol[:]))
                wait("tensor", ev_q)
                ev_qs = emit("tensor", lambda e: e.matmul(
                    out=p_s, lhsT=ones_sq[:], rhs=qcol[:], start=True, stop=True))
                wait("scalar", ev_qs)
                ev_rt = emit("scalar", lambda e: e.activation(
                    out=scol[:], in_=p_s, func=AF.Sqrt))
                wait("vector", ev_rt)
                emit("vector", lambda e: e.reciprocal(out=scol[:], in_=scol[:]))
                emit("vector", lambda e: e.tensor_scalar(
                    out=scol[:], in0=scol[:], scalar1=20.0, scalar2=1.0, op0=ALU.mult, op1=ALU.min))
                if last:
                    waits("vector", [ev_h0w[0]])
                    ev_hcarry_b[k] = emit("vector", lambda e: e.tensor_scalar(
                        out=h_t[:], in0=ht_new[:], scalar1=scol[:], scalar2=None, op0=ALU.mult))
                else:
                    emit("vector", lambda e, c=c: e.scalar_tensor_tensor(
                        out=ht_tmp[:], in0=ht_new[:], scalar=scol[:],
                        in1=chS[:, (c + 1) * 128:(c + 2) * 128], op0=ALU.mult, op1=ALU.add))
                    ev_h0w.append(emit("vector", lambda e, c=c: e.tensor_scalar(
                        out=h0st[:, (c + 1) * 128:(c + 2) * 128], in0=ht_new[:],
                        scalar1=scol[:], scalar2=None, op0=ALU.mult)))
                emit("vector", lambda e: e.tensor_scalar(
                    out=sm1[:], in0=scol[:], scalar1=-1.0, scalar2=None, op0=ALU.add))
                ev_chain_b[k] = emit("vector", lambda e, hi=hi, bt=bt, k=k: e.scalar_tensor_tensor(
                    out=sfull[k % 2][0:bt, :], in0=id_f32[0:bt, hi:hi + 1], scalar=sm1[0:bt, :],
                    in1=sfull[k % 2][0:bt, :], op0=ALU.mult, op1=ALU.add))
                if not last:
                    waits("gpsimd", [ev_h0w[-1]])
                    ev_wb_last[0] = wb_col(c + 1)
            else:
                if last:
                    waits("vector", [ev_h0w[0]])
                    ev_chain_b[k] = ev_hcarry_b[k] = emit("vector", lambda e, c=c: e.tensor_tensor(
                        out=h_t[:], in0=chE[:, c * 128:(c + 1) * 128], in1=ht_tmp[:], op=ALU.mult))
                else:
                    ev_hn = emit("vector", lambda e, c=c: e.tensor_tensor(
                        out=ht_new[:], in0=chE[:, c * 128:(c + 1) * 128], in1=ht_tmp[:], op=ALU.mult))
                    ev_chain_b[k] = emit("vector", lambda e, c=c: e.tensor_tensor(
                        out=ht_tmp[:], in0=ht_new[:], in1=chS[:, (c + 1) * 128:(c + 2) * 128], op=ALU.add))
                    waits("gpsimd", [ev_chain_b[k]])
                    pend_pool_h0 = emit("gpsimd", lambda e, c=c: e.tensor_copy(
                        out=h0st[:, (c + 1) * 128:(c + 2) * 128], in_=ht_new[:]))
                    ev_h0w.append(pend_pool_h0)
                    ev_wb_last[0] = wb_col(c + 1)
        ev_wb_b[k] = ev_wb_last[0]

    barrier(snapshot())
    NBL = len(blocks)
    if stage >= 4:
        for c in head_chunks(0):
            c()
        for k in range(NBL):
            for i in range(NSLC + 5):
                emit_2a_stage(k, i)
                if k >= 1 and stage >= 6:
                    emit_2b_stage(k - 1, i)
            if k >= 1 and stage >= 6:
                emit_finalize(k - 1)
            hc = head_chunks(k + 1) if k + 1 < NBL else []
            if stage >= 5:
                emit_chain(k, hc)
            for c in hc:
                c()
            if debug and k == 0:
                barrier(snapshot())
                emit("gpsimd", lambda e: e.dma_start(out=dbg["dbg_E0"], in_=e_blk[:]), inc=16)
                emit("gpsimd", lambda e: e.dma_start(out=dbg["dbg_S0"], in_=s_blk[:]), inc=16)
                emit("gpsimd", lambda e: e.dma_start(out=dbg["dbg_h0s"], in_=h0st[:]), inc=16)
                emit("gpsimd", lambda e: e.dma_start(out=dbg["dbg_xn0"], in_=xn_t[0:BT, :]), inc=16)
                barrier(snapshot())
        if stage >= 6:
            for i in range(NSLC + 2):
                emit_2b_stage(NBL - 1, i)
            emit_finalize(NBL - 1)
            if debug:
                barrier(snapshot())
                emit("gpsimd", lambda e: e.dma_start(out=dbg["dbg_y0"], in_=base2[(NBL - 1) % 2][0:BT, :]), inc=16)
                emit("gpsimd", lambda e, k=0: e.dma_start(out=dbg["dbg_sfull"], in_=sfull[0][:]), inc=16)
                barrier(snapshot())

    if debug:
        barrier(snapshot())
        emit("gpsimd", lambda e: e.dma_start(out=dbg["dbg_bp"], in_=bp_full[:]), inc=16)
        emit("gpsimd", lambda e: e.dma_start(out=dbg["dbg_cp"], in_=cp_full[:]), inc=16)

    def thunk_final_d(eng):
        eng.wait_ge(sem_d, cnt["d"])
    prog["sync"].append(thunk_final_d)

    def thunk_final_g(eng):
        eng.wait_ge(sem_g, cnt["g"])
        eng.wait_ge(sem_p, cnt["p"])
    prog["gpsimd"].append(thunk_final_g)

    with es, nc.Block() as block:
        @block.sync
        def _(eng):
            for f in prog["sync"]:
                f(eng)

        @block.gpsimd
        def _(eng):
            for f in prog["gpsimd"]:
                f(eng)

        @block.tensor
        def _(eng):
            for f in prog["tensor"]:
                f(eng)

        @block.vector
        def _(eng):
            for f in prog["vector"]:
                f(eng)

        @block.scalar
        def _(eng):
            for f in prog["scalar"]:
                f(eng)

    return nc


def kernel(L0_attn_x, L0_attn_W_dt, L0_attn_b_dt, L0_attn_W_B, L0_attn_W_C,
           L0_attn_A_log, L0_attn_D_param, L0_ffn_weight, L0_ffn_bias):
    _, consts = build_consts()
    nc = build_nc(debug=False)
    base = {
        "W_dtT": np.ascontiguousarray(np.asarray(L0_attn_W_dt, dtype=np.float32).T),
        "b_dt": np.ascontiguousarray(L0_attn_b_dt, dtype=np.float32).reshape(1, D),
        "W_BT": np.ascontiguousarray(np.asarray(L0_attn_W_B, dtype=np.float32).T),
        "W_CT": np.ascontiguousarray(np.asarray(L0_attn_W_C, dtype=np.float32).T),
        "A_logT": np.ascontiguousarray(L0_attn_A_log.T, dtype=np.float32),
        "D_param": np.ascontiguousarray(L0_attn_D_param, dtype=np.float32).reshape(1, D),
        "W_ffnT": np.ascontiguousarray(np.asarray(L0_ffn_weight, dtype=np.float32).T),
        "b_ffn": np.ascontiguousarray(L0_ffn_bias, dtype=np.float32).reshape(1, D),
    }
    base.update(consts)
    x = np.ascontiguousarray(L0_attn_x, dtype=np.float32)
    in_maps = []
    for i in range(8):
        m = dict(base)
        m["x"] = x[i]
        in_maps.append(m)
    res = run_bass_kernel_spmd(nc, in_maps, core_ids=list(range(8)))
    out = np.stack([np.asarray(r["out"]) for r in res.results], axis=0).astype(np.float32)
    return out


# version 19
# speedup vs baseline: 1.1793x; 1.0031x over previous
import contextlib
import numpy as np
import ml_dtypes
import concourse.bass as bass
import concourse.mybir as mybir
from concourse.bass_utils import run_bass_kernel_spmd

F32 = mybir.dt.float32
BF16 = mybir.dt.bfloat16
FP8 = mybir.dt.float8e4
AF = mybir.ActivationFunctionType
ALU = mybir.AluOpType
AX = mybir.AxisListType

BFD = ml_dtypes.bfloat16
F8D = ml_dtypes.float8_e4m3fn

L = 2048
D = 1024
N = 16
DN = D * N
BT = 114
NBLK = 18
NSLC = 32
SL = 512
RMAX = 13
NPAT = 6

DT_C = (0.1 + 0.001) / 2.0
DT_R = (0.1 - 0.001) / 2.0


def block_structure():
    blocks = []
    for k in range(NBLK):
        t0 = k * BT
        bt = min(BT, L - t0)
        regions = []
        lo = 0
        for r in range(1, bt + 1):
            t = t0 + r
            if r == bt or t % 10 == 1:
                hi = r - 1
                clamps = ((t0 + hi) % 10 == 0)
                regions.append((lo, hi, clamps))
                lo = r
        blocks.append((t0, bt, regions))
    return blocks


def pat_of(k):
    return 5 if k == NBLK - 1 else k % 5


def build_consts():
    blocks = block_structure()
    tri = np.zeros((NPAT, BT, 128), dtype=np.float32)
    mre = np.zeros((NPAT, 128, BT), dtype=np.float32)
    done = set()
    for k, (t0, bt, regions) in enumerate(blocks):
        p = pat_of(k)
        if p in done:
            continue
        done.add(p)
        for ci, (lo, hi, cl) in enumerate(regions):
            for tp in range(lo, hi + 1):
                tri[p, lo:tp + 1, tp] = 1.0
                tri[p, tp, 114 + ci] = 1.0
                mre[p, tp, tp] = 1.0
                mre[p, 114 + ci, tp] = 1.0
    ident = np.eye(128, dtype=np.float32)
    c = {
        "c_tri": tri.reshape(NPAT * BT, 128).astype(BFD),
        "c_mre": mre.reshape(NPAT * 128, BT).astype(BFD),
        "c_id_bf": ident.astype(BFD),
        "c_id_f32": ident.copy(),
        "c_id_f8": ident.astype(F8D),
        "c_ones_bf": np.ones((1, BT), dtype=np.float32).astype(BFD),
        "c_ones_row": np.ones((1, 128), dtype=np.float32),
        "c_ones_p": np.ones((128, 1), dtype=np.float32),
        "c_ones_sq": np.ones((128, 128), dtype=np.float32),
        "c_bias": np.tile(np.array([[1e-5, 1.0, -DT_C / DT_R]], dtype=np.float32), (128, 1)),
    }
    return blocks, c


def build_nc(debug=False, nblk=NBLK, stage=9):
    blocks, _ = build_consts()
    blocks = blocks[:nblk]
    nc = bass.Bass("TRN2", target_bir_lowering=False)

    x_h = nc.dram_tensor("x", [L, D], F32, kind="ExternalInput").ap()
    wdt_h = nc.dram_tensor("W_dtT", [D, D], F32, kind="ExternalInput").ap()
    bdt_h = nc.dram_tensor("b_dt", [1, D], F32, kind="ExternalInput").ap()
    wb_h = nc.dram_tensor("W_BT", [D, N], F32, kind="ExternalInput").ap()
    wc_h = nc.dram_tensor("W_CT", [D, N], F32, kind="ExternalInput").ap()
    alog_h = nc.dram_tensor("A_logT", [N, D], F32, kind="ExternalInput").ap()
    dpar_h = nc.dram_tensor("D_param", [1, D], F32, kind="ExternalInput").ap()
    wf_h = nc.dram_tensor("W_ffnT", [D, D], F32, kind="ExternalInput").ap()
    bf_h = nc.dram_tensor("b_ffn", [1, D], F32, kind="ExternalInput").ap()
    tri_h = nc.dram_tensor("c_tri", [NPAT * BT, 128], BF16, kind="ExternalInput").ap()
    mre_h = nc.dram_tensor("c_mre", [NPAT * 128, BT], BF16, kind="ExternalInput").ap()
    idb_h = nc.dram_tensor("c_id_bf", [128, 128], BF16, kind="ExternalInput").ap()
    idf_h = nc.dram_tensor("c_id_f32", [128, 128], F32, kind="ExternalInput").ap()
    id8_h = nc.dram_tensor("c_id_f8", [128, 128], FP8, kind="ExternalInput").ap()
    onesb_h = nc.dram_tensor("c_ones_bf", [1, BT], BF16, kind="ExternalInput").ap()
    onesr_h = nc.dram_tensor("c_ones_row", [1, 128], F32, kind="ExternalInput").ap()
    onesp_h = nc.dram_tensor("c_ones_p", [128, 1], F32, kind="ExternalInput").ap()
    bias_h = nc.dram_tensor("c_bias", [128, 3], F32, kind="ExternalInput").ap()
    onessq_h = nc.dram_tensor("c_ones_sq", [128, 128], F32, kind="ExternalInput").ap()
    a16_d3 = nc.dram_tensor("a16_scratch3", [128, 128], BF16, kind="Internal").ap()
    chE_d = nc.dram_tensor("chE_scratch", [RMAX, DN], BF16, kind="Internal").ap()
    chS_d = nc.dram_tensor("chS_scratch", [RMAX, DN], BF16, kind="Internal").ap()
    h0_d = nc.dram_tensor("h0_scratch", [RMAX, DN], BF16, kind="Internal").ap()
    xq_d = nc.dram_tensor("xq_scratch", [128, NBLK * 8 * BT], FP8, kind="Internal").ap()

    out_h = nc.dram_tensor("out", [L, D], F32, kind="ExternalOutput").ap()
    dbg = {}
    if debug:
        for nm, shp, dt_ in [("dbg_E0", [128, DN], F32),
                             ("dbg_S0", [128, DN], F32), ("dbg_y0", [BT, D], F32),
                             ("dbg_h0s", [128, RMAX * 128], F32),
                             ("dbg_sfull", [BT, 1], F32), ("dbg_bp", [128, NBLK * N], F32),
                             ("dbg_cp", [128, NBLK * N], F32), ("dbg_xn0", [BT, D], F32),
                             ("dbg_a16", [128, DN], F32)]:
            dbg[nm] = nc.dram_tensor(nm, shp, dt_, kind="ExternalOutput").ap()

    es = contextlib.ExitStack()

    def sb(name, shape, dtype):
        return es.enter_context(nc.sbuf_tensor(name, shape, dtype))

    def ps(name, shape, dtype):
        return es.enter_context(nc.psum_tensor(name, shape, dtype))

    dt2 = sb("dt2", [128, 2 * D], BF16)
    wdtT = sb("wdtT", [128, 8 * D], BF16)
    xnT_t = sb("xnT_t", [128, 8 * BT], BF16)
    ep_e = sb("ep_e", [128, 2 * SL], F32)
    wbT = sb("wbT", [128, 8 * N], BF16)
    wcT = sb("wcT", [128, 8 * N], BF16)
    bdt_row = sb("bdt_row", [1, D], BF16)
    fbb_row = sb("fbb_row", [1, D], BF16)
    xqs = [sb(f"xqs{i}", [128, 8 * BT], FP8) for i in range(2)]
    a16b = sb("a16b", [128, DN], BF16)
    s_blk = sb("s_blk", [128, DN], BF16)
    wqT = sb("wqT", [128, 8 * D], FP8)
    tri_c = sb("tri_c", [BT, NPAT * 128], BF16)
    mre_c = sb("mre_c", [128, NPAT * BT], BF16)
    bp_full = sb("bp_full", [128, NBLK * N], BF16)
    cp_full = sb("cp_full", [128, NBLK * N], F32)
    dbc = sb("dbc", [128, D], BF16)
    fbb = sb("fbb", [128, D], BF16)
    id_bf = sb("id_bf", [128, 128], BF16)
    id_f32 = sb("id_f32", [128, 128], F32)
    id_f8 = sb("id_f8", [128, 128], FP8)
    ones_bf = sb("ones_bf", [1, BT], BF16)
    ones_sq = sb("ones_sq", [128, 128], F32)
    bias_c = sb("bias_c", [128, 3], F32)
    chE = sb("chE", [128, RMAX * 128], BF16)
    chS = sb("chS", [128, RMAX * 128], BF16)
    h0st = sb("h0st", [128, RMAX * 128], BF16)
    h_t = sb("h_t", [128, 128], F32)
    ht_tmp = sb("ht_tmp", [128, 128], F32)
    ht_new = sb("ht_new", [128, 128], F32)
    sfull = [sb(f"sfull{i}", [BT, 1], F32) for i in range(2)]
    qcol = sb("qcol", [128, 1], F32)
    scol = sb("scol", [128, 1], F32)
    sm1 = sb("sm1", [128, 1], F32)
    x_t = sb("x_t", [128, D], F32)
    xn_t = sb("xn_t", [128, D], F32)
    w_t = sb("w_t", [128, D], BF16)
    mean_c = sb("mean_c", [128, 1], F32)
    sq_c = sb("sq_c", [128, 1], F32)
    var_c = sb("var_c", [128, 1], F32)
    rstd_c = sb("rstd_c", [128, 1], F32)
    garg = [sb(f"garg{i}", [BT, 2 * SL], BF16) for i in range(2)]
    g_t = [sb(f"g_t{i}", [BT, 2 * SL], BF16) for i in range(2)]
    en2_t = sb("en2_t", [BT, 2 * SL], BF16)
    r2_t = [sb(f"r2_t{i}", [BT, 2 * SL], BF16) for i in range(1)]
    ces_t = sb("ces_t", [BT, 4 * SL], BF16)
    yt_bf = sb("yt_bf", [BT, D], BF16)
    yt_blk = sb("yt_blk", [BT, D], F32)
    base_b = sb("base_b", [BT, D], F32)

    off0 = (nc.sbuf_base + 31) & ~31
    arena = sb("arena", [128, 16384], BF16)

    def at(name, shape, dtype, byte_off):
        return nc.alloc_sbuf_tensor_at(name, shape, dtype, offset=off0 + byte_off)

    e_blk = arena
    wslab = at("wslab", [128, D], F32, 16384)
    wslabB = at("wslabB", [128, D], F32, 20480)
    wsl = [wslab, wslabB]
    atile = at("atile", [128, 128], F32, 0)
    a16t = at("a16t", [128, 128], BF16, 512)

    p_ca = [ps(f"p_ca{i}", [128, 2 * SL], F32) for i in range(2)]
    p_sp = [ps(f"p_sp{i}", [128, SL], F32) for i in range(2)]
    p_ffn = [ps(f"p_ffn{i}", [128, SL], F32) for i in range(2)]
    p_trb = [p_ca[i][:].bitcast(BF16)[:, 0:128] for i in range(2)]
    p_tr8 = [p_ca[i][:].bitcast(FP8)[:, 512:768].rearrange("p (j t) -> p j t", t=2)[:, :, 0:1].squeeze(2) for i in range(2)]
    p_s = p_sp[0][:, 300:301]

    sem_d = es.enter_context(nc.semaphore("sem_d"))
    sem_g = es.enter_context(nc.semaphore("sem_g"))
    sem_p = es.enter_context(nc.semaphore("sem_p"))
    sem_t = es.enter_context(nc.semaphore("sem_t"))
    sem_v = es.enter_context(nc.semaphore("sem_v"))
    sem_s = es.enter_context(nc.semaphore("sem_s"))

    cnt = {"d": 0, "g": 0, "p": 0, "t": 0, "v": 0, "s": 0}
    sems = {"d": sem_d, "g": sem_g, "p": sem_p, "t": sem_t, "v": sem_v, "s": sem_s}
    prog = {"sync": [], "gpsimd": [], "tensor": [], "vector": [], "scalar": []}
    ENG_SEM = {"sync": "d", "gpsimd": "g", "tensor": "t", "vector": "v", "scalar": "s"}
    ENG_KEYS = {"sync": ("d",), "gpsimd": ("g", "p"), "tensor": ("t",),
                "vector": ("v",), "scalar": ("s",)}

    def emit(engine, fn, inc=1):
        if engine == "gpsimd":
            key = "g" if inc == 16 else "p"
        else:
            key = ENG_SEM[engine]
        befores = [(kk, cnt[kk]) for kk in ENG_KEYS[engine]]
        cnt[key] += inc
        sem = sems[key]
        if engine != "tensor":
            for kk, before in befores:
                if before > 0:
                    def wthunk(eng, sem=sems[kk], before=before):
                        eng.wait_ge(sem, before)
                    prog[engine].append(wthunk)

        def thunk(eng, fn=fn, sem=sem, inc=inc):
            fn(eng).then_inc(sem, inc)
        prog[engine].append(thunk)
        return (key, cnt[key])

    def wait(engine, ev):
        if ev is None:
            return
        key, val = ev
        if key in ENG_KEYS[engine]:
            return

        def thunk(eng, key=key, val=val):
            eng.wait_ge(sems[key], val)
        prog[engine].append(thunk)

    def waits(engine, evs):
        best = {}
        for ev in evs:
            if ev is None:
                continue
            kk, vv = ev
            if kk not in ENG_KEYS[engine] and best.get(kk, -1) < vv:
                best[kk] = vv
        for kk, vv in best.items():
            wait(engine, (kk, vv))

    def snapshot():
        return dict(cnt)

    def barrier(snap):
        evs = [(kk, vv) for kk, vv in snap.items()]
        for eng in ("tensor", "vector", "scalar", "gpsimd", "sync"):
            waits(eng, evs)

    ev_pre = []
    for dst, src in [(id_bf[:], idb_h), (id_f32[:], idf_h), (id_f8[:], id8_h),
                     (ones_bf[:], onesb_h),
                     (bias_c[:], bias_h), (ones_sq[:], onessq_h),
                     (atile[:], alog_h.rearrange("n d -> (n d)").rearrange("(p q) -> p q", q=128))]:
        ev_pre.append(emit("sync", lambda e, d=dst, s=src: e.dma_start(out=d, in_=s), inc=16))
    for p in range(NPAT):
        ev_pre.append(emit("sync", lambda e, p=p: e.dma_start(
            out=tri_c[:, p * 128:(p + 1) * 128], in_=tri_h[p * BT:(p + 1) * BT, :]), inc=16))
        ev_pre.append(emit("sync", lambda e, p=p: e.dma_start(
            out=mre_c[:, p * BT:(p + 1) * BT], in_=mre_h[p * 128:(p + 1) * 128, :]), inc=16))

    nc.const_aps.register(F32, 1e-5, bias_c[:, 0:1])
    nc.const_aps.register(F32, 1.0, bias_c[:, 1:2])
    nc.const_aps.register(F32, -DT_C / DT_R, bias_c[:, 2:3])
    waits("scalar", ev_pre)
    emit("scalar", lambda e: e.activation(out=atile[:], in_=atile[:], func=AF.Tanh, scale=0.1))
    ev = emit("scalar", lambda e: e.activation(out=atile[:], in_=atile[:], func=AF.Exp, scale=10.0))
    wait("vector", ev)
    ev_a16 = emit("vector", lambda e: e.tensor_scalar(
        out=a16t[:], in0=atile[:], scalar1=-1.0 / 16.0, scalar2=None, op0=ALU.mult))
    wait("gpsimd", ev_a16)
    ev_a16d = emit("gpsimd", lambda e: e.dma_start(out=a16_d3, in_=a16t[:]), inc=16)
    wait("gpsimd", ev_a16d)
    ev_a16b = emit("gpsimd", lambda e: e.dma_start(
        out=a16b[:],
        in_=a16_d3.rearrange("p q -> (p q)").unsqueeze(0).partition_broadcast(128).squeeze(1)), inc=16)
    waits("gpsimd", ev_pre)
    ev_dbc = emit("gpsimd", lambda e: e.dma_start(
        out=dbc[:], in_=dpar_h.partition_broadcast(128).squeeze(1)), inc=16)
    ev_fbb = emit("gpsimd", lambda e: e.dma_start(
        out=fbb[:], in_=bf_h.partition_broadcast(128).squeeze(1)), inc=16)

    ev_bd = emit("sync", lambda e: e.dma_start(out=x_t[0:1, :], in_=bdt_h), inc=16)
    wait("vector", ev_bd)
    ev_bdt0 = emit("vector", lambda e: e.tensor_copy(out=bdt_row[:], in_=x_t[0:1, :]))
    wait("sync", ev_bdt0)
    ev_bd2 = emit("sync", lambda e: e.dma_start(out=x_t[0:1, :], in_=bf_h), inc=16)
    wait("vector", ev_bd2)
    ev_bdt = emit("vector", lambda e: e.tensor_copy(out=fbb_row[:], in_=x_t[0:1, :]))

    if stage >= 2:
        waits("tensor", ev_pre)
        ev_cps = [ev_bdt, ev_bdt]
        for i in range(8):
            waits("sync", [ev_cps[i % 2]])
            ev1 = emit("sync", lambda e, i=i: e.dma_start(out=wsl[i % 2][:], in_=wdt_h[i * 128:(i + 1) * 128, :]), inc=16)
            wait("vector", ev1)
            ev_cps[i % 2] = emit("vector", lambda e, i=i: e.tensor_copy(
                out=wdtT[:, i * D:(i + 1) * D], in_=wsl[i % 2][:]))
        for i in range(8):
            waits("sync", [ev_cps[i % 2]])
            ev1 = emit("sync", lambda e, i=i: e.dma_start(out=wsl[i % 2][:], in_=wf_h[i * 128:(i + 1) * 128, :]), inc=16)
            wait("scalar", ev1)
            ev_cps[i % 2] = emit("scalar", lambda e, i=i: e.copy(
                out=wqT[:, i * D:(i + 1) * D], in_=wsl[i % 2][:]))
        waits("sync", [ev_cps[0], ev_cps[1]])
        ev1 = emit("sync", lambda e: e.dma_start(
            out=wslab[:, 0:8 * N].rearrange("p (i n) -> p i n", n=N),
            in_=wb_h.rearrange("(i p) n -> p i n", p=128)), inc=16)
        ev1b = emit("sync", lambda e: e.dma_start(
            out=wslab[:, 8 * N:16 * N].rearrange("p (i n) -> p i n", n=N),
            in_=wc_h.rearrange("(i p) n -> p i n", p=128)), inc=16)
        waits("vector", [ev1, ev1b])
        emit("vector", lambda e: e.tensor_copy(out=wbT[:], in_=wslab[:, 0:8 * N]))
        ev_cp_prev = emit("vector", lambda e: e.tensor_copy(out=wcT[:], in_=wslab[:, 8 * N:16 * N]))

    emit("vector", lambda e: e.memset(h_t[:], 0.0))
    emit("vector", lambda e: e.memset(h0st[:], 0.0))
    emit("vector", lambda e: e.memset(s_blk[:], 0.0))
    emit("vector", lambda e: e.memset(chE[:], 0.0))
    emit("vector", lambda e: e.memset(chS[:], 0.0))
    if debug:
        emit("vector", lambda e: e.memset(dt2[:], 0.0))
        emit("vector", lambda e: e.memset(bp_full[:], 0.0))
        emit("vector", lambda e: e.memset(cp_full[:], 0.0))
        emit("vector", lambda e: e.memset(sfull[0][:], 0.0))
        emit("vector", lambda e: e.memset(sfull[1][:], 0.0))
        emit("vector", lambda e: e.memset(yt_blk[:], 0.0))
        emit("vector", lambda e: e.memset(xn_t[:], 0.0))
        barrier(snapshot())
    if debug:
        waits("gpsimd", [ev_a16b])
        emit("gpsimd", lambda e: e.dma_start(out=dbg["dbg_a16"], in_=a16b[:]), inc=16)

    a_sq = [None] * NBLK
    a_v2 = [None] * NBLK
    a_rstd = [None] * NBLK
    a_xn = [None] * NBLK
    a_trlast = [None] * NBLK
    a_xnTl = [None] * NBLK
    a_cp = [None] * NBLK
    a_e1 = [[None, None] for _ in range(NBLK)]
    a_e1s = [None] * NBLK
    a_dtf = [None] * NBLK
    NG = NBLK * NSLC
    ev_g = [None] * NG
    ev_garg = [None] * NG
    ev_ca = [None] * NG
    ev_E = [None] * NG
    ev_en = [None] * NG
    ev_r = [None] * NG
    ev_smm = [None] * NG
    ev_evac = [None] * NG
    ev_re = [None] * NG
    ev_stt = [None] * NG
    ev_yadd = [None] * NG
    ev_sq_b = [None] * NBLK
    ev_v2_b = [None] * NBLK
    ev_sr_b = [None] * NBLK
    ev_xn_b = [None] * NBLK
    ev_w_b = [None] * NBLK
    ev_ext_b = [None] * NBLK
    ev_chain_b = [None] * NBLK
    ev_wb_b = [None] * NBLK
    ev_basex_b = [None] * NBLK
    ev_baseff_b = [None] * NBLK
    ev_store_b = [None] * NBLK
    ev_hcarry_b = [None] * NBLK
    ev_ffn_b = [None] * NBLK

    def pe(lst, i):
        return lst[i] if i >= 0 else None

    base2 = [yt_blk, base_b]

    def head_chunks(k):
        t0, bt, regions = blocks[k]
        kb = k % 2
        st = {"tr": [None] * 8, "evv": [None] * 8, "evs": [None] * 8,
              "mmd": [None, None], "sg": [None, None], "mmc": None}
        ch = []

        def c_ln1():
            waits("sync", [pe(a_sq, k - 1), pe(a_xn, k - 1), pe(ev_basex_b, k - 1), ev_bdt])
            evx = emit("sync", lambda e, t0=t0, bt=bt: e.dma_start(
                out=x_t[0:bt, :], in_=x_h[t0:t0 + bt, :]), inc=16)
            wait("vector", evx)
            emit("vector", lambda e, bt=bt: e.tensor_reduce(
                out=mean_c[0:bt, :], in_=x_t[0:bt, :], axis=AX.X, op=ALU.add))
            waits("scalar", [evx, pe(a_v2, k - 1), pe(ev_r, k * NSLC - 1)])
            ev_sq_b[k] = a_sq[k] = emit("scalar", lambda e, bt=bt: e.activation(
                out=w_t[0:bt, :], in_=x_t[0:bt, :], func=AF.Square, accum_out=sq_c[0:bt, :]))
        ch.append(c_ln1)

        def c_ln2():
            waits("vector", [a_sq[k], pe(a_rstd, k - 1)])
            emit("vector", lambda e, bt=bt: e.tensor_scalar(
                out=mean_c[0:bt, :], in0=mean_c[0:bt, :], scalar1=1.0 / D, scalar2=None, op0=ALU.mult))
            emit("vector", lambda e, bt=bt: e.tensor_tensor(
                out=var_c[0:bt, :], in0=mean_c[0:bt, :], in1=mean_c[0:bt, :], op=ALU.mult))
            a_v2[k] = emit("vector", lambda e, bt=bt: e.scalar_tensor_tensor(
                out=var_c[0:bt, :], in0=sq_c[0:bt, :], scalar=1.0 / D, in1=var_c[0:bt, :],
                op0=ALU.mult, op1=ALU.subtract))
            waits("scalar", [a_v2[k], pe(a_xn, k - 1)])
            emit("scalar", lambda e, bt=bt: e.activation(
                out=rstd_c[0:bt, :], in_=var_c[0:bt, :], func=AF.Ln, bias=1e-5))
            a_rstd[k] = emit("scalar", lambda e, bt=bt: e.activation(
                out=rstd_c[0:bt, :], in_=rstd_c[0:bt, :], func=AF.Exp, scale=-0.5))
            waits("vector", [a_rstd[k], pe(a_trlast, k - 1)])
            a_xn[k] = emit("vector", lambda e, bt=bt: e.tensor_scalar(
                out=xn_t[0:bt, :], in0=x_t[0:bt, :], scalar1=mean_c[0:bt, :], scalar2=rstd_c[0:bt, :],
                op0=ALU.subtract, op1=ALU.mult))
        ch.append(c_ln2)

        def c_tr(j):
            if j < 8:
                waits("tensor", [a_xn[k], pe(a_cp, k - 1),
                                 pe(ev_en, k * NSLC - 1) if j < 2 else None,
                                 st["evs"][j - 2] if j >= 2 else None])
                st["tr"][j] = emit("tensor", lambda e, j=j, bt=bt: e.transpose(
                    out=p_ca[j % 2][:, 0:bt], in_=xn_t[0:bt, j * 128:(j + 1) * 128],
                    identity=id_f32[0:bt, 0:bt]))
            if j >= 1:
                jj = j - 1
                waits("scalar", [st["tr"][jj], pe(ev_ffn_b, k - 2) if jj == 0 else None])
                st["evv"][jj] = emit("scalar", lambda e, jj=jj, bt=bt: e.copy(
                    out=xnT_t[:, jj * BT: jj * BT + bt], in_=p_ca[jj % 2][:, 0:bt]))
                st["evs"][jj] = emit("scalar", lambda e, jj=jj, bt=bt, kb=kb: e.copy(
                    out=xqs[kb][:, jj * BT: jj * BT + bt], in_=p_ca[jj % 2][:, 0:bt]))
            if j == 8:
                a_trlast[k] = st["tr"][7]
                a_xnTl[k] = st["evv"][7]
        for j in range(9):
            ch.append(lambda j=j: c_tr(j))

        def c_dtmm(h):
            waits("tensor", [a_xnTl[k],
                             pe(ev_stt, (k - 2) * NSLC + NSLC - 1) if k >= 2 else None,
                             pe(ev_baseff_b, k - 1), pe(a_e1s, k - 1)])
            for j in range(8):
                emit("tensor", lambda e, j=j, h=h, bt=bt: e.matmul(
                    out=p_ffn[h][0:bt, :], lhsT=xnT_t[:, j * BT: j * BT + bt],
                    rhs=wdtT[:, j * D + h * SL: j * D + h * SL + SL],
                    start=(j == 0), stop=False))
            st["mmd"][h] = emit("tensor", lambda e, h=h, bt=bt: e.matmul(
                out=p_ffn[h][0:bt, :], lhsT=ones_bf[:, 0:bt],
                rhs=bdt_row[:, h * SL: h * SL + SL], start=False, stop=True))
        ch.append(lambda: c_dtmm(0))
        ch.append(lambda: c_dtmm(1))

        def c_epi():
            for h in range(2):
                waits("scalar", [st["mmd"][h], pe(a_dtf, k - 1)])
                a_e1[k][h] = emit("scalar", lambda e, h=h, bt=bt: e.activation(
                    out=ep_e[0:bt, h * SL:(h + 1) * SL], in_=p_ffn[h][0:bt, :], func=AF.Exp))
            a_e1s[k] = a_e1[k][1]
            for h in range(2):
                emit("scalar", lambda e, h=h, bt=bt: e.activation(
                    out=ep_e[0:bt, h * SL:(h + 1) * SL], in_=ep_e[0:bt, h * SL:(h + 1) * SL],
                    func=AF.Ln, bias=1.0))
            for h in range(2):
                st["sg"][h] = emit("scalar", lambda e, h=h, bt=bt: e.activation(
                    out=ep_e[0:bt, h * SL:(h + 1) * SL], in_=ep_e[0:bt, h * SL:(h + 1) * SL],
                    func=AF.Sigmoid, scale=1.0 / DT_R, bias=-DT_C / DT_R))
        ch.append(c_epi)

        def c_dt2():
            for h in range(2):
                wait("vector", st["sg"][h])
                a_dtf[k] = emit("vector", lambda e, h=h, kb=kb, bt=bt: e.tensor_scalar(
                    out=dt2[0:bt, kb * D + h * SL: kb * D + h * SL + SL],
                    in0=ep_e[0:bt, h * SL:(h + 1) * SL],
                    scalar1=4.0 * DT_R, scalar2=DT_C - 2.0 * DT_R, op0=ALU.mult, op1=ALU.add))
        ch.append(c_dt2)

        def c_bmm():
            waits("tensor", [a_xnTl[k], st["evs"][6], st["evs"][7]])
            for j in range(8):
                emit("tensor", lambda e, j=j, bt=bt: e.matmul(
                    out=p_ca[0][0:bt, 0:N], lhsT=xnT_t[:, j * BT: j * BT + bt],
                    rhs=wbT[:, j * N:(j + 1) * N], start=(j == 0), stop=(j == 7)))
        ch.append(c_bmm)

        def c_cmm():
            for j in range(8):
                st["mmc"] = emit("tensor", lambda e, j=j, bt=bt: e.matmul(
                    out=p_ca[0][0:bt, N:2 * N], lhsT=xnT_t[:, j * BT: j * BT + bt],
                    rhs=wcT[:, j * N:(j + 1) * N], start=(j == 0), stop=(j == 7)))
            wait("vector", st["mmc"])
            emit("vector", lambda e, k=k, bt=bt: e.tensor_copy(
                out=bp_full[0:bt, k * N:(k + 1) * N], in_=p_ca[0][0:bt, 0:N]))
            a_cp[k] = emit("vector", lambda e, k=k, bt=bt: e.tensor_copy(
                out=cp_full[0:bt, k * N:(k + 1) * N], in_=p_ca[0][0:bt, N:2 * N]))
        ch.append(c_cmm)

        def c_w():
            ev_w_b[k] = emit("vector", lambda e, kb=kb, bt=bt: e.tensor_tensor(
                out=w_t[0:bt, :], in0=xn_t[0:bt, :], in1=dt2[0:bt, kb * D:(kb + 1) * D], op=ALU.mult))
            emit("vector", lambda e, kb=kb, bt=bt: e.memset(sfull[kb][0:bt, :], 1.0))
        ch.append(c_w)

        def c_ffn(h):
            if h == 0:
                waits("tensor", [a_e1[k][0], a_e1[k][1]])
            for j in range(8):
                st["mmc"] = emit("tensor", lambda e, j=j, h=h, bt=bt, kb=kb: e.matmul(
                    out=p_ffn[h][0:bt, :], lhsT=xqs[kb][:, j * BT: j * BT + bt],
                    rhs=wqT[:, j * D + h * SL: j * D + h * SL + SL],
                    start=(j == 0), stop=False))
            st["mmc"] = emit("tensor", lambda e, h=h, bt=bt: e.matmul(
                out=p_ffn[h][0:bt, :], lhsT=ones_bf[:, 0:bt],
                rhs=fbb_row[:, h * SL: h * SL + SL], start=False, stop=True))
            if h == 1:
                ev_ffn_b[k] = st["mmc"]
        ch.append(lambda: c_ffn(0))
        ch.append(lambda: c_ffn(1))

        def c_base1():
            bb = base2[kb]
            waits("vector", [pe(ev_store_b, k - 2)])
            emit("vector", lambda e, bt=bt, bb=bb: e.tensor_tensor(
                out=bb[0:bt, :], in0=xn_t[0:bt, :], in1=dbc[0:bt, :], op=ALU.mult))
            ev_basex_b[k] = emit("vector", lambda e, bt=bt, bb=bb: e.tensor_tensor(
                out=bb[0:bt, :], in0=bb[0:bt, :], in1=x_t[0:bt, :], op=ALU.add))
        ch.append(c_base1)

        def c_base2():
            bb = base2[kb]
            waits("vector", [ev_ffn_b[k]])
            for h in range(2):
                ev_baseff_b[k] = emit("vector", lambda e, h=h, bt=bt, bb=bb: e.tensor_tensor(
                    out=bb[0:bt, h * SL:(h + 1) * SL], in0=bb[0:bt, h * SL:(h + 1) * SL],
                    in1=p_ffn[h][0:bt, :], op=ALU.add))
        ch.append(c_base2)
        return ch

    def emit_2a_stage(k, i):
        t0, bt, regions = blocks[k]
        pat = pat_of(k)
        g0 = k * NSLC
        if i < NSLC and i % 2 == 0:
            s0, s1 = i, i + 1
            gs0 = g0 + s0
            pb = (s0 >> 1) % 2
            waits("vector", [pe(ev_g, gs0 - 4)])
            ev_garg[gs0] = ev_garg[gs0 + 1] = emit("vector", lambda e, kb=k % 2, bt=bt, s0=s0, pb=pb: e.tensor_tensor(
                out=garg[pb][0:bt, :], in0=dt2[0:bt, kb * D: kb * D + 2 * SL],
                in1=a16b[0:bt, s0 * SL:(s0 + 2) * SL], op=ALU.mult))
        if 0 <= i - 1 < NSLC and (i - 1) % 2 == 0:
            s0 = i - 1
            gs0 = g0 + s0
            pb = (s0 >> 1) % 2
            waits("scalar", [ev_garg[gs0 + 1], pe(ev_ca, gs0 - 3), ev_w_b[k]])
            ev_g[gs0] = ev_g[gs0 + 1] = emit("scalar", lambda e, bt=bt, pb=pb: e.activation(
                out=g_t[pb][0:bt, :], in_=garg[pb][0:bt, :], func=AF.Tanh))
        if 0 <= i - 1 < NSLC:
            s = i - 1
            gs = g0 + s
            waits("tensor", [ev_g[gs], pe(ev_en, gs - 4), a_cp[k] if s < 4 else None])
            ev_ca[gs] = emit("tensor", lambda e, pat=pat, bt=bt, s=s: e.matmul(
                out=p_ca[(s >> 1) % 2][:, (s & 1) * SL:((s & 1) + 1) * SL],
                lhsT=tri_c[0:bt, pat * 128:(pat + 1) * 128],
                rhs=g_t[(s >> 1) % 2][0:bt, (s & 1) * SL:(s & 1) * SL + SL],
                start=True, stop=True))
        if 0 <= i - 2 < NSLC and (i - 2) % 2 == 1:
            s1 = i - 2
            s0 = s1 - 1
            gs1 = g0 + s1
            gs0 = g0 + s0
            pb = (s0 >> 1) % 2
            waits("scalar", [ev_ca[gs1], pe(ev_stt, gs0 - NSLC), pe(ev_stt, gs1 - NSLC), pe(ev_ext_b, k - 1)])
            ev_E[gs0] = ev_E[gs1] = emit("scalar", lambda e, s0=s0, pb=pb: e.activation(
                out=e_blk[:, s0 * SL:(s0 + 2) * SL], in_=p_ca[pb][:], func=AF.Exp, scale=16.0))
            waits("scalar", [pe(ev_r, gs0 - 2)])
            waits("scalar", [pe(ev_r, gs0 - 2)])
            ev_en[gs0] = ev_en[gs1] = emit("scalar", lambda e, bt=bt, pb=pb: e.activation(
                out=en2_t[0:bt, :], in_=p_ca[pb][0:bt, :], func=AF.Exp, scale=-16.0))
        if 0 <= i - 3 < NSLC and (i - 3) % 2 == 1:
            s1 = i - 3
            s0 = s1 - 1
            gs1 = g0 + s1
            gs0 = g0 + s0
            n_i = s0 >> 1
            waits("vector", [ev_en[gs1], pe(ev_smm, gs0 - 2), pe(ev_smm, gs1 - 2)])
            ev_r[gs0] = ev_r[gs1] = emit("vector", lambda e, k=k, bt=bt, n_i=n_i: e.scalar_tensor_tensor(
                out=r2_t[0][0:bt, :], in0=w_t[0:bt, :],
                scalar=bp_full[0:bt, k * N + n_i: k * N + n_i + 1],
                in1=en2_t[0:bt, :], op0=ALU.mult, op1=ALU.mult))
            for s in (s0, s1):
                gs = g0 + s
                waits("tensor", [ev_r[gs], pe(ev_evac, gs - 2)])
                ev_smm[gs] = emit("tensor", lambda e, pat=pat, bt=bt, b=s % 2, s=s: e.matmul(
                    out=p_sp[b][:], lhsT=tri_c[0:bt, pat * 128:(pat + 1) * 128],
                    rhs=r2_t[0][0:bt, (s & 1) * SL:((s & 1) + 1) * SL],
                    start=True, stop=True))
        if 0 <= i - 4 < NSLC:
            s = i - 4
            gs = g0 + s
            b = s % 2
            waits("scalar", [ev_smm[gs], pe(ev_re, gs - NSLC), pe(ev_ext_b, k - 1)])
            ev_evac[gs] = emit("scalar", lambda e, s=s, b=b: e.copy(
                out=s_blk[:, s * SL:(s + 1) * SL], in_=p_sp[b][:]))

    def emit_2b_stage(k, i):
        t0, bt, regions = blocks[k]
        g0 = k * NSLC
        if i < NSLC:
            s = i
            gs = g0 + s
            waits("tensor", [ev_wb_b[k], ev_evac[gs], pe(ev_stt, gs - 2),
                             ev_baseff_b[k + 1] if k + 1 < NBLK and ev_baseff_b[k + 1] is not None else None])
            ev_re[gs] = emit("tensor", lambda e, k=k, bt=bt, s=s: e.matmul(
                out=p_ffn[s % 2][0:bt, :], lhsT=mre_c[:, pat_of(k) * BT: pat_of(k) * BT + bt],
                rhs=s_blk[:, s * SL:(s + 1) * SL], start=True, stop=True))
        if 0 <= i - 1 < NSLC:
            s = i - 1
            gs = g0 + s
            b = s % 2
            n_i = s >> 1
            dh = (s & 1) * SL
            waits("vector", [ev_re[gs]])
            if s < 2:
                ev_stt[gs] = emit("vector", lambda e, k=k, bt=bt, s=s, n_i=n_i, dh=dh, b=b: e.scalar_tensor_tensor(
                    out=yt_bf[0:bt, dh:dh + SL], in0=e_blk[0:bt, s * SL:(s + 1) * SL],
                    scalar=cp_full[0:bt, k * N + n_i: k * N + n_i + 1], in1=p_ffn[b][0:bt, :],
                    op0=ALU.mult, op1=ALU.mult))
                ev_yadd[gs] = ev_stt[gs]
            else:
                waits("vector", [pe(ev_yadd, gs - 4)])
                ev_stt[gs] = emit("vector", lambda e, k=k, bt=bt, s=s, n_i=n_i, cb=s % 4, b=b: e.scalar_tensor_tensor(
                    out=ces_t[0:bt, cb * SL:(cb + 1) * SL], in0=e_blk[0:bt, s * SL:(s + 1) * SL],
                    scalar=cp_full[0:bt, k * N + n_i: k * N + n_i + 1], in1=p_ffn[b][0:bt, :],
                    op0=ALU.mult, op1=ALU.mult))

                def _yadd(e, bt=bt, dh=dh, cb=s % 4):
                    with nc.allow_low_precision(reason="y partial sums kept in bf16; rvr-checked"):
                        return e.tensor_tensor(
                            out=yt_bf[0:bt, dh:dh + SL], in0=yt_bf[0:bt, dh:dh + SL],
                            in1=ces_t[0:bt, cb * SL:(cb + 1) * SL], op=ALU.add)
                ev_yadd[gs] = emit("vector", _yadd)

    def emit_finalize(k):
        t0, bt, regions = blocks[k]
        bb = base2[k % 2]
        waits("vector", [ev_yadd[k * NSLC + NSLC - 1]])
        ev_out = emit("vector", lambda e, k=k, bt=bt, bb=bb: e.scalar_tensor_tensor(
            out=bb[0:bt, :], in0=yt_bf[0:bt, :], scalar=sfull[k % 2][0:bt, :],
            in1=bb[0:bt, :], op0=ALU.mult, op1=ALU.add))
        wait("sync", ev_out)
        ev_store_b[k] = emit("sync", lambda e, t0=t0, bt=bt, bb=bb: e.dma_start(
            out=out_h[t0:t0 + bt, :], in_=bb[0:bt, :]), inc=16)

    def emit_chain(k, feeder=None):
        def feed(n=1):
            for _ in range(n):
                if feeder:
                    feeder.pop(0)()
        t0, bt, regions = blocks[k]
        nreg = len(regions)
        g0 = k * NSLC
        waits("gpsimd", [ev_E[g0 + NSLC - 1], pe(ev_chain_b, k - 1)])
        emit("gpsimd", lambda e, nreg=nreg: e.dma_start(
            out=chE_d[0:nreg, :], in_=e_blk[114:114 + nreg, :]), inc=16)
        ev_ext = [emit("gpsimd", lambda e, nreg=nreg: e.dma_start(
            out=chE[:, 0:nreg * 128].rearrange("p (c j) -> p c j", j=128),
            in_=chE_d[0:nreg, :].rearrange("c (q j) -> q c j", j=128)), inc=16)]
        waits("sync", [ev_evac[g0 + NSLC - 1], ev_evac[g0 + NSLC - 2], pe(ev_chain_b, k - 1)])
        ev_chS_out = emit("sync", lambda e, nreg=nreg: e.dma_start(
            out=chS_d[0:nreg, :], in_=s_blk[114:114 + nreg, :]), inc=16)
        ev_ext.append(emit("sync", lambda e, nreg=nreg: e.dma_start(
            out=chS[:, 0:nreg * 128].rearrange("p (c j) -> p c j", j=128),
            in_=chS_d[0:nreg, :].rearrange("c (q j) -> q c j", j=128)), inc=16))
        ev_ext_b[k] = ev_ext[0]
        feed(2)
        waits("gpsimd", [pe(ev_hcarry_b, k - 1)])
        ev_h0w = [emit("gpsimd", lambda e: e.tensor_copy(out=h0st[:, 0:128], in_=h_t[:]))]
        waits("vector", ev_ext + [pe(ev_wb_b, k - 1)])
        emit("vector", lambda e: e.tensor_tensor(
            out=ht_tmp[:], in0=h_t[:], in1=chS[:, 0:128], op=ALU.add))
        pend_pool_h0 = None

        ev_wb_last = [None]

        def wb_col(c):
            emit("gpsimd", lambda e, c=c: e.dma_start(
                out=h0_d[c:c + 1, :].rearrange("c (q j) -> q c j", j=128),
                in_=h0st[:, c * 128:(c + 1) * 128].rearrange("p (c j) -> p c j", j=128)), inc=16)
            return emit("gpsimd", lambda e, c=c: e.dma_start(
                out=s_blk[114 + c:115 + c, :], in_=h0_d[c:c + 1, :]), inc=16)

        waits("gpsimd", [ev_h0w[0], ev_evac[g0 + NSLC - 1], ev_evac[g0 + NSLC - 2], ev_chS_out])
        ev_wb_last[0] = wb_col(0)
        for c, (lo, hi, clamps) in enumerate(regions):
            feed(2)
            last = (c == nreg - 1)
            waits("vector", [pend_pool_h0])
            pend_pool_h0 = None
            if clamps:
                emit("vector", lambda e, c=c: e.scalar_tensor_tensor(
                    out=ht_new[:], in0=chE[:, c * 128:(c + 1) * 128], scalar=1.0,
                    in1=ht_tmp[:], op0=ALU.mult, op1=ALU.mult))
                ev_q = emit("vector", lambda e: e.scalar_tensor_tensor(
                    out=ht_tmp[:], in0=ht_new[:], scalar=1.0, in1=ht_new[:],
                    op0=ALU.mult, op1=ALU.mult, accum_out=qcol[:]))
                wait("tensor", ev_q)
                ev_qs = emit("tensor", lambda e: e.matmul(
                    out=p_s, lhsT=ones_sq[:], rhs=qcol[:], start=True, stop=True))
                wait("scalar", ev_qs)
                ev_rt = emit("scalar", lambda e: e.activation(
                    out=scol[:], in_=p_s, func=AF.Sqrt))
                wait("vector", ev_rt)
                emit("vector", lambda e: e.reciprocal(out=scol[:], in_=scol[:]))
                emit("vector", lambda e: e.tensor_scalar(
                    out=scol[:], in0=scol[:], scalar1=20.0, scalar2=1.0, op0=ALU.mult, op1=ALU.min))
                if last:
                    waits("vector", [ev_h0w[0]])
                    ev_hcarry_b[k] = emit("vector", lambda e: e.tensor_scalar(
                        out=h_t[:], in0=ht_new[:], scalar1=scol[:], scalar2=None, op0=ALU.mult))
                else:
                    emit("vector", lambda e, c=c: e.scalar_tensor_tensor(
                        out=ht_tmp[:], in0=ht_new[:], scalar=scol[:],
                        in1=chS[:, (c + 1) * 128:(c + 2) * 128], op0=ALU.mult, op1=ALU.add))
                    ev_h0w.append(emit("vector", lambda e, c=c: e.tensor_scalar(
                        out=h0st[:, (c + 1) * 128:(c + 2) * 128], in0=ht_new[:],
                        scalar1=scol[:], scalar2=None, op0=ALU.mult)))
                emit("vector", lambda e: e.tensor_scalar(
                    out=sm1[:], in0=scol[:], scalar1=-1.0, scalar2=None, op0=ALU.add))
                ev_chain_b[k] = emit("vector", lambda e, hi=hi, bt=bt, k=k: e.scalar_tensor_tensor(
                    out=sfull[k % 2][0:bt, :], in0=id_f32[0:bt, hi:hi + 1], scalar=sm1[0:bt, :],
                    in1=sfull[k % 2][0:bt, :], op0=ALU.mult, op1=ALU.add))
                if not last:
                    waits("gpsimd", [ev_h0w[-1]])
                    ev_wb_last[0] = wb_col(c + 1)
            else:
                if last:
                    waits("vector", [ev_h0w[0]])
                    ev_chain_b[k] = ev_hcarry_b[k] = emit("vector", lambda e, c=c: e.tensor_tensor(
                        out=h_t[:], in0=chE[:, c * 128:(c + 1) * 128], in1=ht_tmp[:], op=ALU.mult))
                else:
                    ev_hn = emit("vector", lambda e, c=c: e.tensor_tensor(
                        out=ht_new[:], in0=chE[:, c * 128:(c + 1) * 128], in1=ht_tmp[:], op=ALU.mult))
                    ev_chain_b[k] = emit("vector", lambda e, c=c: e.tensor_tensor(
                        out=ht_tmp[:], in0=ht_new[:], in1=chS[:, (c + 1) * 128:(c + 2) * 128], op=ALU.add))
                    waits("gpsimd", [ev_chain_b[k]])
                    pend_pool_h0 = emit("gpsimd", lambda e, c=c: e.tensor_copy(
                        out=h0st[:, (c + 1) * 128:(c + 2) * 128], in_=ht_new[:]))
                    ev_h0w.append(pend_pool_h0)
                    ev_wb_last[0] = wb_col(c + 1)
        ev_wb_b[k] = ev_wb_last[0]

    barrier(snapshot())
    NBL = len(blocks)
    if stage >= 4:
        for c in head_chunks(0):
            c()
        for k in range(NBL):
            for i in range(NSLC + 5):
                emit_2a_stage(k, i)
                if k >= 1 and stage >= 6:
                    emit_2b_stage(k - 1, i)
            if k >= 1 and stage >= 6:
                emit_finalize(k - 1)
            hc = head_chunks(k + 1) if k + 1 < NBL else []
            if stage >= 5:
                emit_chain(k, hc)
            for c in hc:
                c()
            if debug and k == 0:
                barrier(snapshot())
                emit("gpsimd", lambda e: e.dma_start(out=dbg["dbg_E0"], in_=e_blk[:]), inc=16)
                emit("gpsimd", lambda e: e.dma_start(out=dbg["dbg_S0"], in_=s_blk[:]), inc=16)
                emit("gpsimd", lambda e: e.dma_start(out=dbg["dbg_h0s"], in_=h0st[:]), inc=16)
                emit("gpsimd", lambda e: e.dma_start(out=dbg["dbg_xn0"], in_=xn_t[0:BT, :]), inc=16)
                barrier(snapshot())
        if stage >= 6:
            for i in range(NSLC + 2):
                emit_2b_stage(NBL - 1, i)
            emit_finalize(NBL - 1)
            if debug:
                barrier(snapshot())
                emit("gpsimd", lambda e: e.dma_start(out=dbg["dbg_y0"], in_=base2[(NBL - 1) % 2][0:BT, :]), inc=16)
                emit("gpsimd", lambda e, k=0: e.dma_start(out=dbg["dbg_sfull"], in_=sfull[0][:]), inc=16)
                barrier(snapshot())

    if debug:
        barrier(snapshot())
        emit("gpsimd", lambda e: e.dma_start(out=dbg["dbg_bp"], in_=bp_full[:]), inc=16)
        emit("gpsimd", lambda e: e.dma_start(out=dbg["dbg_cp"], in_=cp_full[:]), inc=16)

    def thunk_final_d(eng):
        eng.wait_ge(sem_d, cnt["d"])
    prog["sync"].append(thunk_final_d)

    def thunk_final_g(eng):
        eng.wait_ge(sem_g, cnt["g"])
        eng.wait_ge(sem_p, cnt["p"])
    prog["gpsimd"].append(thunk_final_g)

    with es, nc.Block() as block:
        @block.sync
        def _(eng):
            for f in prog["sync"]:
                f(eng)

        @block.gpsimd
        def _(eng):
            for f in prog["gpsimd"]:
                f(eng)

        @block.tensor
        def _(eng):
            for f in prog["tensor"]:
                f(eng)

        @block.vector
        def _(eng):
            for f in prog["vector"]:
                f(eng)

        @block.scalar
        def _(eng):
            for f in prog["scalar"]:
                f(eng)

    return nc


def kernel(L0_attn_x, L0_attn_W_dt, L0_attn_b_dt, L0_attn_W_B, L0_attn_W_C,
           L0_attn_A_log, L0_attn_D_param, L0_ffn_weight, L0_ffn_bias):
    _, consts = build_consts()
    nc = build_nc(debug=False)
    base = {
        "W_dtT": np.ascontiguousarray(np.asarray(L0_attn_W_dt, dtype=np.float32).T),
        "b_dt": np.ascontiguousarray(L0_attn_b_dt, dtype=np.float32).reshape(1, D),
        "W_BT": np.ascontiguousarray(np.asarray(L0_attn_W_B, dtype=np.float32).T),
        "W_CT": np.ascontiguousarray(np.asarray(L0_attn_W_C, dtype=np.float32).T),
        "A_logT": np.ascontiguousarray(L0_attn_A_log.T, dtype=np.float32),
        "D_param": np.ascontiguousarray(L0_attn_D_param, dtype=np.float32).reshape(1, D),
        "W_ffnT": np.ascontiguousarray(np.asarray(L0_ffn_weight, dtype=np.float32).T),
        "b_ffn": np.ascontiguousarray(L0_ffn_bias, dtype=np.float32).reshape(1, D),
    }
    base.update(consts)
    x = np.ascontiguousarray(L0_attn_x, dtype=np.float32)
    in_maps = []
    for i in range(8):
        m = dict(base)
        m["x"] = x[i]
        in_maps.append(m)
    res = run_bass_kernel_spmd(nc, in_maps, core_ids=list(range(8)))
    out = np.stack([np.asarray(r["out"]) for r in res.results], axis=0).astype(np.float32)
    return out
